# version 03
# baseline (speedup 1.0000x reference)
import math
import numpy as np
import ml_dtypes
import concourse.bass as bass
import concourse.mybir as mybir
from concourse.bass_utils import run_bass_kernel_spmd

F32 = mybir.dt.float32
BF16 = mybir.dt.bfloat16
I32 = mybir.dt.int32
U8 = mybir.dt.uint8
AF = mybir.ActivationFunctionType
ALU = mybir.AluOpType
AX = mybir.AxisListType

B, L, EMB, HD, NH, DFF = 2, 4096, 2048, 128, 16, 16384
NC = 8
SEQ_CH = L // 128
OWN_CH = 8
TPC = OWN_CH * 128
NPC = DFF // NC
ISQ = 1.0 / math.sqrt(128.0)
CSHIFT = 10.0
EPS = 1e-5

BF = ml_dtypes.bfloat16


def _bf(x):
    return np.ascontiguousarray(x.astype(BF))



def host_constants():
    ident = np.eye(128, dtype=np.float32)
    rt = np.zeros((128, 128), dtype=np.float32)
    for i in range(64):
        rt[2 * i + 1, 2 * i] = -1.0
        rt[2 * i, 2 * i + 1] = 1.0
    ones = np.ones((128, 128), dtype=np.float32)
    tri = (np.arange(128)[None, :] >= np.arange(128)[:, None]).astype(np.float32)
    inv = 1.0 / (10000.0 ** (np.arange(0, HD, 2, dtype=np.float32) / HD))
    t = np.arange(L, dtype=np.float32)
    f = np.outer(t, inv)
    f = np.repeat(f, 2, axis=-1)
    cosT = np.cos(f).astype(np.float32).T.copy()
    sinT = np.sin(f).astype(np.float32).T.copy()
    return ident, rt, ones, tri, cosT, sinT


def chunk_perm(j):
    perm = []
    for m in range(SEQ_CH // 4):
        others = [4 * m + t for t in range(4) if t != j]
        perm.extend(others + [4 * m + j])
    return np.array(perm, dtype=np.int64)


def build_amask(j, tri):
    ones = np.ones((128, 128), dtype=np.float32)
    zeros = np.zeros((128, 128), dtype=np.float32)

    def A(s):
        if s == 3:
            return tri
        return ones if s < j else zeros

    slots = []
    for sp in range(8):
        if sp < 4:
            m = np.concatenate([A(sp), ones], axis=1)
        else:
            m = np.concatenate([zeros, A(sp - 4)], axis=1)
        slots.append(m)
    return np.concatenate(slots, axis=1)


def make_in_maps(inputs):
    x = np.asarray(inputs["L0_attn_x"])
    wq = np.asarray(inputs["L0_attn_Wq"])
    wk = np.asarray(inputs["L0_attn_Wk"])
    wv = np.asarray(inputs["L0_attn_Wv"])
    gpk = np.asarray(inputs["L0_ffn_gate_W_packed"])
    gsc = np.asarray(inputs["L0_ffn_gate_scales"])
    upk = np.asarray(inputs["L0_ffn_up_W_packed"])
    usc = np.asarray(inputs["L0_ffn_up_scales"])
    dpk = np.asarray(inputs["L0_ffn_down_W_packed"])
    dsc = np.asarray(inputs["L0_ffn_down_scales"])
    ident, rt, ones, tri, cosT, sinT = host_constants()

    maps = []
    for c in range(NC):
        s, j = c // 4, c % 4
        perm = chunk_perm(j)
        xs = x[s].reshape(SEQ_CH, 128, EMB)[perm].reshape(L, EMB)
        ct = cosT.reshape(128, SEQ_CH, 128)[:, perm].reshape(128, L)
        st = sinT.reshape(128, SEQ_CH, 128)[:, perm].reshape(128, L)
        own = [4 * m + j for m in range(OWN_CH)]
        cq = cosT.reshape(128, SEQ_CH, 128)[:, own].reshape(128, TPC)
        sq = sinT.reshape(128, SEQ_CH, 128)[:, own].reshape(128, TPC)
        m = {
            "xseq": np.ascontiguousarray(xs),
            "wq": np.ascontiguousarray(wq),
            "wk": np.ascontiguousarray(wk),
            "wv": np.ascontiguousarray(wv),
            "gpk": np.ascontiguousarray(gpk[NPC * c:NPC * (c + 1)]),
            "gsc": np.ascontiguousarray(gsc[NPC * c:NPC * (c + 1)]),
            "upk": np.ascontiguousarray(upk[NPC * c:NPC * (c + 1)]),
            "usc": np.ascontiguousarray(usc[NPC * c:NPC * (c + 1)]),
            "dpk": np.ascontiguousarray(dpk[:, 1024 * c:1024 * (c + 1)]),
            "dsc": np.ascontiguousarray(dsc[:, 64 * c:64 * (c + 1)]),
            "cost": np.ascontiguousarray(ct),
            "sint": np.ascontiguousarray(st),
            "cosq": np.ascontiguousarray(cq),
            "sinq": np.ascontiguousarray(sq),
            "ident": _bf(ident),
            "rotT": _bf(rt),
            "ones": _bf(ones),
            "amask": _bf(build_amask(j, tri)),
        }
        maps.append(m)
    return maps


def assemble_output(results):
    out = np.zeros((B, L, EMB), dtype=np.float32)
    for c in range(NC):
        s, j = c // 4, c % 4
        shard = results[c]["out"].reshape(OWN_CH, 128, EMB)
        for m in range(OWN_CH):
            g = 4 * m + j
            out[s, 128 * g:128 * (g + 1)] = shard[m]
    return out



ENGINES = ("sync", "tensor", "vector", "scalar", "gpsimd")


class Sem:
    __slots__ = ("h", "n", "virtual", "eng", "counts")

    def __init__(self, nc, name):
        self.h = nc.alloc_semaphore(name)
        self.n = 0
        self.virtual = False


class VSem:
    __slots__ = ("n", "eng", "counts", "virtual")

    def __init__(self):
        self.n = 0
        self.eng = None
        self.counts = []
        self.virtual = True


CHAINED = ("vector", "scalar")


class Prog:
    def __init__(self, nc):
        self.nc = nc
        self.st = {e: [] for e in ENGINES}
        self.sems = {}
        self.chain = {e: Sem(nc, f"CH_{e}") for e in CHAINED}

    def sem(self, name):
        if name not in self.sems:
            self.sems[name] = VSem()
        return self.sems[name]

    def _realize(self, name):
        v = self.sems.get(name)
        if v is None or v.virtual and v.eng is None and not v.counts and v.n == 0:
            pass
        return v

    def o(self, eng, fn, inc=None, inc_n=1):
        chained = eng in CHAINED and inc_n == 1
        if chained:
            ch = self.chain[eng]
            prior = ch.n
            ch.n += 1
            if inc is not None:
                vs = self.sem(inc)
                if isinstance(vs, Sem):
                    raise RuntimeError(f"sem {inc} already real, inc from {eng}")
                if vs.eng is None:
                    vs.eng = eng
                assert vs.eng == eng, (inc, vs.eng, eng)
                vs.n += 1
                vs.counts.append(ch.n)

            def wrapped(E, fn=fn, h=ch.h, prior=prior):
                if prior:
                    E.wait_ge(h, prior)
                fn(E).then_inc(h, 1)
            self.st[eng].append(wrapped)
        else:
            if inc is not None:
                vs = self.sems.get(inc)
                if vs is None or isinstance(vs, VSem):
                    if vs is not None and (vs.counts or vs.eng):
                        raise RuntimeError(f"sem {inc} already virtual")
                    self.sems[inc] = Sem(self.nc, inc)
                sem = self.sems[inc]
                prior = sem.n
                sem.n += inc_n
                if inc_n == 16 and prior:
                    self.st[eng].append(lambda E, h=sem.h, v=prior: E.wait_ge(h, v))

                def wrapped(E, fn=fn, h=sem.h, k=inc_n):
                    fn(E).then_inc(h, k)
                self.st[eng].append(wrapped)
            else:
                self.st[eng].append(fn)

    def w(self, eng, sem, val=None):
        if isinstance(sem, str):
            sem = self.sem(sem)
        v = sem.n if val is None else val
        if v <= 0:
            return
        if getattr(sem, "virtual", False):
            assert sem.eng is not None, "wait on never-incremented virtual sem"
            assert v <= len(sem.counts), (v, len(sem.counts))
            h = self.chain[sem.eng].h
            tgt = sem.counts[v - 1]
            self.st[eng].append(lambda E, h=h, t=tgt: E.wait_ge(h, t))
        else:
            self.st[eng].append(lambda E, h=sem.h, v=v: E.wait_ge(h, v))

    def emit(self):
        with self.nc.Block() as blk:
            for e in ENGINES:
                ops = self.st[e]
                if not ops:
                    continue

                def runner(E, ops=ops):
                    for f in ops:
                        f(E)
                getattr(blk, e)(runner)


ESZ = {F32: 4, BF16: 2, I32: 4, U8: 1}


class Arena:

    def __init__(self, nc, size):
        self.h = nc.alloc_sbuf_tensor("arena", [128, size], U8)
        self.size = size

    def view(self, off, n_elems, dtype, p=128):
        nb = n_elems * ESZ[dtype]
        assert off % 64 == 0 and off + nb <= self.size, (off, nb, self.size)
        return self.h[0:p, off:off + nb].bitcast(dtype)


class Layout:

    def __init__(self):
        o = 0

        def take(nb):
            nonlocal o
            off = (o + 63) & ~63
            o = off + nb
            return off

        self.cbank = take(64)
        self.ident = take(256)
        self.rotT = take(256)
        self.ones = take(256)
        self.wkb = take(4096)
        self.wvb = take(4096)
        self.PT = self.wkb
        self.KT = take(8192)
        self.Vsb = take(8192)
        self.dn_wdt = self.KT
        self.dn_int = None
        r2 = take(32768)
        self.QT = r2
        self.acc0 = r2
        s = r2
        self.xring = s; s += 16384
        self.xnt_sweep = s; s += 8192
        self.csring = s; s += 2048
        self.k0 = s; s += 256
        self.vt0 = s; s += 256
        self.u1 = s; s += 2048
        self.u2 = s; s += 2048
        self.lnt = s; s += 1024
        assert s - r2 <= 32768, s - r2
        r3 = take(32768)
        self.xnt_own = r3
        self.acc1 = r3
        s = r3
        self.attnT = s; s += 16384
        self.recip = s; s += 1024
        self.xmid = s; s += 8192
        self.amask = s; s += 4096
        assert s - r3 <= 32768, s - r3
        r4 = take(32768)
        self.xn2T = r4
        self.wq0 = r4
        self.wq1 = r4 + 16384
        self.accq = None
        r5 = take(45056)
        s = r5
        self.pk = s; s += 8192
        self.scl = s; s += 1024
        self.dq = s; s += 26624
        self.wdec = s; s += 4096
        self.wstg = s; s += 4096
        assert s - r5 <= 45056, s - r5
        self.gslab = r5
        self.gelu = r5 + 32768
        self.inter = r5 + 32768 + 4096
        r6 = take(24576)
        self.r6 = r6
        self.total = (o + 63) & ~63



class Build:
    def __init__(self, nc, dbg=()):
        self.nc = nc
        self.dbg = set(dbg)
        self.P = Prog(nc)
        self.lay = Layout()
        self.ar = Arena(nc, self.lay.total)
        L = self.lay

        def din(name, shape, dt):
            return nc.dram_tensor(name, shape, dt, kind="ExternalInput").ap()

        self.xseq = din("xseq", [L_ := 4096, EMB], F32)
        self.wq = din("wq", [EMB, EMB], F32)
        self.wk = din("wk", [EMB, HD], F32)
        self.wv = din("wv", [EMB, HD], F32)
        self.gpk = din("gpk", [NPC, 1024], I32)
        self.gsc = din("gsc", [NPC, 64], F32)
        self.upk = din("upk", [NPC, 1024], I32)
        self.usc = din("usc", [NPC, 64], F32)
        self.dpk = din("dpk", [EMB, 1024], I32)
        self.dsc = din("dsc", [EMB, 64], F32)
        self.cost = din("cost", [128, 4096], F32)
        self.sint = din("sint", [128, 4096], F32)
        self.cosq = din("cosq", [128, TPC], F32)
        self.sinq = din("sinq", [128, TPC], F32)
        self.identD = din("ident", [128, 128], BF16)
        self.rotD = din("rotT", [128, 128], BF16)
        self.onesD = din("ones", [128, 128], BF16)
        self.amaskD = din("amask", [128, 2048], BF16)
        self.out = nc.dram_tensor("out", [TPC, EMB], F32, kind="ExternalOutput").ap()

        self.contrib_g = [nc.dram_tensor(f"cg{s}", [EMB, 512], BF16) for s in range(4)]
        self.contrib_u = [nc.dram_tensor(f"cu{s}", [EMB, 512], BF16) for s in range(4)]
        self.contrib_d = nc.dram_tensor("cd", [NPC, EMB], BF16)
        self.gath_g = [nc.dram_tensor(f"gg{s}", [DFF, 512], BF16) for s in range(4)]
        self.gath_u = [nc.dram_tensor(f"gu{s}", [DFF, 512], BF16) for s in range(4)]
        self.gath_d = [nc.dram_tensor(f"gd{i}", [4096, EMB], BF16) for i in range(4)]
        self.interT = nc.dram_tensor("interT", [DFF, TPC], BF16)
        self.xmidD = nc.dram_tensor("xmidD", [TPC, EMB], F32)

        self.PS = [nc.alloc_psum_tensor(f"ps{i}", [128, 1024], F32) for i in range(4)]

        a = self.ar
        self.ident = a.view(L.ident, 128 * 128 // 128, BF16)
        self.rotT = a.view(L.rotT, 128, BF16)
        self.ones = a.view(L.ones, 128, BF16)
        self.wkb = a.view(L.wkb, 16 * 128, BF16)
        self.wvb = a.view(L.wvb, 16 * 128, BF16)
        self.KT = a.view(L.KT, 4096, BF16)
        self.Vsb = a.view(L.Vsb, 4096, BF16)
        self.QT = a.view(L.QT, 16 * 1024, BF16)
        self.xn2T = a.view(L.xn2T, 16 * 1024, BF16)
        self.xnt_own = a.view(L.xnt_own, 16 * 1024, BF16)
        self.amask = a.view(L.amask, 2048, BF16)
        cb = a.view(L.cbank, 16, F32)
        self.cb75 = cb[:, 0:1]
        self.cbm6 = cb[:, 1:2]
        self.cbeps = cb[:, 2:3]
        self.cbsh = cb[:, 3:4]
        self.cbm4 = cb[:, 4:5]

    def V(self, off, n, dt):
        return self.ar.view(off, n, dt)

    def boot(self):
        P, L = self.P, self.lay
        for src, off in ((self.identD, L.ident), (self.rotD, L.rotT),
                         (self.onesD, L.ones)):
            n = src.shape[1]
            P.o("sync", lambda E, s=src, o=off, n=n: E.dma_start(
                out=self.V(o, n, BF16), in_=s), inc="Lc", inc_n=16)
        for src, off in ((self.wk, L.wkb), (self.wv, L.wvb)):
            P.o("gpsimd", lambda E, s=src, o=off: E.dma_start(
                out=self.V(o, 16 * 128, BF16).rearrange("p (h d) -> p h d", d=128),
                in_=s.rearrange("(h p) d -> p h d", p=128)), inc="Lcg", inc_n=16)
        P.o("vector", lambda E: E.memset(self.cb75, 7.5))
        P.o("vector", lambda E: E.memset(self.cbm6, -6.0))
        P.o("vector", lambda E: E.memset(self.cbeps, EPS))
        P.o("vector", lambda E: E.memset(self.cbm4, -4.0))
        P.o("vector", lambda E: E.memset(self.cbsh, -CSHIFT), inc="D_cb")
        self.P.w("tensor", "Lc")
        self.P.w("tensor", "Lcg")
        self.P.w("vector", "Lc")
        self.P.w("vector", "Lcg")
        self.P.w("scalar", "Lc")
        self.P.w("scalar", "Lcg")
        self.P.w("scalar", "D_cb")
        self.P.w("gpsimd", "Lc")

    def emit_ln(self, xap, xnap, par, wait_before=None, lnt_off=None):
        P, L = self.P, self.lay
        lnt = (L.lnt + 1024 * (par & 1)) if lnt_off is None else lnt_off
        cnt = self.ln_cnt
        self.ln_cnt += 1
        s1 = self.V(lnt, 8, F32)
        sq4 = self.V(lnt + 64, 8, F32)
        mm = self.V(lnt + 128, 8, F32)
        msq = self.V(lnt + 192, 8, F32)
        var = self.V(lnt + 256, 8, F32)
        rsd = self.V(lnt + 320, 8, F32)
        nb = self.V(lnt + 384, 8, F32)
        sqs = self.V(lnt + 448, 8, F32)
        PS3 = self.PS[3]

        if wait_before:
            for eng, sem, v in wait_before:
                P.w(eng, sem, v)
        P.o("vector", lambda E: E.reduce_sum(s1[:, 0:1], xap, AX.X), inc="D_s1")
        if cnt >= 2:
            P.w("scalar", "D_var", cnt - 1)
        P.o("scalar", lambda E: E.activation(xnap, xap, AF.Square), inc="A_sqp")
        P.w("vector", "A_sqp")
        P.o("vector", lambda E: E.reduce_sum(sqs[:, 0:1], xnap, AX.X), inc="D_sqr")
        P.o("vector", lambda E: E.tensor_scalar_mul(mm[:, 0:1], s1[:, 0:1], 1.0 / EMB))
        P.o("vector", lambda E: E.tensor_mul(msq[:, 0:1], mm[:, 0:1], mm[:, 0:1]))
        P.o("vector", lambda E: E.scalar_tensor_tensor(
            var[:, 0:1], sqs[:, 0:1], 1.0 / EMB, msq[:, 0:1],
            ALU.mult, ALU.subtract), inc="D_var")
        P.w("scalar", "D_var")
        if cnt >= 2:
            P.w("scalar", "D_ln", cnt - 1)
        P.o("scalar", lambda E: E.activation(rsd[:, 0:1], var[:, 0:1], AF.Sqrt,
                                             bias=self.cbeps), inc="A_sqrt")
        P.w("vector", "A_sqrt")
        if cnt >= 2:
            P.w("vector", "A_xn", cnt - 1)
        P.o("vector", lambda E: E.reciprocal(rsd[:, 0:1], rsd[:, 0:1]))
        P.o("vector", lambda E: E.scalar_tensor_tensor(
            nb[:, 0:1], mm[:, 0:1], -1.0, rsd[:, 0:1], ALU.mult, ALU.mult),
            inc="D_ln")
        P.w("scalar", "D_ln")
        P.o("scalar", lambda E: E.activation(
            xnap, xap, AF.Identity, bias=nb[:, 0:1], scale=rsd[:, 0:1]),
            inc="A_xn")
        if "dump" in self.dbg and cnt == 32:
            snap = self.V(L.r6 + 20480, 128, F32)
            P.o("vector", lambda E, snap=snap: E.tensor_copy(
                snap[:, 0:16], self.V(lnt, 16, F32)[:, 0:16]))
            P.o("vector", lambda E, snap=snap: E.tensor_copy(
                snap[:, 16:32], self.V(lnt + 448, 16, F32)[:, 0:16]))
            P.o("vector", lambda E, snap=snap: E.tensor_copy(
                snap[:, 32:48], self.V(lnt + 256, 16, F32)[:, 0:16]))
            P.o("vector", lambda E, snap=snap: E.tensor_copy(
                snap[:, 48:64], self.V(lnt + 320, 16, F32)[:, 0:16]), inc="D_snap")
            dsn = self.nc.dram_tensor("dbg_ln2m0", [128, 128], F32, kind="ExternalOutput").ap()
            P.w("sync", "D_snap")
            P.o("sync", lambda E, dsn=dsn, snap=snap: E.dma_start(out=dsn, in_=snap),
                inc="L_dbg6", inc_n=16)

    def emit_deq_tile(self, mat, ti):
        P, L = self.P, self.lay
        t = self.deq_cnt
        self.deq_cnt += 1
        par = t % 2
        pk = self.V(L.pk + 4096 * par, 1024, I32)
        scl = self.V(L.scl + 512 * par, 64, F32)
        src_pk = {"g": self.gpk, "u": self.upk, "d": self.dpk}[mat]
        src_sc = {"g": self.gsc, "u": self.usc, "d": self.dsc}[mat]
        dqo = L.dq
        wf = self.V(dqo, 1024, BF16)
        lo = self.V(dqo + 2048, 1024, BF16)
        sgl = self.V(dqo + 4096, 1024, BF16)
        la = self.V(dqo + 6144, 1024, BF16)
        lb = self.V(dqo + 8192, 1024, BF16)
        ld_ = self.V(dqo + 10240, 1024, BF16)
        dqh = L.dq + 12288
        hi_i = self.V(dqh, 1024, I32)
        hi = self.V(dqh + 4096, 1024, BF16)
        sgh = self.V(dqh + 6144, 1024, BF16)
        ha = self.V(dqh + 8192, 1024, BF16)
        hb = self.V(dqh + 10240, 1024, BF16)
        hd_ = self.V(dqh + 12288, 1024, BF16)
        wdec = self.V(L.wdec, 2048, BF16)
        wstg = self.V(L.wstg, 16 * 128, BF16)
        PS3 = self.PS[3]

        if t >= 2:
            P.w("sync", "D_dec", t - 1)
            P.w("sync", "P_dec", t - 1)
        P.o("sync", lambda E: E.dma_start(out=pk, in_=src_pk[128 * ti:128 * (ti + 1), :]),
            inc="L_pk", inc_n=16)
        P.o("sync", lambda E: E.dma_start(out=scl, in_=src_sc[128 * ti:128 * (ti + 1), :]),
            inc="L_pk", inc_n=16)

        scv = scl.unsqueeze(2).broadcast_to([128, 64, 16])
        P.w("vector", "L_pk", 32 * (t + 1))
        if t >= 1:
            P.w("vector", "PE_dqT", 4 * t)
        P.o("vector", lambda E: E.tensor_single_scalar(hi_i, pk, 4, ALU.logical_shift_right))
        P.o("vector", lambda E: E.tensor_copy(hi, hi_i), inc="P_hi")
        P.o("vector", lambda E: E.tensor_copy(wf, pk))
        P.w("scalar", "P_hi")
        P.o("scalar", lambda E: E.activation(sgh, hi, AF.Sign, bias=self.cb75, scale=-1.0), inc="A_sgh")
        P.o("vector", lambda E: E.scalar_tensor_tensor(lo, hi, -16.0, wf, ALU.mult, ALU.add),
            inc="D_lo")
        P.w("scalar", "D_lo")
        P.o("scalar", lambda E: E.activation(sgl, lo, AF.Sign, bias=self.cb75, scale=-1.0), inc="A_sgl")
        P.o("scalar", lambda E: E.activation(hb, hi, AF.Identity, bias=self.cbm4),
            inc="A_nm4h")
        P.w("vector", "A_sgh")
        P.w("vector", "A_nm4h")
        P.o("vector", lambda E: E.scalar_tensor_tensor(ha, sgh, 4.0, hb, ALU.mult, ALU.add),
            inc="P_kh")
        P.w("scalar", "P_kh")
        P.o("scalar", lambda E: E.activation(hb, ha, AF.Relu, bias=self.cbm4))
        P.o("scalar", lambda E: E.activation(hd_, ha, AF.Relu, bias=self.cbm6), inc="A_r2h")
        P.w("vector", "A_r2h")
        P.o("vector", lambda E: E.tensor_add(hb, hb, ha))
        P.o("vector", lambda E: E.scalar_tensor_tensor(hb, hb, 0.5, hd_, ALU.mult, ALU.add))
        P.o("vector", lambda E: E.tensor_mul(hb, hb, sgh))
        P.o("vector", lambda E: E.tensor_tensor(
            wdec[:, 1::2].rearrange("p (a c) -> p a c", c=16),
            hb.rearrange("p (a c) -> p a c", c=16), scv, ALU.mult), inc="P_dec")
        P.o("scalar", lambda E: E.activation(la, lo, AF.Identity, bias=self.cbm4),
            inc="A_nm4l")
        P.w("vector", "A_sgl")
        P.w("vector", "A_nm4l")
        P.o("vector", lambda E: E.scalar_tensor_tensor(lb, sgl, 4.0, la, ALU.mult, ALU.add),
            inc="D_kl")
        P.w("scalar", "D_kl")
        P.o("scalar", lambda E: E.activation(la, lb, AF.Relu, bias=self.cbm4))
        P.o("scalar", lambda E: E.activation(ld_, lb, AF.Relu, bias=self.cbm6), inc="A_r2l")
        P.w("vector", "A_r2l")
        P.o("vector", lambda E: E.tensor_add(la, la, lb))
        P.o("vector", lambda E: E.scalar_tensor_tensor(la, la, 0.5, ld_, ALU.mult, ALU.add))
        P.o("vector", lambda E: E.tensor_mul(la, la, sgl))
        P.o("vector", lambda E: E.tensor_tensor(
            wdec[:, 0::2].rearrange("p (a c) -> p a c", c=16),
            la.rearrange("p (a c) -> p a c", c=16), scv, ALU.mult), inc="D_dec")

        P.w("tensor", "D_dec", t + 1)
        P.w("tensor", "P_dec", t + 1)
        if t >= 1:
            P.w("scalar", "L_stg", 16 * t)
        for r in range(4):
            P.w("tensor", "A_stg", 4 * t + r)
            for k in range(4):
                kc = 4 * r + k
                P.o("tensor", lambda E, kc=kc, k=k: E.matmul(
                    PS3[:, 512 + 128 * k:512 + 128 * (k + 1)],
                    wdec[:, 128 * kc:128 * (kc + 1)], self.ident,
                    start=True, stop=True), inc=("PE_dqT" if k == 3 else None))
            P.w("scalar", "PE_dqT")
            P.o("scalar", lambda E, r=r: E.activation(
                wstg[:, 512 * r:512 * (r + 1)], PS3[:, 512:1024], AF.Copy),
                inc="A_stg")
        P.w("sync", "A_stg", 4 * (t + 1))
        if mat in ("g", "u"):
            dst = (self.contrib_g if mat == "g" else self.contrib_u)[ti // 4]
            dview = dst.ap().rearrange("(kc p) n -> p kc n", p=128)[
                :, :, 128 * (ti % 4):128 * (ti % 4 + 1)]
        else:
            dview = self.contrib_d.ap().rearrange("(nc p) h -> p nc h", p=128)[
                :, :, 128 * ti:128 * (ti + 1)]
        sview = wstg.rearrange("p (kc n) -> p kc n", n=128)
        P.o("sync", lambda E: E.dma_start(out=dview, in_=sview), inc="L_stg", inc_n=16)

    def emit_ag(self, kind, idx):
        P = self.P
        if kind == "g":
            src, dst, wait = self.contrib_g[idx], self.gath_g[idx], self.agw[("g", idx)]
        elif kind == "u":
            src, dst, wait = self.contrib_u[idx], self.gath_u[idx], self.agw[("u", idx)]
        else:
            src = self.contrib_d.ap()[512 * idx:512 * (idx + 1), :]
            dst, wait = self.gath_d[idx], self.agw[("d", idx)]
        P.w("gpsimd", "L_stg", wait)
        sem = f"CC_{kind}{idx}"
        if kind == "d":
            P.o("gpsimd", lambda E: E.collective_compute(
                "AllGather", ALU.bypass, replica_groups=[list(range(NC))],
                ins=[src.opt()], outs=[dst.ap().opt()]), inc=sem)
        else:
            P.o("gpsimd", lambda E: E.collective_compute(
                "AllGather", ALU.bypass, replica_groups=[list(range(NC))],
                ins=[src.ap().opt()], outs=[dst.ap().opt()]), inc=sem)

    def emit_sweep_chunk(self, g):
        P, L = self.P, self.lay
        par = g % 2
        xb = self.V(L.xring + 8192 * par, 2048, F32)
        xnb = self.V(L.wq0 + 4096 * par, 2048, BF16)
        cs = self.V(L.csring + 1024 * par, 128, F32)
        sn = self.V(L.csring + 1024 * par + 512, 128, F32)
        k0 = self.V(L.k0, 128, BF16)
        vt0 = self.V(L.vt0, 128, BF16)
        u1 = self.V(L.u1, 512, F32)
        u2 = self.V(L.u2, 512, F32)
        PSp = self.PS[par]
        PS2 = self.PS[2]
        own = (g % 4 == 3)
        m = g // 4

        if g >= 2:
            P.w("sync", "A_xn", g - 1)
        P.o("sync", lambda E: E.dma_start(out=xb, in_=self.xseq[128 * g:128 * (g + 1), :]),
            inc="L_x", inc_n=16)
        if g >= 2:
            P.w("scalar", "D_kv2", g - 1)
        P.o("scalar", lambda E: E.dma_start(out=cs, in_=self.cost[:, 128 * g:128 * (g + 1)]),
            inc="L_cs", inc_n=16)
        P.o("scalar", lambda E: E.dma_start(out=sn, in_=self.sint[:, 128 * g:128 * (g + 1)]),
            inc="L_cs", inc_n=16)

        if "dump" in self.dbg and g == 0:
            dx0 = self.nc.dram_tensor("dbg_x0", [128, 2048], F32, kind="ExternalOutput").ap()
            P.o("sync", lambda E, xb=xb, dx0=dx0: E.dma_start(out=dx0, in_=xb),
                inc="L_dbg5", inc_n=16)
        wb = [("vector", P.sem("L_x"), P.sem("L_x").n),
              ("scalar", P.sem("L_x"), P.sem("L_x").n)]
        if g >= 2:
            wb.append(("vector", P.sem("PE_xnt"), 2 * (g - 1)))
            wb.append(("scalar", P.sem("PE_xnt"), 2 * (g - 1)))
        self.emit_ln(xb, xnb, par, wait_before=wb)

        if "dump" in self.dbg and g == 0:
            dxn0 = self.nc.dram_tensor("dbg_xn0", [128, 2048], BF16, kind="ExternalOutput").ap()
            dln0 = self.nc.dram_tensor("dbg_ln0", [128, 128], F32, kind="ExternalOutput").ap()
            P.w("sync", "A_xn", P.sem("A_xn").n)
            P.o("sync", lambda E, xnb=xnb, dxn0=dxn0: E.dma_start(out=dxn0, in_=xnb),
                inc="L_dbg5", inc_n=16)
            P.o("sync", lambda E, dln0=dln0: E.dma_start(
                out=dln0, in_=self.V(L.lnt, 128, F32)), inc="L_dbg5", inc_n=16)
        dst_own = self.xnt_own.rearrange("p (hc t) -> p hc t", t=1024)
        dst_sw = self.V(L.xnt_sweep + 4096 * par, 16 * 128, BF16)
        P.w("tensor", "A_xn")
        for half in range(2):
            P.w("tensor", "D_xnte", (2 * (g - 1) if g >= 2 else 0) if half == 0 else 2 * g + 1)
            for i in range(8):
                hc = 8 * half + i
                P.o("tensor", lambda E, hc=hc, i=i: E.matmul(
                    PSp[:, 128 * i:128 * (i + 1)],
                    xnb[:, 128 * hc:128 * (hc + 1)], self.ident,
                    start=True, stop=True), inc=("PE_xnt" if i == 7 else None))
            P.w("scalar", "PE_xnt")
            if own:
                dv = dst_own[:, 8 * half:8 * (half + 1), 128 * m:128 * (m + 1)]
            else:
                dv = dst_sw.rearrange("p (hc t) -> p hc t", t=128)[:, 8 * half:8 * (half + 1), :]
                if g >= 2:
                    P.w("scalar", "PE_kv1", g - 1)
            P.o("scalar", lambda E, dv=dv: E.activation(
                dv, PSp[:, :].rearrange("p (hc t) -> p hc t", t=128), AF.Copy),
                inc="D_xnte")

        xnt_src = dst_own[:, :, 128 * m:128 * (m + 1)] if own else \
            dst_sw.rearrange("p (hc t) -> p hc t", t=128)
        P.w("tensor", "D_xnte")
        P.w("tensor", "D_kv2", g)
        for hc in range(16):
            P.o("tensor", lambda E, hc=hc: E.matmul(
                PS2[:, 0:128],
                self.wkb.rearrange("p (h d) -> p h d", d=128)[:, hc, :],
                xnt_src[:, hc, :], start=(hc == 0), stop=(hc == 15)))
        for hc in range(16):
            P.o("tensor", lambda E, hc=hc: E.matmul(
                PS2[:, 128:256],
                self.wvb.rearrange("p (h d) -> p h d", d=128)[:, hc, :],
                xnt_src[:, hc, :], start=(hc == 0), stop=(hc == 15)),
                inc=("PE_kv1" if hc == 15 else None))
        P.w("vector", "PE_kv1")
        P.o("vector", lambda E: E.tensor_copy(k0, PS2[:, 0:128]))
        P.o("vector", lambda E: E.tensor_copy(vt0, PS2[:, 128:256]), inc="D_kv1")
        P.w("tensor", "D_kv1")
        P.o("tensor", lambda E: E.matmul(PS2[:, 256:384], vt0, self.ident,
                                         start=True, stop=True))
        P.o("tensor", lambda E: E.matmul(PS2[:, 384:512], self.rotT, k0,
                                         start=True, stop=True), inc="PE_kv2")
        P.w("vector", "PE_kv2")
        P.w("vector", "L_cs")
        P.o("vector", lambda E: E.tensor_copy(
            self.Vsb[:, 128 * g:128 * (g + 1)], PS2[:, 256:384]))
        P.o("vector", lambda E: E.tensor_tensor(u1[:, 0:128], k0, cs, ALU.mult))
        P.o("vector", lambda E: E.tensor_tensor(u2[:, 0:128], PS2[:, 384:512], sn, ALU.mult))
        P.o("vector", lambda E: E.tensor_add(
            self.KT[:, 128 * g:128 * (g + 1)], u1[:, 0:128], u2[:, 0:128]),
            inc="D_kv2")

    def emit_q_pre(self):
        P, L = self.P, self.lay
        P.w("tensor", "D_xnte")
        P.w("tensor", "D_kv2")
        self._q0 = self.V(L.r6 + 8192, 1024, BF16)
        self._cosq = self.V(L.r6, 1024, F32)
        self._sinq = self.V(L.r6 + 4096, 1024, F32)
        P.o("sync", lambda E: E.dma_start(out=self._cosq, in_=self.cosq), inc="L_cq", inc_n=16)
        P.o("sync", lambda E: E.dma_start(out=self._sinq, in_=self.sinq), inc="L_cq", inc_n=16)
        P.w("vector", "L_cq")
        P.w("gpsimd", "PE_xnt")

    def emit_q_group(self, dcg):
        P, L = self.P, self.lay
        PS2 = self.PS[2]
        q0, cosq, sinq = self._q0, self._cosq, self._sinq
        slab = self.V(L.wq0 + 16384 * (dcg % 2), 16 * 512, BF16)
        if dcg >= 2:
            P.w("gpsimd", "PE_qmm", 4 * (dcg - 1))
        P.o("gpsimd", lambda E, dcg=dcg, slab=slab: E.dma_start(
            out=slab.rearrange("p (h d) -> p h d", d=512),
            in_=self.wq.rearrange("(h p) d -> p h d", p=128)[:, :, 512 * dcg:512 * (dcg + 1)]),
            inc="L_wq", inc_n=16)
        P.w("tensor", "L_wq", 16 * (dcg + 1))
        for dc4 in range(4):
            dc = 4 * dcg + dc4
            PSq = self.PS[dc % 2]
            if dc >= 2:
                P.w("tensor", "D_q0", dc - 1)
            for hc in range(16):
                for th in range(2):
                    P.o("tensor", lambda E, hc=hc, th=th, slab=slab, dc4=dc4, PSq=PSq: E.matmul(
                        PSq[:, 512 * th:512 * (th + 1)],
                        slab.rearrange("p (h d) -> p h d", d=512)[:, hc, 128 * dc4:128 * (dc4 + 1)],
                        self.xnt_own.rearrange("p (hc t) -> p hc t", t=1024)[:, hc, 512 * th:512 * (th + 1)],
                        start=(hc == 0), stop=(hc == 15)),
                        inc=("PE_qmm" if hc == 15 and th == 1 else None))
            P.w("scalar", "PE_qmm")
            if dc >= 1:
                P.w("scalar", "D_qr", dc)
            P.o("scalar", lambda E, PSq=PSq: E.activation(q0, PSq[:, :], AF.Copy), inc="D_q0")
            P.w("tensor", "D_q0")
            if dc >= 1:
                P.w("tensor", "D_qr", dc)
            for th in range(2):
                P.o("tensor", lambda E, th=th: E.matmul(
                    PS2[:, 512 * th:512 * (th + 1)], self.rotT,
                    q0[:, 512 * th:512 * (th + 1)], start=True, stop=True),
                    inc=("PE_qrk" if th == 1 else None))
            P.w("vector", "PE_qrk")
            u1 = self.V(L.r6 + 10240, 512, F32)
            for th in range(2):
                P.o("vector", lambda E, th=th, dc=dc: E.tensor_tensor(
                    u1, q0[:, 512 * th:512 * (th + 1)],
                    cosq[:, 512 * th:512 * (th + 1)], ALU.mult))
                P.o("vector", lambda E, th=th, dc=dc: E.tensor_tensor(
                    self.QT[:, 1024 * dc + 512 * th:1024 * dc + 512 * (th + 1)],
                    PS2[:, 512 * th:512 * (th + 1)],
                    sinq[:, 512 * th:512 * (th + 1)], ALU.mult))
                P.o("vector", lambda E, th=th, dc=dc: E.tensor_add(
                    self.QT[:, 1024 * dc + 512 * th:1024 * dc + 512 * (th + 1)],
                    self.QT[:, 1024 * dc + 512 * th:1024 * dc + 512 * (th + 1)],
                    u1), inc=("D_qr" if th == 1 else None))

    def emit_attention(self):
        P, L = self.P, self.lay
        PS2 = self.PS[2]
        PS3 = self.PS[3]
        amask = self.amask
        P.w("sync", "PE_qmm")
        P.o("sync", lambda E: E.dma_start(out=amask, in_=self.amaskD),
            inc="L_am", inc_n=16)
        P.w("vector", "L_am")
        P.w("tensor", "D_qr")
        P.w("tensor", "D_q0")
        P.w("tensor", "A_stg")
        self.Ggl = 0
        self.Hc = 0
        self.xm_r = 0
        self.m_cnt = 0
        for pm in range(4):
            for head in range(16):
                self.emit_attn_head(pm, head)
            for qc in range(2):
                self.emit_xmid_ln2(pm, qc)

    def emit_attn_head(self, pm, head):
        P, L = self.P, self.lay
        PS2 = self.PS[2]
        hp = head % 2
        PSh = PS2 if hp == 0 else self.PS[3]
        pv = PSh[:, 0:256]
        sm = PSh[:, 512:768]
        ng = 2 * pm + 2
        Hc = self.Hc
        self.Hc += 1
        qs = 256 * pm
        last_jc = 8 * pm + 7
        P.w("tensor", "D_att", max(0, Hc - 1))
        if hp == 1 and pm >= 1:
            P.w("tensor", "D_x2e", 8 * pm)

        def emit_S(gg, Ggl, slot):
            Spar = self.PS[Ggl % 2]
            P.w("tensor", "A_exp", max(0, Ggl - 1))
            for k in range(4):
                jc = 4 * gg + k
                P.o("tensor", lambda E, jc=jc, k=k, Spar=Spar: E.matmul(
                    Spar[:, 256 * k:256 * (k + 1)],
                    self.KT[:, 128 * jc:128 * (jc + 1)],
                    self.QT[:, 1024 * head + qs:1024 * head + qs + 256],
                    start=True, stop=True), inc=("PE_S" if k == 3 else None))
            P.w("scalar", "PE_S")
            P.w("scalar", "PE_ptf", max(0, Ggl - 2))
            P.o("scalar", lambda E, slot=slot, Spar=Spar: E.activation(
                slot, Spar[:, :], AF.Exp, scale=ISQ, bias=self.cbsh), inc="A_exp")
            zone = gg >= ng - 2
            if zone:
                zi = gg - (ng - 2)
                P.w("vector", "A_exp")
                P.o("vector", lambda E, slot=slot, zi=zi: E.tensor_tensor(
                    slot, slot, self.amask[:, 1024 * zi:1024 * (zi + 1)], ALU.mult),
                    inc="D_msk")
            return zone

        def emit_PV(gg, zone, slot):
            if zone:
                P.w("tensor", "D_msk")
            else:
                P.w("tensor", "A_exp", self.exp_at[gg])
            for k in range(4):
                jc = 4 * gg + k
                P.o("tensor", lambda E, jc=jc, k=k, slot=slot: E.matmul(
                    pv, self.Vsb[:, 128 * jc:128 * (jc + 1)],
                    slot[:, 256 * k:256 * (k + 1)],
                    start=(jc == 0), stop=(jc == last_jc)))
                P.o("tensor", lambda E, jc=jc, k=k, slot=slot: E.matmul(
                    sm, self.ones, slot[:, 256 * k:256 * (k + 1)],
                    start=(jc == 0), stop=(jc == last_jc)),
                    inc=("PE_ptf" if k == 3 else None))

        self.exp_at = {}
        pend = None
        for gg in range(ng):
            Ggl = self.Ggl
            self.Ggl += 1
            slot = self.V(L.PT + 2048 * (Ggl % 3), 1024, BF16)
            zone = emit_S(gg, Ggl, slot)
            self.exp_at[gg] = self.P.sem("A_exp").n
            if pend is not None:
                emit_PV(*pend)
            pend = (gg, zone, slot)
        emit_PV(*pend)
        self._attn_tail(pm, head, pv, sm)


    def _attn_tail(self, pm, head, pv, sm):
        P, L = self.P, self.lay
        recip = self.V(L.recip, 256, F32)
        att = self.V(L.attnT + 8192 * (pm % 2), 16 * 256, BF16)
        P.w("vector", "PE_ptf")
        if "dump" in self.dbg and pm == 0 and head == 0:
            dkt = self.nc.dram_tensor("dbg_kt", [128, 4096], BF16, kind="ExternalOutput").ap()
            dqt = self.nc.dram_tensor("dbg_qt", [128, 16384], BF16, kind="ExternalOutput").ap()
            dvs = self.nc.dram_tensor("dbg_vs", [128, 4096], BF16, kind="ExternalOutput").ap()
            dam = self.nc.dram_tensor("dbg_am", [128, 2048], BF16, kind="ExternalOutput").ap()
            P.o("sync", lambda E: E.dma_start(out=dkt, in_=self.KT), inc="L_dbg3", inc_n=16)
            P.o("sync", lambda E: E.dma_start(out=dqt, in_=self.QT), inc="L_dbg3", inc_n=16)
            P.o("sync", lambda E: E.dma_start(out=dvs, in_=self.Vsb), inc="L_dbg3", inc_n=16)
            P.o("sync", lambda E: E.dma_start(out=dam, in_=self.amask), inc="L_dbg3", inc_n=16)
        if "dump" in self.dbg and pm == 1 and head == 0:
            pvd = self.V(L.r6 + 18432, 256, F32)
            smd = self.V(L.r6 + 19456, 256, F32)
            P.o("vector", lambda E: E.tensor_copy(pvd, pv))
            P.o("vector", lambda E: E.tensor_copy(smd, sm), inc="D_dbgpv")
            P.w("sync", "D_dbgpv")
            dpv = self.nc.dram_tensor("dbg_pv", [128, 512], F32, kind="ExternalOutput").ap()
            P.o("sync", lambda E, dpv=dpv: E.dma_start(out=dpv[:, 0:256], in_=pvd), inc="L_dbg2", inc_n=16)
            P.o("sync", lambda E, dpv=dpv: E.dma_start(out=dpv[:, 256:512], in_=smd), inc="L_dbg2", inc_n=16)
        P.o("vector", lambda E, sm=sm: E.reciprocal(recip, sm))
        P.o("vector", lambda E, pv=pv, head=head, att=att: E.tensor_tensor(
            att[:, 256 * head:256 * (head + 1)], pv, recip, ALU.mult), inc="D_att")

    def emit_xmid_ln2(self, pm, qc):
        P, L = self.P, self.lay
        PS3 = self.PS[3]
        m = 2 * pm + qc
        xmid = self.V(L.xmid, 2048, F32)
        att = self.V(L.attnT + 8192 * (pm % 2), 16 * 256, BF16)
        xn2b = self.V(L.r6 + 14336, 2048, BF16)
        if m >= 1:
            P.w("sync", "A_xn", P.sem("A_xn").n)
            P.w("sync", "L_xmD", 16 * m)
        g = 4 * m + 3
        P.o("sync", lambda E, g=g, xmid=xmid: E.dma_start(
            out=xmid, in_=self.xseq[128 * g:128 * (g + 1), :]), inc="L_xo", inc_n=16)
        P.w("tensor", "D_att")
        for r in range(4):
            xr = self.xm_r
            self.xm_r += 1
            P.w("tensor", "D_xm", max(0, xr))
            if r == 0 and m >= 1:
                P.w("tensor", "D_x2e", 4 * m)
            for i in range(4):
                head = 4 * r + i
                P.o("tensor", lambda E, head=head, i=i, att=att, qc=qc: E.matmul(
                    PS3[:, 128 * i:128 * (i + 1)],
                    att[:, 256 * head + 128 * qc:256 * head + 128 * (qc + 1)],
                    self.ident, start=True, stop=True),
                    inc=("PE_xm" if i == 3 else None))
            P.w("vector", "PE_xm")
            if r == 0:
                P.w("vector", "L_xo", 16 * (m + 1))
            P.o("vector", lambda E, r=r, xmid=xmid: E.scalar_tensor_tensor(
                xmid[:, 512 * r:512 * (r + 1)], PS3[:, 0:512], 1.0,
                xmid[:, 512 * r:512 * (r + 1)], ALU.mult, ALU.add), inc="D_xm")
        P.w("sync", "D_xm", P.sem("D_xm").n)
        P.o("sync", lambda E, m=m, xmid=xmid: E.dma_start(
            out=self.xmidD.ap()[128 * m:128 * (m + 1), :], in_=xmid),
            inc="L_xmD", inc_n=16)
        wb = [("vector", P.sem("D_xm"), P.sem("D_xm").n),
              ("scalar", P.sem("D_xm"), P.sem("D_xm").n)]
        if m >= 1:
            wb.append(("vector", P.sem("PE_x2t"), 4 * m))
            wb.append(("scalar", P.sem("PE_x2t"), 4 * m))
        self.emit_ln(xmid, xn2b, m, wait_before=wb, lnt_off=L.r6 + 12288 + 1024 * (m % 2))
        P.w("tensor", "A_xn")
        dst = self.xn2T.rearrange("p (hc t) -> p hc t", t=1024)
        for r in range(4):
            P.w("tensor", "D_x2e", 4 * m + r)
            for i in range(4):
                hc = 4 * r + i
                P.o("tensor", lambda E, hc=hc, i=i, xn2b=xn2b: E.matmul(
                    PS3[:, 128 * i:128 * (i + 1)],
                    xn2b[:, 128 * hc:128 * (hc + 1)], self.ident,
                    start=True, stop=True), inc=("PE_x2t" if i == 3 else None))
            P.w("vector", "PE_x2t")
            P.o("vector", lambda E, r=r, m=m, dst=dst: E.tensor_copy(
                dst[:, 4 * r:4 * (r + 1), 128 * m:128 * (m + 1)],
                PS3[:, 0:512].rearrange("p (hc t) -> p hc t", t=128)), inc="D_x2e")

    def emit_gate(self):
        P, L = self.P, self.lay
        for eng in ("sync", "vector", "scalar", "tensor"):
            for sem in ("PE_dqT", "D_dec", "P_dec", "L_stg", "L_pk"):
                P.w(eng, sem)
        for sem in ("A_exp", "D_att", "D_x2e", "A_stg", "D_qr"):
            P.w("tensor", sem)
        self.gn = 0
        un = 0
        for sl in range(4):
            P.w("sync", f"CC_g{sl}", 1)
            P.w("sync", f"CC_u{sl}", 1)
            for rr in range(8):
                for h2 in range(2):
                    self.emit_gate_unit(sl, rr, h2, un)
                    un += 1

    def emit_gate_unit(self, sl, rr, h2, un):
        P, L = self.P, self.lay
        par = un % 2
        wg = self.V(L.gslab + 8192 * par, 16 * 256, BF16)
        wu = self.V(L.gslab + 16384 + 8192 * par, 16 * 256, BF16)
        gelu = self.V(L.gelu, 1024, F32)
        if un >= 2:
            P.w("sync", "PE_gmm", 2 * (un - 1))
        for (wsl, gath) in ((wg, self.gath_g[sl]), (wu, self.gath_u[sl])):
            P.o("sync", lambda E, wsl=wsl, gath=gath, rr=rr, h2=h2: E.dma_start(
                out=wsl.rearrange("p (h n) -> p h n", n=256),
                in_=gath.ap().rearrange("(r h p) n -> r p h n", p=128, h=16)[rr][
                    :, :, 256 * h2:256 * (h2 + 1)]), inc="L_gs", inc_n=16)
        P.w("tensor", "L_gs", 32 * (un + 1))
        for nc2 in range(2):
            gn = self.gn
            self.gn += 1
            psG = self.PS[2 * (gn % 2)]
            psU = self.PS[2 * (gn % 2) + 1]
            if gn >= 2:
                P.w("tensor", "A_gelu", gn - 1)
                P.w("tensor", "D_int", gn - 1)
            for wsl, ps in ((wg, psG), (wu, psU)):
                for hc in range(16):
                    for th in range(2):
                        lastg = wsl is wu and hc == 15 and th == 1
                        P.o("tensor", lambda E, wsl=wsl, ps=ps, hc=hc, th=th, nc2=nc2: E.matmul(
                            ps[:, 512 * th:512 * (th + 1)],
                            wsl.rearrange("p (h n) -> p h n", n=256)[:, hc, 128 * nc2:128 * (nc2 + 1)],
                            self.xn2T.rearrange("p (hc t) -> p hc t", t=1024)[:, hc, 512 * th:512 * (th + 1)],
                            start=(hc == 0), stop=(hc == 15)),
                            inc=("PE_gmm" if lastg else None))
            P.w("scalar", "PE_gmm")
            if gn >= 1:
                P.w("scalar", "D_int", gn)
            P.o("scalar", lambda E, psG=psG, gelu=gelu: E.activation(
                gelu, psG[:, :], AF.Gelu_apprx_tanh), inc="A_gelu")
            ib = self.V(L.inter + 2048 * (gn % 2), 1024, BF16)
            P.w("vector", "A_gelu")
            if gn >= 2:
                P.w("vector", "L_int", 16 * (gn - 1))
            P.o("vector", lambda E, ib=ib, gelu=gelu, psU=psU: E.tensor_tensor(
                ib, gelu, psU[:, :], ALU.mult), inc="D_int")
            nrow = 2048 * rr + 512 * sl + 128 * (2 * h2 + nc2)
            P.w("sync", "D_int")
            P.o("sync", lambda E, ib=ib, nrow=nrow: E.dma_start(
                out=self.interT.ap()[nrow:nrow + 128, :], in_=ib),
                inc="L_int", inc_n=16)

    def emit_down(self):
        P, L = self.P, self.lay
        for sem in ("A_gelu", "D_int"):
            P.w("tensor", sem)
        for sem in ("PE_ptf", "PE_xm", "A_xn", "D_att", "D_msk", "L_int", "PE_gmm",
                    "L_xmD", "L_xo", "L_am", "L_cq"):
            P.w("sync", sem)
        acc = [self.V((L.acc0 if t < 4 else L.acc1) + 8192 * (t % 4), 2048, F32)
               for t in range(8)]
        for t in range(8):
            P.o("sync", lambda E, t=t, a=acc[t]: E.dma_start(
                out=a, in_=self.xmidD.ap()[128 * t:128 * (t + 1), :]),
                inc="L_seed", inc_n=16)
        P.w("vector", "L_seed")
        P.w("tensor", "L_seed")
        gidx = 0
        gt = 0
        for i_d in range(4):
            P.w("sync", f"CC_d{i_d}", 1)
            for rr in range(8):
                par = gidx % 2
                wdt = [self.V((L.dn_wdt if par == 0 else L.r6) + 4096 * k, 2048, BF16)
                       for k in range(4)]
                itt = [self.V(L.xn2T + 8192 * par + 2048 * k, 1024, BF16)
                       for k in range(4)]
                if gidx >= 2:
                    P.w("sync", "PE_dn", 8 * (gidx - 1))
                for k in range(4):
                    P.o("sync", lambda E, k=k, i_d=i_d, rr=rr, w=wdt[k]: E.dma_start(
                        out=w, in_=self.gath_d[i_d].ap().rearrange(
                            "(r k p) h -> r k p h", k=4, p=128)[rr, k]),
                        inc="L_dn", inc_n=16)
                    P.o("sync", lambda E, k=k, i_d=i_d, rr=rr, it=itt[k]: E.dma_start(
                        out=it, in_=self.interT.ap().rearrange(
                            "(r s k p) t -> r s k p t", s=4, k=4, p=128)[rr, i_d, k]),
                        inc="L_dn", inc_n=16)
                P.w("tensor", "L_dn", 128 * (gidx + 1))
                for t in range(8):
                    pa = self.PS[2 * (t % 2)]
                    pb = self.PS[2 * (t % 2) + 1]
                    if gt >= 2:
                        P.w("tensor", "D_acc", gt - 1)
                    for k in range(4):
                        for hr in range(4):
                            ps = (pa, pa, pb, pb)[hr][:, 512 * (hr % 2):512 * (hr % 2 + 1)]
                            P.o("tensor", lambda E, k=k, hr=hr, t=t, ps=ps, w=wdt[k], it=itt[k]: E.matmul(
                                ps, it[:, 128 * t:128 * (t + 1)],
                                w[:, 512 * hr:512 * (hr + 1)],
                                start=(k == 0), stop=(k == 3)),
                                inc=("PE_dn" if k == 3 and hr == 3 else None))
                    P.w("vector", "PE_dn")
                    P.o("vector", lambda E, pa=pa, t=t, a=acc[t]: E.scalar_tensor_tensor(
                        a[:, 0:1024], pa[:, :], 1.0, a[:, 0:1024], ALU.mult, ALU.add))
                    P.o("vector", lambda E, pb=pb, t=t, a=acc[t]: E.scalar_tensor_tensor(
                        a[:, 1024:2048], pb[:, :], 1.0, a[:, 1024:2048], ALU.mult, ALU.add),
                        inc="D_acc")
                    if i_d == 3 and rr == 7:
                        P.w("sync", "D_acc")
                        P.o("sync", lambda E, t=t, a=acc[t]: E.dma_start(
                            out=self.out[128 * t:128 * (t + 1), :], in_=a),
                            inc="L_out", inc_n=16)
                    gt += 1
                gidx += 1
        P.w("sync", "L_out", 16 * 8)

    def schedule(self):
        self.deq_cnt = 0
        self.ln_cnt = 0
        self.agw = {}
        self.boot()
        sweeponly = ("sweeponly" in self.dbg) or ("swdeq" in self.dbg)
        nodeq = ("sweeponly" in self.dbg) or ("nodeq" in self.dbg)
        dorder = []
        for blk in range(4):
            dorder += [("g", 4 * blk + i) for i in range(4)]
            dorder += [("u", 4 * blk + i) for i in range(4)]
        for g in range(SEQ_CH):
            self.emit_sweep_chunk(g)
            if not nodeq:
                mat, ti = dorder[g]
                self.emit_deq_tile(mat, ti)
                if g % 4 == 3:
                    mat, ti = dorder[g]
                    sl = ti // 4
                    self.agw[(mat, sl)] = self.P.sem("L_stg").n
                    self.emit_ag(mat, sl)
        if sweeponly:
            P = self.P
            dkt = self.nc.dram_tensor("dbg_kt", [128, 4096], BF16, kind="ExternalOutput").ap()
            dvs = self.nc.dram_tensor("dbg_vs", [128, 4096], BF16, kind="ExternalOutput").ap()
            dxo = self.nc.dram_tensor("dbg_xo", [128, 16384], BF16, kind="ExternalOutput").ap()
            P.w("sync", "D_kv2")
            P.w("sync", "D_xnte")
            P.o("sync", lambda E: E.dma_start(out=dkt, in_=self.KT), inc="L_dbg4", inc_n=16)
            P.o("sync", lambda E: E.dma_start(out=dvs, in_=self.Vsb), inc="L_dbg4", inc_n=16)
            P.o("sync", lambda E: E.dma_start(out=dxo, in_=self.xnt_own), inc="L_dbg4", inc_n=16)
            P.w("sync", "L_dbg4", 48)
            self.P.emit()
            return
        self.emit_q_pre()
        for dcg in range(4):
            self.emit_q_group(dcg)
            if not nodeq:
                for i in range(4):
                    self.emit_deq_tile("d", 4 * dcg + i)
        if not nodeq:
            for i in range(4):
                self.agw[("d", i)] = self.P.sem("L_stg").n
                self.emit_ag("d", i)
        if "noattn" in self.dbg:
            P = self.P
            dkt = self.nc.dram_tensor("dbg_kt", [128, 4096], BF16, kind="ExternalOutput").ap()
            dvs = self.nc.dram_tensor("dbg_vs", [128, 4096], BF16, kind="ExternalOutput").ap()
            dqt = self.nc.dram_tensor("dbg_qt", [128, 16384], BF16, kind="ExternalOutput").ap()
            for sem in ("D_qr", "D_dec", "P_dec", "A_stg"):
                P.w("sync", sem)
            P.o("sync", lambda E: E.dma_start(out=dkt, in_=self.KT), inc="L_dbg4", inc_n=16)
            P.o("sync", lambda E: E.dma_start(out=dvs, in_=self.Vsb), inc="L_dbg4", inc_n=16)
            P.o("sync", lambda E: E.dma_start(out=dqt, in_=self.QT), inc="L_dbg4", inc_n=16)
            dar = self.nc.dram_tensor("dbg_arena", [128, self.ar.size], U8, kind="ExternalOutput").ap()
            P.o("sync", lambda E: E.dma_start(out=dar, in_=self.ar.h[0:128, :]), inc="L_dbg4", inc_n=16)
            P.w("sync", "L_dbg4", 64)
            if not nodeq:
                for i in range(4):
                    P.w("gpsimd", f"CC_d{i}", 1)
            self.P.emit()
            return
        self.emit_attention()
        self.emit_gate()
        self.emit_down()
        if "dump" in self.dbg:
            P = self.P
            dxm = self.nc.dram_tensor("dbg_xmid", [TPC, EMB], F32, kind="ExternalOutput").ap()
            dcg = self.nc.dram_tensor("dbg_cg", [EMB, 512], BF16, kind="ExternalOutput").ap()
            dgd = self.nc.dram_tensor("dbg_gd", [512, EMB], BF16, kind="ExternalOutput").ap()
            dint = self.nc.dram_tensor("dbg_int", [DFF, TPC], BF16, kind="ExternalOutput").ap()
            dx2 = self.nc.dram_tensor("dbg_xn2t", [128, 16 * TPC], BF16, kind="ExternalOutput").ap()
            P.w("sync", "L_out", 16 * 8)
            P.o("sync", lambda E: E.dma_start(out=dxm, in_=self.xmidD.ap()), inc="L_dbg", inc_n=16)
            P.o("sync", lambda E: E.dma_start(out=dcg, in_=self.contrib_g[0].ap()), inc="L_dbg", inc_n=16)
            P.o("sync", lambda E: E.dma_start(out=dgd, in_=self.gath_d[0].ap()[0:512, :]), inc="L_dbg", inc_n=16)
            P.o("sync", lambda E: E.dma_start(out=dint, in_=self.interT.ap()), inc="L_dbg", inc_n=16)
            P.o("sync", lambda E: E.dma_start(out=dx2, in_=self.xn2T), inc="L_dbg", inc_n=16)
            P.w("sync", "L_dbg", 80)
        self.P.emit()


def build_nc(dbg=()):
    nc = bass.Bass("TRN2", target_bir_lowering=False)
    b = Build(nc, dbg=dbg)
    b.schedule()
    return nc


def kernel(**inputs):
    maps = make_in_maps(inputs)
    nc = build_nc()
    res = run_bass_kernel_spmd(nc, maps, core_ids=list(range(NC)))
    return assemble_output(res.results)


# version 04
# speedup vs baseline: 1.0322x; 1.0322x over previous
import math
import numpy as np
import ml_dtypes
import concourse.bass as bass
import concourse.mybir as mybir
from concourse.bass_utils import run_bass_kernel_spmd

F32 = mybir.dt.float32
BF16 = mybir.dt.bfloat16
I32 = mybir.dt.int32
U8 = mybir.dt.uint8
AF = mybir.ActivationFunctionType
ALU = mybir.AluOpType
AX = mybir.AxisListType

B, L, EMB, HD, NH, DFF = 2, 4096, 2048, 128, 16, 16384
NC = 8
SEQ_CH = L // 128
OWN_CH = 8
TPC = OWN_CH * 128
NPC = DFF // NC
ISQ = 1.0 / math.sqrt(128.0)
CSHIFT = 10.0
EPS = 1e-5

BF = ml_dtypes.bfloat16


def _bf(x):
    return np.ascontiguousarray(x.astype(BF))



def host_constants():
    ident = np.eye(128, dtype=np.float32)
    rt = np.zeros((128, 128), dtype=np.float32)
    for i in range(64):
        rt[2 * i + 1, 2 * i] = -1.0
        rt[2 * i, 2 * i + 1] = 1.0
    ones = np.ones((128, 128), dtype=np.float32)
    tri = (np.arange(128)[None, :] >= np.arange(128)[:, None]).astype(np.float32)
    inv = 1.0 / (10000.0 ** (np.arange(0, HD, 2, dtype=np.float32) / HD))
    t = np.arange(L, dtype=np.float32)
    f = np.outer(t, inv)
    f = np.repeat(f, 2, axis=-1)
    cosT = np.cos(f).astype(np.float32).T.copy()
    sinT = np.sin(f).astype(np.float32).T.copy()
    return ident, rt, ones, tri, cosT, sinT


def chunk_perm(j):
    perm = []
    for m in range(SEQ_CH // 4):
        others = [4 * m + t for t in range(4) if t != j]
        perm.extend(others + [4 * m + j])
    return np.array(perm, dtype=np.int64)


def build_amask(j, tri):
    ones = np.ones((128, 128), dtype=np.float32)
    zeros = np.zeros((128, 128), dtype=np.float32)

    def A(s):
        if s == 3:
            return tri
        return ones if s < j else zeros

    slots = []
    for sp in range(8):
        if sp < 4:
            m = np.concatenate([A(sp), ones], axis=1)
        else:
            m = np.concatenate([zeros, A(sp - 4)], axis=1)
        slots.append(m)
    return np.concatenate(slots, axis=1)


def make_in_maps(inputs):
    x = np.asarray(inputs["L0_attn_x"])
    wq = np.asarray(inputs["L0_attn_Wq"])
    wk = np.asarray(inputs["L0_attn_Wk"])
    wv = np.asarray(inputs["L0_attn_Wv"])
    gpk = np.asarray(inputs["L0_ffn_gate_W_packed"])
    gsc = np.asarray(inputs["L0_ffn_gate_scales"])
    upk = np.asarray(inputs["L0_ffn_up_W_packed"])
    usc = np.asarray(inputs["L0_ffn_up_scales"])
    dpk = np.asarray(inputs["L0_ffn_down_W_packed"])
    dsc = np.asarray(inputs["L0_ffn_down_scales"])
    ident, rt, ones, tri, cosT, sinT = host_constants()

    maps = []
    for c in range(NC):
        s, j = c // 4, c % 4
        perm = chunk_perm(j)
        xs = x[s].reshape(SEQ_CH, 128, EMB)[perm].reshape(L, EMB)
        ct = cosT.reshape(128, SEQ_CH, 128)[:, perm].reshape(128, L)
        st = sinT.reshape(128, SEQ_CH, 128)[:, perm].reshape(128, L)
        own = [4 * m + j for m in range(OWN_CH)]
        cq = cosT.reshape(128, SEQ_CH, 128)[:, own].reshape(128, TPC)
        sq = sinT.reshape(128, SEQ_CH, 128)[:, own].reshape(128, TPC)
        m = {
            "xseq": np.ascontiguousarray(xs),
            "wq": np.ascontiguousarray(wq),
            "wk": np.ascontiguousarray(wk),
            "wv": np.ascontiguousarray(wv),
            "gpk": np.ascontiguousarray(gpk[NPC * c:NPC * (c + 1)]),
            "gsc": np.ascontiguousarray(gsc[NPC * c:NPC * (c + 1)]),
            "upk": np.ascontiguousarray(upk[NPC * c:NPC * (c + 1)]),
            "usc": np.ascontiguousarray(usc[NPC * c:NPC * (c + 1)]),
            "dpk": np.ascontiguousarray(dpk[:, 1024 * c:1024 * (c + 1)]),
            "dsc": np.ascontiguousarray(dsc[:, 64 * c:64 * (c + 1)]),
            "cost": np.ascontiguousarray(ct),
            "sint": np.ascontiguousarray(st),
            "cosq": np.ascontiguousarray(cq),
            "sinq": np.ascontiguousarray(sq),
            "ident": _bf(ident),
            "rotT": _bf(rt),
            "ones": _bf(ones),
            "amask": _bf(build_amask(j, tri)),
        }
        maps.append(m)
    return maps


def assemble_output(results):
    out = np.zeros((B, L, EMB), dtype=np.float32)
    for c in range(NC):
        s, j = c // 4, c % 4
        shard = results[c]["out"].reshape(OWN_CH, 128, EMB)
        for m in range(OWN_CH):
            g = 4 * m + j
            out[s, 128 * g:128 * (g + 1)] = shard[m]
    return out



ENGINES = ("sync", "tensor", "vector", "scalar", "gpsimd")


class Sem:
    __slots__ = ("h", "n", "virtual", "eng", "counts")

    def __init__(self, nc, name):
        self.h = nc.alloc_semaphore(name)
        self.n = 0
        self.virtual = False


class VSem:
    __slots__ = ("n", "eng", "counts", "virtual")

    def __init__(self):
        self.n = 0
        self.eng = None
        self.counts = []
        self.virtual = True


CHAINED = ("vector", "scalar")


class Prog:
    def __init__(self, nc):
        self.nc = nc
        self.st = {e: [] for e in ENGINES}
        self.sems = {}
        self.chain = {e: Sem(nc, f"CH_{e}") for e in CHAINED}

    def sem(self, name):
        if name not in self.sems:
            self.sems[name] = VSem()
        return self.sems[name]

    def _realize(self, name):
        v = self.sems.get(name)
        if v is None or v.virtual and v.eng is None and not v.counts and v.n == 0:
            pass
        return v

    def o(self, eng, fn, inc=None, inc_n=1):
        chained = eng in CHAINED and inc_n == 1
        if chained:
            ch = self.chain[eng]
            prior = ch.n
            ch.n += 1
            if inc is not None:
                vs = self.sem(inc)
                if isinstance(vs, Sem):
                    raise RuntimeError(f"sem {inc} already real, inc from {eng}")
                if vs.eng is None:
                    vs.eng = eng
                assert vs.eng == eng, (inc, vs.eng, eng)
                vs.n += 1
                vs.counts.append(ch.n)

            def wrapped(E, fn=fn, h=ch.h, prior=prior):
                if prior:
                    E.wait_ge(h, prior)
                fn(E).then_inc(h, 1)
            self.st[eng].append(wrapped)
        else:
            if inc is not None:
                vs = self.sems.get(inc)
                if vs is None or isinstance(vs, VSem):
                    if vs is not None and (vs.counts or vs.eng):
                        raise RuntimeError(f"sem {inc} already virtual")
                    self.sems[inc] = Sem(self.nc, inc)
                sem = self.sems[inc]
                prior = sem.n
                sem.n += inc_n
                if inc_n == 16 and prior:
                    self.st[eng].append(lambda E, h=sem.h, v=prior: E.wait_ge(h, v))

                def wrapped(E, fn=fn, h=sem.h, k=inc_n):
                    fn(E).then_inc(h, k)
                self.st[eng].append(wrapped)
            else:
                self.st[eng].append(fn)

    def w(self, eng, sem, val=None):
        if isinstance(sem, str):
            sem = self.sem(sem)
        v = sem.n if val is None else val
        if v <= 0:
            return
        if getattr(sem, "virtual", False):
            assert sem.eng is not None, "wait on never-incremented virtual sem"
            assert v <= len(sem.counts), (v, len(sem.counts))
            h = self.chain[sem.eng].h
            tgt = sem.counts[v - 1]
            self.st[eng].append(lambda E, h=h, t=tgt: E.wait_ge(h, t))
        else:
            self.st[eng].append(lambda E, h=sem.h, v=v: E.wait_ge(h, v))

    def emit(self):
        with self.nc.Block() as blk:
            for e in ENGINES:
                ops = self.st[e]
                if not ops:
                    continue

                def runner(E, ops=ops):
                    for f in ops:
                        f(E)
                getattr(blk, e)(runner)


ESZ = {F32: 4, BF16: 2, I32: 4, U8: 1}


class Arena:

    def __init__(self, nc, size):
        self.h = nc.alloc_sbuf_tensor("arena", [128, size], U8)
        self.size = size

    def view(self, off, n_elems, dtype, p=128):
        nb = n_elems * ESZ[dtype]
        assert off % 64 == 0 and off + nb <= self.size, (off, nb, self.size)
        return self.h[0:p, off:off + nb].bitcast(dtype)


class Layout:

    def __init__(self):
        o = 0

        def take(nb):
            nonlocal o
            off = (o + 63) & ~63
            o = off + nb
            return off

        self.cbank = take(64)
        self.ident = take(256)
        self.rotT = take(256)
        self.ones = take(256)
        self.wkb = take(4096)
        self.wvb = take(4096)
        self.PT = self.wkb
        self.KT = take(8192)
        self.Vsb = take(8192)
        self.dn_wdt = self.KT
        self.dn_int = None
        r2 = take(32768)
        self.QT = r2
        self.acc0 = r2
        s = r2
        self.xring = s; s += 16384
        self.xnt_sweep = s; s += 8192
        self.csring = s; s += 2048
        self.k0 = s; s += 256
        self.vt0 = s; s += 256
        self.u1 = s; s += 2048
        self.u2 = s; s += 2048
        self.lnt = s; s += 1024
        assert s - r2 <= 32768, s - r2
        r3 = take(32768)
        self.xnt_own = r3
        self.acc1 = r3
        s = r3
        self.attnT = s; s += 16384
        self.recip = s; s += 1024
        self.xmid = s; s += 8192
        self.amask = s; s += 4096
        assert s - r3 <= 32768, s - r3
        r4 = take(32768)
        self.xn2T = r4
        self.wq0 = r4
        self.wq1 = r4 + 16384
        self.accq = None
        r5 = take(45056)
        s = r5
        self.pk = s; s += 8192
        self.scl = s; s += 1024
        self.dq = s; s += 26624
        self.wdec = s; s += 4096
        self.wstg = s; s += 4096
        assert s - r5 <= 45056, s - r5
        self.gslab = r5
        self.gelu = r5 + 32768
        self.inter = r5 + 32768 + 4096
        r6 = take(24576)
        self.r6 = r6
        self.total = (o + 63) & ~63



class Build:
    def __init__(self, nc, dbg=()):
        self.nc = nc
        self.dbg = set(dbg)
        self.P = Prog(nc)
        self.lay = Layout()
        self.ar = Arena(nc, self.lay.total)
        L = self.lay

        def din(name, shape, dt):
            return nc.dram_tensor(name, shape, dt, kind="ExternalInput").ap()

        self.xseq = din("xseq", [L_ := 4096, EMB], F32)
        self.wq = din("wq", [EMB, EMB], F32)
        self.wk = din("wk", [EMB, HD], F32)
        self.wv = din("wv", [EMB, HD], F32)
        self.gpk = din("gpk", [NPC, 1024], I32)
        self.gsc = din("gsc", [NPC, 64], F32)
        self.upk = din("upk", [NPC, 1024], I32)
        self.usc = din("usc", [NPC, 64], F32)
        self.dpk = din("dpk", [EMB, 1024], I32)
        self.dsc = din("dsc", [EMB, 64], F32)
        self.cost = din("cost", [128, 4096], F32)
        self.sint = din("sint", [128, 4096], F32)
        self.cosq = din("cosq", [128, TPC], F32)
        self.sinq = din("sinq", [128, TPC], F32)
        self.identD = din("ident", [128, 128], BF16)
        self.rotD = din("rotT", [128, 128], BF16)
        self.onesD = din("ones", [128, 128], BF16)
        self.amaskD = din("amask", [128, 2048], BF16)
        self.out = nc.dram_tensor("out", [TPC, EMB], F32, kind="ExternalOutput").ap()

        self.contrib_g = [nc.dram_tensor(f"cg{s}", [EMB, 512], BF16) for s in range(4)]
        self.contrib_u = [nc.dram_tensor(f"cu{s}", [EMB, 512], BF16) for s in range(4)]
        self.contrib_d = nc.dram_tensor("cd", [NPC, EMB], BF16)
        self.gath_g = [nc.dram_tensor(f"gg{s}", [DFF, 512], BF16) for s in range(4)]
        self.gath_u = [nc.dram_tensor(f"gu{s}", [DFF, 512], BF16) for s in range(4)]
        self.gath_d = [nc.dram_tensor(f"gd{i}", [4096, EMB], BF16) for i in range(4)]
        self.interT = nc.dram_tensor("interT", [DFF, TPC], BF16)
        self.xmidD = nc.dram_tensor("xmidD", [TPC, EMB], F32)

        self.PS = [nc.alloc_psum_tensor(f"ps{i}", [128, 1024], F32) for i in range(4)]

        a = self.ar
        self.ident = a.view(L.ident, 128 * 128 // 128, BF16)
        self.rotT = a.view(L.rotT, 128, BF16)
        self.ones = a.view(L.ones, 128, BF16)
        self.wkb = a.view(L.wkb, 16 * 128, BF16)
        self.wvb = a.view(L.wvb, 16 * 128, BF16)
        self.KT = a.view(L.KT, 4096, BF16)
        self.Vsb = a.view(L.Vsb, 4096, BF16)
        self.QT = a.view(L.QT, 16 * 1024, BF16)
        self.xn2T = a.view(L.xn2T, 16 * 1024, BF16)
        self.xnt_own = a.view(L.xnt_own, 16 * 1024, BF16)
        self.amask = a.view(L.amask, 2048, BF16)
        cb = a.view(L.cbank, 16, F32)
        self.cb75 = cb[:, 0:1]
        self.cbm6 = cb[:, 1:2]
        self.cbeps = cb[:, 2:3]
        self.cbsh = cb[:, 3:4]
        self.cbm4 = cb[:, 4:5]

    def V(self, off, n, dt):
        return self.ar.view(off, n, dt)

    def boot(self):
        P, L = self.P, self.lay
        for src, off in ((self.identD, L.ident), (self.rotD, L.rotT),
                         (self.onesD, L.ones)):
            n = src.shape[1]
            P.o("sync", lambda E, s=src, o=off, n=n: E.dma_start(
                out=self.V(o, n, BF16), in_=s), inc="Lc", inc_n=16)
        for src, off in ((self.wk, L.wkb), (self.wv, L.wvb)):
            P.o("gpsimd", lambda E, s=src, o=off: E.dma_start(
                out=self.V(o, 16 * 128, BF16).rearrange("p (h d) -> p h d", d=128),
                in_=s.rearrange("(h p) d -> p h d", p=128)), inc="Lcg", inc_n=16)
        P.o("vector", lambda E: E.memset(self.cb75, 7.5))
        P.o("vector", lambda E: E.memset(self.cbm6, -6.0))
        P.o("vector", lambda E: E.memset(self.cbeps, EPS))
        P.o("vector", lambda E: E.memset(self.cbm4, -4.0))
        P.o("vector", lambda E: E.memset(self.cbsh, -CSHIFT), inc="D_cb")
        self.P.w("tensor", "Lc")
        self.P.w("tensor", "Lcg")
        self.P.w("vector", "Lc")
        self.P.w("vector", "Lcg")
        self.P.w("scalar", "Lc")
        self.P.w("scalar", "Lcg")
        self.P.w("scalar", "D_cb")
        self.P.w("gpsimd", "Lc")

    def emit_ln(self, xap, xnap, par, wait_before=None, lnt_off=None):
        P, L = self.P, self.lay
        lnt = (L.lnt + 1024 * (par & 1)) if lnt_off is None else lnt_off
        cnt = self.ln_cnt
        self.ln_cnt += 1
        s1 = self.V(lnt, 8, F32)
        sq4 = self.V(lnt + 64, 8, F32)
        mm = self.V(lnt + 128, 8, F32)
        msq = self.V(lnt + 192, 8, F32)
        var = self.V(lnt + 256, 8, F32)
        rsd = self.V(lnt + 320, 8, F32)
        nb = self.V(lnt + 384, 8, F32)
        sqs = self.V(lnt + 448, 8, F32)
        PS3 = self.PS[3]

        if wait_before:
            for eng, sem, v in wait_before:
                P.w(eng, sem, v)
        P.o("vector", lambda E: E.reduce_sum(s1[:, 0:1], xap, AX.X), inc="D_s1")
        if cnt >= 2:
            P.w("scalar", "D_var", cnt - 1)
        P.o("scalar", lambda E: E.activation(xnap, xap, AF.Square), inc="A_sqp")
        P.w("vector", "A_sqp")
        P.o("vector", lambda E: E.reduce_sum(sqs[:, 0:1], xnap, AX.X), inc="D_sqr")
        P.o("vector", lambda E: E.tensor_scalar_mul(mm[:, 0:1], s1[:, 0:1], 1.0 / EMB))
        P.o("vector", lambda E: E.tensor_mul(msq[:, 0:1], mm[:, 0:1], mm[:, 0:1]))
        P.o("vector", lambda E: E.scalar_tensor_tensor(
            var[:, 0:1], sqs[:, 0:1], 1.0 / EMB, msq[:, 0:1],
            ALU.mult, ALU.subtract), inc="D_var")
        P.w("scalar", "D_var")
        if cnt >= 2:
            P.w("scalar", "D_ln", cnt - 1)
        P.o("scalar", lambda E: E.activation(rsd[:, 0:1], var[:, 0:1], AF.Sqrt,
                                             bias=self.cbeps), inc="A_sqrt")
        P.w("vector", "A_sqrt")
        if cnt >= 2:
            P.w("vector", "A_xn", cnt - 1)
        P.o("vector", lambda E: E.reciprocal(rsd[:, 0:1], rsd[:, 0:1]))
        P.o("vector", lambda E: E.scalar_tensor_tensor(
            nb[:, 0:1], mm[:, 0:1], -1.0, rsd[:, 0:1], ALU.mult, ALU.mult),
            inc="D_ln")
        P.w("scalar", "D_ln")
        P.o("scalar", lambda E: E.activation(
            xnap, xap, AF.Identity, bias=nb[:, 0:1], scale=rsd[:, 0:1]),
            inc="A_xn")
        if "dump" in self.dbg and cnt == 32:
            snap = self.V(L.r6 + 20480, 128, F32)
            P.o("vector", lambda E, snap=snap: E.tensor_copy(
                snap[:, 0:16], self.V(lnt, 16, F32)[:, 0:16]))
            P.o("vector", lambda E, snap=snap: E.tensor_copy(
                snap[:, 16:32], self.V(lnt + 448, 16, F32)[:, 0:16]))
            P.o("vector", lambda E, snap=snap: E.tensor_copy(
                snap[:, 32:48], self.V(lnt + 256, 16, F32)[:, 0:16]))
            P.o("vector", lambda E, snap=snap: E.tensor_copy(
                snap[:, 48:64], self.V(lnt + 320, 16, F32)[:, 0:16]), inc="D_snap")
            dsn = self.nc.dram_tensor("dbg_ln2m0", [128, 128], F32, kind="ExternalOutput").ap()
            P.w("sync", "D_snap")
            P.o("sync", lambda E, dsn=dsn, snap=snap: E.dma_start(out=dsn, in_=snap),
                inc="L_dbg6", inc_n=16)

    def emit_deq_tile(self, mat, ti):
        P, L = self.P, self.lay
        t = self.deq_cnt
        self.deq_cnt += 1
        par = t % 2
        pk = self.V(L.pk + 4096 * par, 1024, I32)
        scl = self.V(L.scl + 512 * par, 64, F32)
        src_pk = {"g": self.gpk, "u": self.upk, "d": self.dpk}[mat]
        src_sc = {"g": self.gsc, "u": self.usc, "d": self.dsc}[mat]
        dqo = L.dq
        wf = self.V(dqo, 1024, BF16)
        lo = self.V(dqo + 2048, 1024, BF16)
        sgl = self.V(dqo + 4096, 1024, BF16)
        la = self.V(dqo + 6144, 1024, BF16)
        lb = self.V(dqo + 8192, 1024, BF16)
        ld_ = self.V(dqo + 10240, 1024, BF16)
        dqh = L.dq + 12288
        hi_i = self.V(dqh, 1024, I32)
        hi = self.V(dqh + 4096, 1024, BF16)
        sgh = self.V(dqh + 6144, 1024, BF16)
        ha = self.V(dqh + 8192, 1024, BF16)
        hb = self.V(dqh + 10240, 1024, BF16)
        hd_ = self.V(dqh + 12288, 1024, BF16)
        wdec = self.V(L.wdec, 2048, BF16)
        wstg = self.V(L.wstg, 16 * 128, BF16)
        PS3 = self.PS[3]

        if t >= 2:
            P.w("sync", "D_dec", t - 1)
            P.w("sync", "P_dec", t - 1)
        P.o("sync", lambda E: E.dma_start(out=pk, in_=src_pk[128 * ti:128 * (ti + 1), :]),
            inc="L_pk", inc_n=16)
        P.o("sync", lambda E: E.dma_start(out=scl, in_=src_sc[128 * ti:128 * (ti + 1), :]),
            inc="L_pk", inc_n=16)

        scv = scl.unsqueeze(2).broadcast_to([128, 64, 16])
        P.w("vector", "L_pk", 32 * (t + 1))
        if t >= 1:
            P.w("vector", "PE_dqT", 4 * t)
        P.o("vector", lambda E: E.tensor_single_scalar(hi_i, pk, 4, ALU.logical_shift_right))
        P.o("vector", lambda E: E.tensor_copy(hi, hi_i), inc="P_hi")
        P.o("vector", lambda E: E.tensor_copy(wf, pk))
        P.w("scalar", "P_hi")
        P.o("scalar", lambda E: E.activation(sgh, hi, AF.Sign, bias=self.cb75, scale=-1.0), inc="A_sgh")
        P.o("vector", lambda E: E.scalar_tensor_tensor(lo, hi, -16.0, wf, ALU.mult, ALU.add),
            inc="D_lo")
        P.w("scalar", "D_lo")
        P.o("scalar", lambda E: E.activation(sgl, lo, AF.Sign, bias=self.cb75, scale=-1.0), inc="A_sgl")
        P.o("scalar", lambda E: E.activation(hb, hi, AF.Identity, bias=self.cbm4),
            inc="A_nm4h")
        P.w("vector", "A_sgh")
        P.w("vector", "A_nm4h")
        P.o("vector", lambda E: E.scalar_tensor_tensor(ha, sgh, 4.0, hb, ALU.mult, ALU.add),
            inc="P_kh")
        P.w("scalar", "P_kh")
        P.o("scalar", lambda E: E.activation(hb, ha, AF.Relu, bias=self.cbm4))
        P.o("scalar", lambda E: E.activation(hd_, ha, AF.Relu, bias=self.cbm6), inc="A_r2h")
        P.w("vector", "A_r2h")
        P.o("vector", lambda E: E.tensor_add(hb, hb, ha))
        P.o("vector", lambda E: E.scalar_tensor_tensor(hb, hb, 0.5, hd_, ALU.mult, ALU.add))
        P.o("vector", lambda E: E.tensor_mul(hb, hb, sgh))
        P.o("vector", lambda E: E.tensor_tensor(
            wdec[:, 1::2].rearrange("p (a c) -> p a c", c=16),
            hb.rearrange("p (a c) -> p a c", c=16), scv, ALU.mult), inc="P_dec")
        P.o("scalar", lambda E: E.activation(la, lo, AF.Identity, bias=self.cbm4),
            inc="A_nm4l")
        P.w("vector", "A_sgl")
        P.w("vector", "A_nm4l")
        P.o("vector", lambda E: E.scalar_tensor_tensor(lb, sgl, 4.0, la, ALU.mult, ALU.add),
            inc="D_kl")
        P.w("scalar", "D_kl")
        P.o("scalar", lambda E: E.activation(la, lb, AF.Relu, bias=self.cbm4))
        P.o("scalar", lambda E: E.activation(ld_, lb, AF.Relu, bias=self.cbm6), inc="A_r2l")
        P.w("vector", "A_r2l")
        P.o("vector", lambda E: E.tensor_add(la, la, lb))
        P.o("vector", lambda E: E.scalar_tensor_tensor(la, la, 0.5, ld_, ALU.mult, ALU.add))
        P.o("vector", lambda E: E.tensor_mul(la, la, sgl))
        P.o("vector", lambda E: E.tensor_tensor(
            wdec[:, 0::2].rearrange("p (a c) -> p a c", c=16),
            la.rearrange("p (a c) -> p a c", c=16), scv, ALU.mult), inc="D_dec")

        P.w("tensor", "D_dec", t + 1)
        P.w("tensor", "P_dec", t + 1)
        if t >= 1:
            P.w("scalar", "L_stg", 16 * t)
        for r in range(4):
            P.w("tensor", "A_stg", 4 * t + r)
            for k in range(4):
                kc = 4 * r + k
                P.o("tensor", lambda E, kc=kc, k=k: E.matmul(
                    PS3[:, 512 + 128 * k:512 + 128 * (k + 1)],
                    wdec[:, 128 * kc:128 * (kc + 1)], self.ident,
                    start=True, stop=True), inc=("PE_dqT" if k == 3 else None))
            P.w("scalar", "PE_dqT")
            P.o("scalar", lambda E, r=r: E.activation(
                wstg[:, 512 * r:512 * (r + 1)], PS3[:, 512:1024], AF.Copy),
                inc="A_stg")
        P.w("sync", "A_stg", 4 * (t + 1))
        if mat in ("g", "u"):
            dst = (self.contrib_g if mat == "g" else self.contrib_u)[ti // 4]
            dview = dst.ap().rearrange("(kc p) n -> p kc n", p=128)[
                :, :, 128 * (ti % 4):128 * (ti % 4 + 1)]
        else:
            dview = self.contrib_d.ap().rearrange("(nc p) h -> p nc h", p=128)[
                :, :, 128 * ti:128 * (ti + 1)]
        sview = wstg.rearrange("p (kc n) -> p kc n", n=128)
        P.o("sync", lambda E: E.dma_start(out=dview, in_=sview), inc="L_stg", inc_n=16)

    def emit_ag(self, kind, idx):
        P = self.P
        if kind == "g":
            src, dst, wait = self.contrib_g[idx], self.gath_g[idx], self.agw[("g", idx)]
        elif kind == "u":
            src, dst, wait = self.contrib_u[idx], self.gath_u[idx], self.agw[("u", idx)]
        else:
            src = self.contrib_d.ap()[512 * idx:512 * (idx + 1), :]
            dst, wait = self.gath_d[idx], self.agw[("d", idx)]
        P.w("gpsimd", "L_stg", wait)
        sem = f"CC_{kind}{idx}"
        if kind == "d":
            P.o("gpsimd", lambda E: E.collective_compute(
                "AllGather", ALU.bypass, replica_groups=[list(range(NC))],
                ins=[src.opt()], outs=[dst.ap().opt()]), inc=sem)
        else:
            P.o("gpsimd", lambda E: E.collective_compute(
                "AllGather", ALU.bypass, replica_groups=[list(range(NC))],
                ins=[src.ap().opt()], outs=[dst.ap().opt()]), inc=sem)

    def emit_sweep_chunk(self, g):
        P, L = self.P, self.lay
        par = g % 2
        xb = self.V(L.xring + 8192 * par, 2048, F32)
        xnb = self.V(L.wq0 + 4096 * par, 2048, BF16)
        cs = self.V(L.csring + 1024 * par, 128, F32)
        sn = self.V(L.csring + 1024 * par + 512, 128, F32)
        k0 = self.V(L.k0, 128, BF16)
        vt0 = self.V(L.vt0, 128, BF16)
        u1 = self.V(L.u1, 512, F32)
        u2 = self.V(L.u2, 512, F32)
        PSp = self.PS[par]
        PS2 = self.PS[2]
        own = (g % 4 == 3)
        m = g // 4

        if g >= 2:
            P.w("sync", "A_xn", g - 1)
        P.o("sync", lambda E: E.dma_start(out=xb, in_=self.xseq[128 * g:128 * (g + 1), :]),
            inc="L_x", inc_n=16)
        if g >= 2:
            P.w("scalar", "D_kv2", g - 1)
        P.o("scalar", lambda E: E.dma_start(out=cs, in_=self.cost[:, 128 * g:128 * (g + 1)]),
            inc="L_cs", inc_n=16)
        P.o("scalar", lambda E: E.dma_start(out=sn, in_=self.sint[:, 128 * g:128 * (g + 1)]),
            inc="L_cs", inc_n=16)

        if "dump" in self.dbg and g == 0:
            dx0 = self.nc.dram_tensor("dbg_x0", [128, 2048], F32, kind="ExternalOutput").ap()
            P.o("sync", lambda E, xb=xb, dx0=dx0: E.dma_start(out=dx0, in_=xb),
                inc="L_dbg5", inc_n=16)
        wb = [("vector", P.sem("L_x"), P.sem("L_x").n),
              ("scalar", P.sem("L_x"), P.sem("L_x").n)]
        if g >= 2:
            wb.append(("vector", P.sem("PE_xnt"), 2 * (g - 1)))
            wb.append(("scalar", P.sem("PE_xnt"), 2 * (g - 1)))
        self.emit_ln(xb, xnb, par, wait_before=wb)

        if "dump" in self.dbg and g == 0:
            dxn0 = self.nc.dram_tensor("dbg_xn0", [128, 2048], BF16, kind="ExternalOutput").ap()
            dln0 = self.nc.dram_tensor("dbg_ln0", [128, 128], F32, kind="ExternalOutput").ap()
            P.w("sync", "A_xn", P.sem("A_xn").n)
            P.o("sync", lambda E, xnb=xnb, dxn0=dxn0: E.dma_start(out=dxn0, in_=xnb),
                inc="L_dbg5", inc_n=16)
            P.o("sync", lambda E, dln0=dln0: E.dma_start(
                out=dln0, in_=self.V(L.lnt, 128, F32)), inc="L_dbg5", inc_n=16)
        dst_own = self.xnt_own.rearrange("p (hc t) -> p hc t", t=1024)
        dst_sw = self.V(L.xnt_sweep + 4096 * par, 16 * 128, BF16)
        P.w("tensor", "A_xn")
        for half in range(2):
            P.w("tensor", "D_xnte", (2 * (g - 1) if g >= 2 else 0) if half == 0 else 2 * g + 1)
            for i in range(8):
                hc = 8 * half + i
                P.o("tensor", lambda E, hc=hc, i=i: E.matmul(
                    PSp[:, 128 * i:128 * (i + 1)],
                    xnb[:, 128 * hc:128 * (hc + 1)], self.ident,
                    start=True, stop=True), inc=("PE_xnt" if i == 7 else None))
            P.w("scalar", "PE_xnt")
            if own:
                dv = dst_own[:, 8 * half:8 * (half + 1), 128 * m:128 * (m + 1)]
            else:
                dv = dst_sw.rearrange("p (hc t) -> p hc t", t=128)[:, 8 * half:8 * (half + 1), :]
                if g >= 2:
                    P.w("scalar", "PE_kv1", g - 1)
            P.o("scalar", lambda E, dv=dv: E.activation(
                dv, PSp[:, :].rearrange("p (hc t) -> p hc t", t=128), AF.Copy),
                inc="D_xnte")

        xnt_src = dst_own[:, :, 128 * m:128 * (m + 1)] if own else \
            dst_sw.rearrange("p (hc t) -> p hc t", t=128)
        P.w("tensor", "D_xnte")
        P.w("tensor", "D_kv2", g)
        for hc in range(16):
            P.o("tensor", lambda E, hc=hc: E.matmul(
                PS2[:, 0:128],
                self.wkb.rearrange("p (h d) -> p h d", d=128)[:, hc, :],
                xnt_src[:, hc, :], start=(hc == 0), stop=(hc == 15)))
        for hc in range(16):
            P.o("tensor", lambda E, hc=hc: E.matmul(
                PS2[:, 128:256],
                self.wvb.rearrange("p (h d) -> p h d", d=128)[:, hc, :],
                xnt_src[:, hc, :], start=(hc == 0), stop=(hc == 15)),
                inc=("PE_kv1" if hc == 15 else None))
        P.w("vector", "PE_kv1")
        P.o("vector", lambda E: E.tensor_copy(k0, PS2[:, 0:128]))
        P.o("vector", lambda E: E.tensor_copy(vt0, PS2[:, 128:256]), inc="D_kv1")
        P.w("tensor", "D_kv1")
        P.o("tensor", lambda E: E.matmul(PS2[:, 256:384], vt0, self.ident,
                                         start=True, stop=True))
        P.o("tensor", lambda E: E.matmul(PS2[:, 384:512], self.rotT, k0,
                                         start=True, stop=True), inc="PE_kv2")
        P.w("vector", "PE_kv2")
        P.w("vector", "L_cs")
        P.o("vector", lambda E: E.tensor_copy(
            self.Vsb[:, 128 * g:128 * (g + 1)], PS2[:, 256:384]))
        P.o("vector", lambda E: E.tensor_tensor(u1[:, 0:128], k0, cs, ALU.mult))
        P.o("vector", lambda E: E.tensor_tensor(u2[:, 0:128], PS2[:, 384:512], sn, ALU.mult))
        P.o("vector", lambda E: E.tensor_add(
            self.KT[:, 128 * g:128 * (g + 1)], u1[:, 0:128], u2[:, 0:128]),
            inc="D_kv2")

    def emit_q_pre(self):
        P, L = self.P, self.lay
        P.w("tensor", "D_xnte")
        P.w("tensor", "D_kv2")
        self._q0 = self.V(L.r6 + 8192, 1024, BF16)
        self._cosq = self.V(L.r6, 1024, F32)
        self._sinq = self.V(L.r6 + 4096, 1024, F32)
        P.o("sync", lambda E: E.dma_start(out=self._cosq, in_=self.cosq), inc="L_cq", inc_n=16)
        P.o("sync", lambda E: E.dma_start(out=self._sinq, in_=self.sinq), inc="L_cq", inc_n=16)
        P.w("vector", "L_cq")
        P.w("gpsimd", "PE_xnt")

    def emit_q_group(self, dcg):
        P, L = self.P, self.lay
        PS2 = self.PS[2]
        q0, cosq, sinq = self._q0, self._cosq, self._sinq
        slab = self.V(L.wq0 + 16384 * (dcg % 2), 16 * 512, BF16)
        if dcg >= 2:
            P.w("gpsimd", "PE_qmm", 4 * (dcg - 1))
        P.o("gpsimd", lambda E, dcg=dcg, slab=slab: E.dma_start(
            out=slab.rearrange("p (h d) -> p h d", d=512),
            in_=self.wq.rearrange("(h p) d -> p h d", p=128)[:, :, 512 * dcg:512 * (dcg + 1)]),
            inc="L_wq", inc_n=16)
        P.w("tensor", "L_wq", 16 * (dcg + 1))
        for dc4 in range(4):
            dc = 4 * dcg + dc4
            PSq = self.PS[dc % 2]
            if dc >= 2:
                P.w("tensor", "D_q0", dc - 1)
            for hc in range(16):
                for th in range(2):
                    P.o("tensor", lambda E, hc=hc, th=th, slab=slab, dc4=dc4, PSq=PSq: E.matmul(
                        PSq[:, 512 * th:512 * (th + 1)],
                        slab.rearrange("p (h d) -> p h d", d=512)[:, hc, 128 * dc4:128 * (dc4 + 1)],
                        self.xnt_own.rearrange("p (hc t) -> p hc t", t=1024)[:, hc, 512 * th:512 * (th + 1)],
                        start=(hc == 0), stop=(hc == 15)),
                        inc=("PE_qmm" if hc == 15 and th == 1 else None))
            P.w("scalar", "PE_qmm")
            if dc >= 1:
                P.w("scalar", "D_qr", dc)
            P.o("scalar", lambda E, PSq=PSq: E.activation(q0, PSq[:, :], AF.Copy), inc="D_q0")
            P.w("tensor", "D_q0")
            if dc >= 1:
                P.w("tensor", "D_qr", dc)
            for th in range(2):
                P.o("tensor", lambda E, th=th: E.matmul(
                    PS2[:, 512 * th:512 * (th + 1)], self.rotT,
                    q0[:, 512 * th:512 * (th + 1)], start=True, stop=True),
                    inc=("PE_qrk" if th == 1 else None))
            P.w("vector", "PE_qrk")
            u1 = self.V(L.r6 + 10240, 512, F32)
            for th in range(2):
                P.o("vector", lambda E, th=th, dc=dc: E.tensor_tensor(
                    u1, q0[:, 512 * th:512 * (th + 1)],
                    cosq[:, 512 * th:512 * (th + 1)], ALU.mult))
                P.o("vector", lambda E, th=th, dc=dc: E.tensor_tensor(
                    self.QT[:, 1024 * dc + 512 * th:1024 * dc + 512 * (th + 1)],
                    PS2[:, 512 * th:512 * (th + 1)],
                    sinq[:, 512 * th:512 * (th + 1)], ALU.mult))
                P.o("vector", lambda E, th=th, dc=dc: E.tensor_add(
                    self.QT[:, 1024 * dc + 512 * th:1024 * dc + 512 * (th + 1)],
                    self.QT[:, 1024 * dc + 512 * th:1024 * dc + 512 * (th + 1)],
                    u1), inc=("D_qr" if th == 1 else None))

    def emit_attention(self):
        P, L = self.P, self.lay
        PS2 = self.PS[2]
        PS3 = self.PS[3]
        amask = self.amask
        P.w("sync", "PE_qmm")
        P.o("sync", lambda E: E.dma_start(out=amask, in_=self.amaskD),
            inc="L_am", inc_n=16)
        P.w("vector", "L_am")
        P.w("tensor", "D_qr")
        P.w("tensor", "D_q0")
        P.w("tensor", "A_stg")
        self.Ggl = 0
        self.Hc = 0
        self.xm_r = 0
        self.m_cnt = 0
        for pm in range(4):
            for head in range(16):
                self.emit_attn_head(pm, head)
            for qc in range(2):
                self.emit_xmid_ln2(pm, qc)

    def emit_attn_head(self, pm, head):
        P, L = self.P, self.lay
        PS2 = self.PS[2]
        hp = head % 2
        PSh = PS2 if hp == 0 else self.PS[3]
        pv = PSh[:, 0:256]
        sm = PSh[:, 512:768]
        ng = 2 * pm + 2
        Hc = self.Hc
        self.Hc += 1
        qs = 256 * pm
        last_jc = 8 * pm + 7
        P.w("tensor", "D_att", max(0, Hc - 1))
        if hp == 1 and pm >= 1:
            P.w("tensor", "D_x2e", 8 * pm)

        def emit_S(gg, Ggl, slot):
            Spar = self.PS[Ggl % 2]
            P.w("tensor", "A_exp", max(0, Ggl - 1))
            for k in range(4):
                jc = 4 * gg + k
                P.o("tensor", lambda E, jc=jc, k=k, Spar=Spar: E.matmul(
                    Spar[:, 256 * k:256 * (k + 1)],
                    self.KT[:, 128 * jc:128 * (jc + 1)],
                    self.QT[:, 1024 * head + qs:1024 * head + qs + 256],
                    start=True, stop=True), inc=("PE_S" if k == 3 else None))
            P.w("scalar", "PE_S")
            P.w("scalar", "PE_ptf", max(0, Ggl - 2))
            P.o("scalar", lambda E, slot=slot, Spar=Spar: E.activation(
                slot, Spar[:, :], AF.Exp, scale=ISQ, bias=self.cbsh), inc="A_exp")
            zone = gg >= ng - 2
            if zone:
                zi = gg - (ng - 2)
                P.w("vector", "A_exp")
                P.o("vector", lambda E, slot=slot, zi=zi: E.tensor_tensor(
                    slot, slot, self.amask[:, 1024 * zi:1024 * (zi + 1)], ALU.mult),
                    inc="D_msk")
            return zone

        def emit_PV(gg, zone, slot):
            if zone:
                P.w("tensor", "D_msk")
            else:
                P.w("tensor", "A_exp", self.exp_at[gg])
            for k in range(4):
                jc = 4 * gg + k
                P.o("tensor", lambda E, jc=jc, k=k, slot=slot: E.matmul(
                    pv, self.Vsb[:, 128 * jc:128 * (jc + 1)],
                    slot[:, 256 * k:256 * (k + 1)],
                    start=(jc == 0), stop=(jc == last_jc)))
                P.o("tensor", lambda E, jc=jc, k=k, slot=slot: E.matmul(
                    sm, self.ones, slot[:, 256 * k:256 * (k + 1)],
                    start=(jc == 0), stop=(jc == last_jc)),
                    inc=("PE_ptf" if k == 3 else None))

        self.exp_at = {}
        pend = None
        for gg in range(ng):
            Ggl = self.Ggl
            self.Ggl += 1
            slot = self.V(L.PT + 2048 * (Ggl % 3), 1024, BF16)
            zone = emit_S(gg, Ggl, slot)
            self.exp_at[gg] = self.P.sem("A_exp").n
            if pend is not None:
                emit_PV(*pend)
            pend = (gg, zone, slot)
        emit_PV(*pend)
        self._attn_tail(pm, head, pv, sm)


    def _attn_tail(self, pm, head, pv, sm):
        P, L = self.P, self.lay
        recip = self.V(L.recip, 256, F32)
        att = self.V(L.attnT + 8192 * (pm % 2), 16 * 256, BF16)
        P.w("vector", "PE_ptf")
        if "dump" in self.dbg and pm == 0 and head == 0:
            dkt = self.nc.dram_tensor("dbg_kt", [128, 4096], BF16, kind="ExternalOutput").ap()
            dqt = self.nc.dram_tensor("dbg_qt", [128, 16384], BF16, kind="ExternalOutput").ap()
            dvs = self.nc.dram_tensor("dbg_vs", [128, 4096], BF16, kind="ExternalOutput").ap()
            dam = self.nc.dram_tensor("dbg_am", [128, 2048], BF16, kind="ExternalOutput").ap()
            P.o("sync", lambda E: E.dma_start(out=dkt, in_=self.KT), inc="L_dbg3", inc_n=16)
            P.o("sync", lambda E: E.dma_start(out=dqt, in_=self.QT), inc="L_dbg3", inc_n=16)
            P.o("sync", lambda E: E.dma_start(out=dvs, in_=self.Vsb), inc="L_dbg3", inc_n=16)
            P.o("sync", lambda E: E.dma_start(out=dam, in_=self.amask), inc="L_dbg3", inc_n=16)
        if "dump" in self.dbg and pm == 1 and head == 0:
            pvd = self.V(L.r6 + 18432, 256, F32)
            smd = self.V(L.r6 + 19456, 256, F32)
            P.o("vector", lambda E: E.tensor_copy(pvd, pv))
            P.o("vector", lambda E: E.tensor_copy(smd, sm), inc="D_dbgpv")
            P.w("sync", "D_dbgpv")
            dpv = self.nc.dram_tensor("dbg_pv", [128, 512], F32, kind="ExternalOutput").ap()
            P.o("sync", lambda E, dpv=dpv: E.dma_start(out=dpv[:, 0:256], in_=pvd), inc="L_dbg2", inc_n=16)
            P.o("sync", lambda E, dpv=dpv: E.dma_start(out=dpv[:, 256:512], in_=smd), inc="L_dbg2", inc_n=16)
        P.o("vector", lambda E, sm=sm: E.reciprocal(recip, sm))
        P.o("vector", lambda E, pv=pv, head=head, att=att: E.tensor_tensor(
            att[:, 256 * head:256 * (head + 1)], pv, recip, ALU.mult), inc="D_att")

    def emit_xmid_ln2(self, pm, qc):
        P, L = self.P, self.lay
        PS3 = self.PS[3]
        m = 2 * pm + qc
        xmid = self.V(L.xmid, 2048, F32)
        att = self.V(L.attnT + 8192 * (pm % 2), 16 * 256, BF16)
        xn2b = self.V(L.r6 + 14336, 2048, BF16)
        if m >= 1:
            P.w("sync", "A_xn", P.sem("A_xn").n)
            P.w("sync", "L_xmD", 16 * m)
        g = 4 * m + 3
        P.o("sync", lambda E, g=g, xmid=xmid: E.dma_start(
            out=xmid, in_=self.xseq[128 * g:128 * (g + 1), :]), inc="L_xo", inc_n=16)
        P.w("tensor", "D_att")
        for r in range(4):
            xr = self.xm_r
            self.xm_r += 1
            P.w("tensor", "D_xm", max(0, xr))
            if r == 0 and m >= 1:
                P.w("tensor", "D_x2e", 4 * m)
            for i in range(4):
                head = 4 * r + i
                P.o("tensor", lambda E, head=head, i=i, att=att, qc=qc: E.matmul(
                    PS3[:, 128 * i:128 * (i + 1)],
                    att[:, 256 * head + 128 * qc:256 * head + 128 * (qc + 1)],
                    self.ident, start=True, stop=True),
                    inc=("PE_xm" if i == 3 else None))
            P.w("vector", "PE_xm")
            if r == 0:
                P.w("vector", "L_xo", 16 * (m + 1))
            P.o("vector", lambda E, r=r, xmid=xmid: E.scalar_tensor_tensor(
                xmid[:, 512 * r:512 * (r + 1)], PS3[:, 0:512], 1.0,
                xmid[:, 512 * r:512 * (r + 1)], ALU.mult, ALU.add), inc="D_xm")
        P.w("sync", "D_xm", P.sem("D_xm").n)
        P.o("sync", lambda E, m=m, xmid=xmid: E.dma_start(
            out=self.xmidD.ap()[128 * m:128 * (m + 1), :], in_=xmid),
            inc="L_xmD", inc_n=16)
        wb = [("vector", P.sem("D_xm"), P.sem("D_xm").n),
              ("scalar", P.sem("D_xm"), P.sem("D_xm").n)]
        if m >= 1:
            wb.append(("vector", P.sem("PE_x2t"), 4 * m))
            wb.append(("scalar", P.sem("PE_x2t"), 4 * m))
        self.emit_ln(xmid, xn2b, m, wait_before=wb, lnt_off=L.r6 + 12288 + 1024 * (m % 2))
        P.w("tensor", "A_xn")
        dst = self.xn2T.rearrange("p (hc t) -> p hc t", t=1024)
        for r in range(4):
            P.w("tensor", "D_x2e", 4 * m + r)
            for i in range(4):
                hc = 4 * r + i
                P.o("tensor", lambda E, hc=hc, i=i, xn2b=xn2b: E.matmul(
                    PS3[:, 128 * i:128 * (i + 1)],
                    xn2b[:, 128 * hc:128 * (hc + 1)], self.ident,
                    start=True, stop=True), inc=("PE_x2t" if i == 3 else None))
            P.w("vector", "PE_x2t")
            P.o("vector", lambda E, r=r, m=m, dst=dst: E.tensor_copy(
                dst[:, 4 * r:4 * (r + 1), 128 * m:128 * (m + 1)],
                PS3[:, 0:512].rearrange("p (hc t) -> p hc t", t=128)), inc="D_x2e")

    def emit_gate(self):
        P, L = self.P, self.lay
        for eng in ("sync", "vector", "scalar", "tensor"):
            for sem in ("PE_dqT", "D_dec", "P_dec", "L_stg", "L_pk"):
                P.w(eng, sem)
        for sem in ("A_exp", "D_att", "D_x2e", "A_stg", "D_qr"):
            P.w("tensor", sem)
        self.gn = 0
        un = 0
        for sl in range(4):
            P.w("sync", f"CC_g{sl}", 1)
            P.w("sync", f"CC_u{sl}", 1)
            P.w("scalar", f"CC_g{sl}", 1)
            P.w("scalar", f"CC_u{sl}", 1)
            for rr in range(8):
                for h2 in range(2):
                    self.emit_gate_unit(sl, rr, h2, un)
                    un += 1

    def emit_gate_unit(self, sl, rr, h2, un):
        P, L = self.P, self.lay
        par = un % 2
        wg = self.V(L.gslab + 8192 * par, 16 * 256, BF16)
        wu = self.V(L.gslab + 16384 + 8192 * par, 16 * 256, BF16)
        gelu = self.V(L.gelu, 1024, F32)
        if un >= 2:
            P.w("scalar", "PE_gmm", 2 * (un - 1))
        for (wsl, gath) in ((wg, self.gath_g[sl]), (wu, self.gath_u[sl])):
            P.o("scalar", lambda E, wsl=wsl, gath=gath, rr=rr, h2=h2: E.dma_start(
                out=wsl.rearrange("p (h n) -> p h n", n=256),
                in_=gath.ap().rearrange("(r h p) n -> r p h n", p=128, h=16)[rr][
                    :, :, 256 * h2:256 * (h2 + 1)]), inc="L_gs", inc_n=16)
        P.w("tensor", "L_gs", 32 * (un + 1))
        for nc2 in range(2):
            gn = self.gn
            self.gn += 1
            psG = self.PS[2 * (gn % 2)]
            psU = self.PS[2 * (gn % 2) + 1]
            if gn >= 2:
                P.w("tensor", "A_gelu", gn - 1)
                P.w("tensor", "D_int", gn - 1)
            for wsl, ps in ((wg, psG), (wu, psU)):
                for hc in range(16):
                    for th in range(2):
                        lastg = wsl is wu and hc == 15 and th == 1
                        P.o("tensor", lambda E, wsl=wsl, ps=ps, hc=hc, th=th, nc2=nc2: E.matmul(
                            ps[:, 512 * th:512 * (th + 1)],
                            wsl.rearrange("p (h n) -> p h n", n=256)[:, hc, 128 * nc2:128 * (nc2 + 1)],
                            self.xn2T.rearrange("p (hc t) -> p hc t", t=1024)[:, hc, 512 * th:512 * (th + 1)],
                            start=(hc == 0), stop=(hc == 15)),
                            inc=("PE_gmm" if lastg else None))
            P.w("scalar", "PE_gmm")
            if gn >= 1:
                P.w("scalar", "D_int", gn)
            P.o("scalar", lambda E, psG=psG, gelu=gelu: E.activation(
                gelu, psG[:, :], AF.Gelu_apprx_tanh), inc="A_gelu")
            ib = self.V(L.inter + 2048 * (gn % 2), 1024, BF16)
            P.w("vector", "A_gelu")
            if gn >= 2:
                P.w("vector", "L_int", 16 * (gn - 1))
            P.o("vector", lambda E, ib=ib, gelu=gelu, psU=psU: E.tensor_tensor(
                ib, gelu, psU[:, :], ALU.mult), inc="D_int")
            nrow = 2048 * rr + 512 * sl + 128 * (2 * h2 + nc2)
            P.w("sync", "D_int")
            P.o("sync", lambda E, ib=ib, nrow=nrow: E.dma_start(
                out=self.interT.ap()[nrow:nrow + 128, :], in_=ib),
                inc="L_int", inc_n=16)

    def emit_down(self):
        P, L = self.P, self.lay
        for sem in ("A_gelu", "D_int"):
            P.w("tensor", sem)
        for sem in ("PE_ptf", "PE_xm", "A_xn", "D_att", "D_msk", "L_int", "PE_gmm",
                    "L_xmD", "L_xo", "L_am", "L_cq"):
            P.w("sync", sem)
        acc = [self.V((L.acc0 if t < 4 else L.acc1) + 8192 * (t % 4), 2048, F32)
               for t in range(8)]
        for t in range(8):
            P.o("sync", lambda E, t=t, a=acc[t]: E.dma_start(
                out=a, in_=self.xmidD.ap()[128 * t:128 * (t + 1), :]),
                inc="L_seed", inc_n=16)
        P.w("vector", "L_seed")
        P.w("tensor", "L_seed")
        gidx = 0
        gt = 0
        for i_d in range(4):
            P.w("sync", f"CC_d{i_d}", 1)
            P.w("scalar", f"CC_d{i_d}", 1)
            for rr in range(8):
                par = gidx % 2
                wdt = [self.V((L.dn_wdt if par == 0 else L.r6) + 4096 * k, 2048, BF16)
                       for k in range(4)]
                itt = [self.V(L.xn2T + 8192 * par + 2048 * k, 1024, BF16)
                       for k in range(4)]
                if gidx >= 2:
                    P.w("scalar", "PE_dn", 8 * (gidx - 1))
                    P.w("sync", "PE_dn", 8 * (gidx - 1))
                for k in range(4):
                    P.o("scalar", lambda E, k=k, i_d=i_d, rr=rr, w=wdt[k]: E.dma_start(
                        out=w, in_=self.gath_d[i_d].ap().rearrange(
                            "(r k p) h -> r k p h", k=4, p=128)[rr, k]),
                        inc="L_dnw", inc_n=16)
                    P.o("sync", lambda E, k=k, i_d=i_d, rr=rr, it=itt[k]: E.dma_start(
                        out=it, in_=self.interT.ap().rearrange(
                            "(r s k p) t -> r s k p t", s=4, k=4, p=128)[rr, i_d, k]),
                        inc="L_dn", inc_n=16)
                P.w("tensor", "L_dn", 64 * (gidx + 1))
                P.w("tensor", "L_dnw", 64 * (gidx + 1))
                for t in range(8):
                    pa = self.PS[2 * (t % 2)]
                    pb = self.PS[2 * (t % 2) + 1]
                    if gt >= 2:
                        P.w("tensor", "D_acc", gt - 1)
                    for k in range(4):
                        for hr in range(4):
                            ps = (pa, pa, pb, pb)[hr][:, 512 * (hr % 2):512 * (hr % 2 + 1)]
                            P.o("tensor", lambda E, k=k, hr=hr, t=t, ps=ps, w=wdt[k], it=itt[k]: E.matmul(
                                ps, it[:, 128 * t:128 * (t + 1)],
                                w[:, 512 * hr:512 * (hr + 1)],
                                start=(k == 0), stop=(k == 3)),
                                inc=("PE_dn" if k == 3 and hr == 3 else None))
                    P.w("vector", "PE_dn")
                    P.o("vector", lambda E, pa=pa, t=t, a=acc[t]: E.scalar_tensor_tensor(
                        a[:, 0:1024], pa[:, :], 1.0, a[:, 0:1024], ALU.mult, ALU.add))
                    P.o("vector", lambda E, pb=pb, t=t, a=acc[t]: E.scalar_tensor_tensor(
                        a[:, 1024:2048], pb[:, :], 1.0, a[:, 1024:2048], ALU.mult, ALU.add),
                        inc="D_acc")
                    if i_d == 3 and rr == 7:
                        P.w("sync", "D_acc")
                        P.o("sync", lambda E, t=t, a=acc[t]: E.dma_start(
                            out=self.out[128 * t:128 * (t + 1), :], in_=a),
                            inc="L_out", inc_n=16)
                    gt += 1
                gidx += 1
        P.w("sync", "L_out", 16 * 8)

    def schedule(self):
        self.deq_cnt = 0
        self.ln_cnt = 0
        self.agw = {}
        self.boot()
        sweeponly = ("sweeponly" in self.dbg) or ("swdeq" in self.dbg)
        nodeq = ("sweeponly" in self.dbg) or ("nodeq" in self.dbg)
        dorder = []
        for blk in range(4):
            dorder += [("g", 4 * blk + i) for i in range(4)]
            dorder += [("u", 4 * blk + i) for i in range(4)]
        for g in range(SEQ_CH):
            self.emit_sweep_chunk(g)
            if not nodeq:
                mat, ti = dorder[g]
                self.emit_deq_tile(mat, ti)
                if g % 4 == 3:
                    mat, ti = dorder[g]
                    sl = ti // 4
                    self.agw[(mat, sl)] = self.P.sem("L_stg").n
                    self.emit_ag(mat, sl)
        if sweeponly:
            P = self.P
            dkt = self.nc.dram_tensor("dbg_kt", [128, 4096], BF16, kind="ExternalOutput").ap()
            dvs = self.nc.dram_tensor("dbg_vs", [128, 4096], BF16, kind="ExternalOutput").ap()
            dxo = self.nc.dram_tensor("dbg_xo", [128, 16384], BF16, kind="ExternalOutput").ap()
            P.w("sync", "D_kv2")
            P.w("sync", "D_xnte")
            P.o("sync", lambda E: E.dma_start(out=dkt, in_=self.KT), inc="L_dbg4", inc_n=16)
            P.o("sync", lambda E: E.dma_start(out=dvs, in_=self.Vsb), inc="L_dbg4", inc_n=16)
            P.o("sync", lambda E: E.dma_start(out=dxo, in_=self.xnt_own), inc="L_dbg4", inc_n=16)
            P.w("sync", "L_dbg4", 48)
            self.P.emit()
            return
        self.emit_q_pre()
        for dcg in range(4):
            self.emit_q_group(dcg)
            if not nodeq:
                for i in range(4):
                    self.emit_deq_tile("d", 4 * dcg + i)
        if not nodeq:
            for i in range(4):
                self.agw[("d", i)] = self.P.sem("L_stg").n
                self.emit_ag("d", i)
        if "noattn" in self.dbg:
            P = self.P
            dkt = self.nc.dram_tensor("dbg_kt", [128, 4096], BF16, kind="ExternalOutput").ap()
            dvs = self.nc.dram_tensor("dbg_vs", [128, 4096], BF16, kind="ExternalOutput").ap()
            dqt = self.nc.dram_tensor("dbg_qt", [128, 16384], BF16, kind="ExternalOutput").ap()
            for sem in ("D_qr", "D_dec", "P_dec", "A_stg"):
                P.w("sync", sem)
            P.o("sync", lambda E: E.dma_start(out=dkt, in_=self.KT), inc="L_dbg4", inc_n=16)
            P.o("sync", lambda E: E.dma_start(out=dvs, in_=self.Vsb), inc="L_dbg4", inc_n=16)
            P.o("sync", lambda E: E.dma_start(out=dqt, in_=self.QT), inc="L_dbg4", inc_n=16)
            dar = self.nc.dram_tensor("dbg_arena", [128, self.ar.size], U8, kind="ExternalOutput").ap()
            P.o("sync", lambda E: E.dma_start(out=dar, in_=self.ar.h[0:128, :]), inc="L_dbg4", inc_n=16)
            P.w("sync", "L_dbg4", 64)
            if not nodeq:
                for i in range(4):
                    P.w("gpsimd", f"CC_d{i}", 1)
            self.P.emit()
            return
        self.emit_attention()
        self.emit_gate()
        self.emit_down()
        if "dump" in self.dbg:
            P = self.P
            dxm = self.nc.dram_tensor("dbg_xmid", [TPC, EMB], F32, kind="ExternalOutput").ap()
            dcg = self.nc.dram_tensor("dbg_cg", [EMB, 512], BF16, kind="ExternalOutput").ap()
            dgd = self.nc.dram_tensor("dbg_gd", [512, EMB], BF16, kind="ExternalOutput").ap()
            dint = self.nc.dram_tensor("dbg_int", [DFF, TPC], BF16, kind="ExternalOutput").ap()
            dx2 = self.nc.dram_tensor("dbg_xn2t", [128, 16 * TPC], BF16, kind="ExternalOutput").ap()
            P.w("sync", "L_out", 16 * 8)
            P.o("sync", lambda E: E.dma_start(out=dxm, in_=self.xmidD.ap()), inc="L_dbg", inc_n=16)
            P.o("sync", lambda E: E.dma_start(out=dcg, in_=self.contrib_g[0].ap()), inc="L_dbg", inc_n=16)
            P.o("sync", lambda E: E.dma_start(out=dgd, in_=self.gath_d[0].ap()[0:512, :]), inc="L_dbg", inc_n=16)
            P.o("sync", lambda E: E.dma_start(out=dint, in_=self.interT.ap()), inc="L_dbg", inc_n=16)
            P.o("sync", lambda E: E.dma_start(out=dx2, in_=self.xn2T), inc="L_dbg", inc_n=16)
            P.w("sync", "L_dbg", 80)
        self.P.emit()


def build_nc(dbg=()):
    nc = bass.Bass("TRN2", target_bir_lowering=False)
    b = Build(nc, dbg=dbg)
    b.schedule()
    return nc


def kernel(**inputs):
    maps = make_in_maps(inputs)
    nc = build_nc()
    res = run_bass_kernel_spmd(nc, maps, core_ids=list(range(NC)))
    return assemble_output(res.results)


# version 05
# speedup vs baseline: 1.0334x; 1.0012x over previous
import math
import numpy as np
import ml_dtypes
import concourse.bass as bass
import concourse.mybir as mybir
from concourse.bass_utils import run_bass_kernel_spmd

F32 = mybir.dt.float32
BF16 = mybir.dt.bfloat16
I32 = mybir.dt.int32
U8 = mybir.dt.uint8
AF = mybir.ActivationFunctionType
ALU = mybir.AluOpType
AX = mybir.AxisListType

B, L, EMB, HD, NH, DFF = 2, 4096, 2048, 128, 16, 16384
NC = 8
SEQ_CH = L // 128
OWN_CH = 8
TPC = OWN_CH * 128
NPC = DFF // NC
ISQ = 1.0 / math.sqrt(128.0)
CSHIFT = 10.0
EPS = 1e-5

BF = ml_dtypes.bfloat16


def _bf(x):
    return np.ascontiguousarray(x.astype(BF))



def host_constants():
    ident = np.eye(128, dtype=np.float32)
    rt = np.zeros((128, 128), dtype=np.float32)
    for i in range(64):
        rt[2 * i + 1, 2 * i] = -1.0
        rt[2 * i, 2 * i + 1] = 1.0
    ones = np.ones((128, 128), dtype=np.float32)
    tri = (np.arange(128)[None, :] >= np.arange(128)[:, None]).astype(np.float32)
    inv = 1.0 / (10000.0 ** (np.arange(0, HD, 2, dtype=np.float32) / HD))
    t = np.arange(L, dtype=np.float32)
    f = np.outer(t, inv)
    f = np.repeat(f, 2, axis=-1)
    cosT = np.cos(f).astype(np.float32).T.copy()
    sinT = np.sin(f).astype(np.float32).T.copy()
    return ident, rt, ones, tri, cosT, sinT


def chunk_perm(j):
    perm = []
    for m in range(SEQ_CH // 4):
        others = [4 * m + t for t in range(4) if t != j]
        perm.extend(others + [4 * m + j])
    return np.array(perm, dtype=np.int64)


def build_amask(j, tri):
    ones = np.ones((128, 128), dtype=np.float32)
    zeros = np.zeros((128, 128), dtype=np.float32)

    def A(s):
        if s == 3:
            return tri
        return ones if s < j else zeros

    slots = []
    for sp in range(8):
        if sp < 4:
            m = np.concatenate([A(sp), ones], axis=1)
        else:
            m = np.concatenate([zeros, A(sp - 4)], axis=1)
        slots.append(m)
    return np.concatenate(slots, axis=1)


def make_in_maps(inputs):
    x = np.asarray(inputs["L0_attn_x"])
    wq = np.asarray(inputs["L0_attn_Wq"])
    wk = np.asarray(inputs["L0_attn_Wk"])
    wv = np.asarray(inputs["L0_attn_Wv"])
    gpk = np.asarray(inputs["L0_ffn_gate_W_packed"])
    gsc = np.asarray(inputs["L0_ffn_gate_scales"])
    upk = np.asarray(inputs["L0_ffn_up_W_packed"])
    usc = np.asarray(inputs["L0_ffn_up_scales"])
    dpk = np.asarray(inputs["L0_ffn_down_W_packed"])
    dsc = np.asarray(inputs["L0_ffn_down_scales"])
    ident, rt, ones, tri, cosT, sinT = host_constants()

    maps = []
    for c in range(NC):
        s, j = c // 4, c % 4
        perm = chunk_perm(j)
        xs = x[s].reshape(SEQ_CH, 128, EMB)[perm].reshape(L, EMB)
        ct = cosT.reshape(128, SEQ_CH, 128)[:, perm].reshape(128, L)
        st = sinT.reshape(128, SEQ_CH, 128)[:, perm].reshape(128, L)
        own = [4 * m + j for m in range(OWN_CH)]
        cq = cosT.reshape(128, SEQ_CH, 128)[:, own].reshape(128, TPC)
        sq = sinT.reshape(128, SEQ_CH, 128)[:, own].reshape(128, TPC)
        m = {
            "xseq": np.ascontiguousarray(xs),
            "wq": np.ascontiguousarray(wq),
            "wk": np.ascontiguousarray(wk),
            "wv": np.ascontiguousarray(wv),
            "gpk": np.ascontiguousarray(gpk[NPC * c:NPC * (c + 1)]),
            "gsc": np.ascontiguousarray(gsc[NPC * c:NPC * (c + 1)]),
            "upk": np.ascontiguousarray(upk[NPC * c:NPC * (c + 1)]),
            "usc": np.ascontiguousarray(usc[NPC * c:NPC * (c + 1)]),
            "dpk": np.ascontiguousarray(dpk[:, 1024 * c:1024 * (c + 1)]),
            "dsc": np.ascontiguousarray(dsc[:, 64 * c:64 * (c + 1)]),
            "cost": np.ascontiguousarray(ct),
            "sint": np.ascontiguousarray(st),
            "cosq": np.ascontiguousarray(cq),
            "sinq": np.ascontiguousarray(sq),
            "ident": _bf(ident),
            "rotT": _bf(rt),
            "ones": _bf(ones),
            "amask": _bf(build_amask(j, tri)),
        }
        maps.append(m)
    return maps


def assemble_output(results):
    out = np.zeros((B, L, EMB), dtype=np.float32)
    for c in range(NC):
        s, j = c // 4, c % 4
        shard = results[c]["out"].reshape(OWN_CH, 128, EMB)
        for m in range(OWN_CH):
            g = 4 * m + j
            out[s, 128 * g:128 * (g + 1)] = shard[m]
    return out



ENGINES = ("sync", "tensor", "vector", "scalar", "gpsimd")


class Sem:
    __slots__ = ("h", "n", "virtual", "eng", "counts")

    def __init__(self, nc, name):
        self.h = nc.alloc_semaphore(name)
        self.n = 0
        self.virtual = False


class VSem:
    __slots__ = ("n", "eng", "counts", "virtual")

    def __init__(self):
        self.n = 0
        self.eng = None
        self.counts = []
        self.virtual = True


CHAINED = ("vector", "scalar")


class Prog:
    def __init__(self, nc):
        self.nc = nc
        self.st = {e: [] for e in ENGINES}
        self.sems = {}
        self.chain = {e: Sem(nc, f"CH_{e}") for e in CHAINED}

    def sem(self, name):
        if name not in self.sems:
            self.sems[name] = VSem()
        return self.sems[name]

    def _realize(self, name):
        v = self.sems.get(name)
        if v is None or v.virtual and v.eng is None and not v.counts and v.n == 0:
            pass
        return v

    def o(self, eng, fn, inc=None, inc_n=1):
        chained = eng in CHAINED and inc_n == 1
        if chained:
            ch = self.chain[eng]
            prior = ch.n
            ch.n += 1
            if inc is not None:
                vs = self.sem(inc)
                if isinstance(vs, Sem):
                    raise RuntimeError(f"sem {inc} already real, inc from {eng}")
                if vs.eng is None:
                    vs.eng = eng
                assert vs.eng == eng, (inc, vs.eng, eng)
                vs.n += 1
                vs.counts.append(ch.n)

            def wrapped(E, fn=fn, h=ch.h, prior=prior):
                if prior:
                    E.wait_ge(h, prior)
                fn(E).then_inc(h, 1)
            self.st[eng].append(wrapped)
        else:
            if inc is not None:
                vs = self.sems.get(inc)
                if vs is None or isinstance(vs, VSem):
                    if vs is not None and (vs.counts or vs.eng):
                        raise RuntimeError(f"sem {inc} already virtual")
                    self.sems[inc] = Sem(self.nc, inc)
                sem = self.sems[inc]
                prior = sem.n
                sem.n += inc_n
                if inc_n == 16 and prior:
                    self.st[eng].append(lambda E, h=sem.h, v=prior: E.wait_ge(h, v))

                def wrapped(E, fn=fn, h=sem.h, k=inc_n):
                    fn(E).then_inc(h, k)
                self.st[eng].append(wrapped)
            else:
                self.st[eng].append(fn)

    def w(self, eng, sem, val=None):
        if isinstance(sem, str):
            sem = self.sem(sem)
        v = sem.n if val is None else val
        if v <= 0:
            return
        if getattr(sem, "virtual", False):
            assert sem.eng is not None, "wait on never-incremented virtual sem"
            assert v <= len(sem.counts), (v, len(sem.counts))
            h = self.chain[sem.eng].h
            tgt = sem.counts[v - 1]
            self.st[eng].append(lambda E, h=h, t=tgt: E.wait_ge(h, t))
        else:
            self.st[eng].append(lambda E, h=sem.h, v=v: E.wait_ge(h, v))

    def emit(self):
        with self.nc.Block() as blk:
            for e in ENGINES:
                ops = self.st[e]
                if not ops:
                    continue

                def runner(E, ops=ops):
                    for f in ops:
                        f(E)
                getattr(blk, e)(runner)


ESZ = {F32: 4, BF16: 2, I32: 4, U8: 1}


class Arena:

    def __init__(self, nc, size):
        self.h = nc.alloc_sbuf_tensor("arena", [128, size], U8)
        self.size = size

    def view(self, off, n_elems, dtype, p=128):
        nb = n_elems * ESZ[dtype]
        assert off % 64 == 0 and off + nb <= self.size, (off, nb, self.size)
        return self.h[0:p, off:off + nb].bitcast(dtype)


class Layout:

    def __init__(self):
        o = 0

        def take(nb):
            nonlocal o
            off = (o + 63) & ~63
            o = off + nb
            return off

        self.cbank = take(64)
        self.ident = take(256)
        self.rotT = take(256)
        self.ones = take(256)
        self.wkb = take(4096)
        self.wvb = take(4096)
        self.PT = self.wkb
        self.KT = take(8192)
        self.Vsb = take(8192)
        self.dn_wdt = self.KT
        self.dn_int = None
        r2 = take(32768)
        self.QT = r2
        self.acc0 = r2
        s = r2
        self.xring = s; s += 16384
        self.xnt_sweep = s; s += 8192
        self.csring = s; s += 2048
        self.k0 = s; s += 256
        self.vt0 = s; s += 256
        self.u1 = s; s += 2048
        self.u2 = s; s += 2048
        self.lnt = s; s += 1024
        assert s - r2 <= 32768, s - r2
        r3 = take(32768)
        self.xnt_own = r3
        self.acc1 = r3
        s = r3
        self.attnT = s; s += 16384
        self.recip = s; s += 1024
        self.xmid = s; s += 8192
        self.amask = s; s += 4096
        assert s - r3 <= 32768, s - r3
        r4 = take(32768)
        self.xn2T = r4
        self.wq0 = r4
        self.wq1 = r4 + 16384
        self.accq = None
        r5 = take(45056)
        s = r5
        self.pk = s; s += 8192
        self.scl = s; s += 1024
        self.dq = s; s += 26624
        self.wdec = s; s += 4096
        self.wstg = s; s += 4096
        assert s - r5 <= 45056, s - r5
        self.gslab = r5
        self.gelu = r5 + 32768
        self.inter = r5 + 32768 + 4096
        r6 = take(24576)
        self.r6 = r6
        self.total = (o + 63) & ~63



class Build:
    def __init__(self, nc, dbg=()):
        self.nc = nc
        self.dbg = set(dbg)
        self.P = Prog(nc)
        self.lay = Layout()
        self.ar = Arena(nc, self.lay.total)
        L = self.lay

        def din(name, shape, dt):
            return nc.dram_tensor(name, shape, dt, kind="ExternalInput").ap()

        self.xseq = din("xseq", [L_ := 4096, EMB], F32)
        self.wq = din("wq", [EMB, EMB], F32)
        self.wk = din("wk", [EMB, HD], F32)
        self.wv = din("wv", [EMB, HD], F32)
        self.gpk = din("gpk", [NPC, 1024], I32)
        self.gsc = din("gsc", [NPC, 64], F32)
        self.upk = din("upk", [NPC, 1024], I32)
        self.usc = din("usc", [NPC, 64], F32)
        self.dpk = din("dpk", [EMB, 1024], I32)
        self.dsc = din("dsc", [EMB, 64], F32)
        self.cost = din("cost", [128, 4096], F32)
        self.sint = din("sint", [128, 4096], F32)
        self.cosq = din("cosq", [128, TPC], F32)
        self.sinq = din("sinq", [128, TPC], F32)
        self.identD = din("ident", [128, 128], BF16)
        self.rotD = din("rotT", [128, 128], BF16)
        self.onesD = din("ones", [128, 128], BF16)
        self.amaskD = din("amask", [128, 2048], BF16)
        self.out = nc.dram_tensor("out", [TPC, EMB], F32, kind="ExternalOutput").ap()

        self.contrib_g = [nc.dram_tensor(f"cg{s}", [EMB, 512], BF16) for s in range(4)]
        self.contrib_u = [nc.dram_tensor(f"cu{s}", [EMB, 512], BF16) for s in range(4)]
        self.contrib_d = nc.dram_tensor("cd", [NPC, EMB], BF16)
        self.gath_g = [nc.dram_tensor(f"gg{s}", [DFF, 512], BF16) for s in range(4)]
        self.gath_u = [nc.dram_tensor(f"gu{s}", [DFF, 512], BF16) for s in range(4)]
        self.gath_d = [nc.dram_tensor(f"gd{i}", [4096, EMB], BF16) for i in range(4)]
        self.interT = nc.dram_tensor("interT", [DFF, TPC], BF16)
        self.xmidD = nc.dram_tensor("xmidD", [TPC, EMB], F32)

        self.PS = [nc.alloc_psum_tensor(f"ps{i}", [128, 1024], F32) for i in range(4)]

        a = self.ar
        self.ident = a.view(L.ident, 128 * 128 // 128, BF16)
        self.rotT = a.view(L.rotT, 128, BF16)
        self.ones = a.view(L.ones, 128, BF16)
        self.wkb = a.view(L.wkb, 16 * 128, BF16)
        self.wvb = a.view(L.wvb, 16 * 128, BF16)
        self.KT = a.view(L.KT, 4096, BF16)
        self.Vsb = a.view(L.Vsb, 4096, BF16)
        self.QT = a.view(L.QT, 16 * 1024, BF16)
        self.xn2T = a.view(L.xn2T, 16 * 1024, BF16)
        self.xnt_own = a.view(L.xnt_own, 16 * 1024, BF16)
        self.amask = a.view(L.amask, 2048, BF16)
        cb = a.view(L.cbank, 16, F32)
        self.cb75 = cb[:, 0:1]
        self.cbm6 = cb[:, 1:2]
        self.cbeps = cb[:, 2:3]
        self.cbsh = cb[:, 3:4]
        self.cbm4 = cb[:, 4:5]

    def V(self, off, n, dt):
        return self.ar.view(off, n, dt)

    def boot(self):
        P, L = self.P, self.lay
        for src, off in ((self.identD, L.ident), (self.rotD, L.rotT),
                         (self.onesD, L.ones)):
            n = src.shape[1]
            P.o("sync", lambda E, s=src, o=off, n=n: E.dma_start(
                out=self.V(o, n, BF16), in_=s), inc="Lc", inc_n=16)
        for src, off in ((self.wk, L.wkb), (self.wv, L.wvb)):
            P.o("gpsimd", lambda E, s=src, o=off: E.dma_start(
                out=self.V(o, 16 * 128, BF16).rearrange("p (h d) -> p h d", d=128),
                in_=s.rearrange("(h p) d -> p h d", p=128)), inc="Lcg", inc_n=16)
        P.o("vector", lambda E: E.memset(self.cb75, 7.5))
        P.o("vector", lambda E: E.memset(self.cbm6, -6.0))
        P.o("vector", lambda E: E.memset(self.cbeps, EPS))
        P.o("vector", lambda E: E.memset(self.cbm4, -4.0))
        P.o("vector", lambda E: E.memset(self.cbsh, -CSHIFT), inc="D_cb")
        self.P.w("tensor", "Lc")
        self.P.w("tensor", "Lcg")
        self.P.w("vector", "Lc")
        self.P.w("vector", "Lcg")
        self.P.w("scalar", "Lc")
        self.P.w("scalar", "Lcg")
        self.P.w("scalar", "D_cb")
        self.P.w("gpsimd", "Lc")

    def emit_ln(self, xap, xnap, par, wait_before=None, lnt_off=None):
        P, L = self.P, self.lay
        lnt = (L.lnt + 1024 * (par & 1)) if lnt_off is None else lnt_off
        cnt = self.ln_cnt
        self.ln_cnt += 1
        s1 = self.V(lnt, 8, F32)
        sq4 = self.V(lnt + 64, 8, F32)
        mm = self.V(lnt + 128, 8, F32)
        msq = self.V(lnt + 192, 8, F32)
        var = self.V(lnt + 256, 8, F32)
        rsd = self.V(lnt + 320, 8, F32)
        nb = self.V(lnt + 384, 8, F32)
        sqs = self.V(lnt + 448, 8, F32)
        PS3 = self.PS[3]

        if wait_before:
            for eng, sem, v in wait_before:
                P.w(eng, sem, v)
        P.o("vector", lambda E: E.reduce_sum(s1[:, 0:1], xap, AX.X), inc="D_s1")
        if cnt >= 2:
            P.w("scalar", "D_var", cnt - 1)
        P.o("scalar", lambda E: E.activation(xnap, xap, AF.Square), inc="A_sqp")
        P.w("vector", "A_sqp")
        P.o("vector", lambda E: E.reduce_sum(sqs[:, 0:1], xnap, AX.X), inc="D_sqr")
        P.o("vector", lambda E: E.tensor_scalar_mul(mm[:, 0:1], s1[:, 0:1], 1.0 / EMB))
        P.o("vector", lambda E: E.tensor_mul(msq[:, 0:1], mm[:, 0:1], mm[:, 0:1]))
        P.o("vector", lambda E: E.scalar_tensor_tensor(
            var[:, 0:1], sqs[:, 0:1], 1.0 / EMB, msq[:, 0:1],
            ALU.mult, ALU.subtract), inc="D_var")
        P.w("scalar", "D_var")
        if cnt >= 2:
            P.w("scalar", "D_ln", cnt - 1)
        P.o("scalar", lambda E: E.activation(rsd[:, 0:1], var[:, 0:1], AF.Sqrt,
                                             bias=self.cbeps), inc="A_sqrt")
        P.w("vector", "A_sqrt")
        if cnt >= 2:
            P.w("vector", "A_xn", cnt - 1)
        P.o("vector", lambda E: E.reciprocal(rsd[:, 0:1], rsd[:, 0:1]))
        P.o("vector", lambda E: E.scalar_tensor_tensor(
            nb[:, 0:1], mm[:, 0:1], -1.0, rsd[:, 0:1], ALU.mult, ALU.mult),
            inc="D_ln")
        P.w("scalar", "D_ln")
        P.o("scalar", lambda E: E.activation(
            xnap, xap, AF.Identity, bias=nb[:, 0:1], scale=rsd[:, 0:1]),
            inc="A_xn")
        if "dump" in self.dbg and cnt == 32:
            snap = self.V(L.r6 + 20480, 128, F32)
            P.o("vector", lambda E, snap=snap: E.tensor_copy(
                snap[:, 0:16], self.V(lnt, 16, F32)[:, 0:16]))
            P.o("vector", lambda E, snap=snap: E.tensor_copy(
                snap[:, 16:32], self.V(lnt + 448, 16, F32)[:, 0:16]))
            P.o("vector", lambda E, snap=snap: E.tensor_copy(
                snap[:, 32:48], self.V(lnt + 256, 16, F32)[:, 0:16]))
            P.o("vector", lambda E, snap=snap: E.tensor_copy(
                snap[:, 48:64], self.V(lnt + 320, 16, F32)[:, 0:16]), inc="D_snap")
            dsn = self.nc.dram_tensor("dbg_ln2m0", [128, 128], F32, kind="ExternalOutput").ap()
            P.w("sync", "D_snap")
            P.o("sync", lambda E, dsn=dsn, snap=snap: E.dma_start(out=dsn, in_=snap),
                inc="L_dbg6", inc_n=16)

    def emit_deq_tile(self, mat, ti):
        P, L = self.P, self.lay
        t = self.deq_cnt
        self.deq_cnt += 1
        par = t % 2
        pk = self.V(L.pk + 4096 * par, 1024, I32)
        scl = self.V(L.scl + 512 * par, 64, F32)
        src_pk = {"g": self.gpk, "u": self.upk, "d": self.dpk}[mat]
        src_sc = {"g": self.gsc, "u": self.usc, "d": self.dsc}[mat]
        dqo = L.dq
        wf = self.V(dqo, 1024, BF16)
        lo = self.V(dqo + 2048, 1024, BF16)
        sgl = self.V(dqo + 4096, 1024, BF16)
        la = self.V(dqo + 6144, 1024, BF16)
        lb = self.V(dqo + 8192, 1024, BF16)
        ld_ = self.V(dqo + 10240, 1024, BF16)
        dqh = L.dq + 12288
        hi_i = self.V(dqh, 1024, I32)
        hi = self.V(dqh + 4096, 1024, BF16)
        sgh = self.V(dqh + 6144, 1024, BF16)
        ha = self.V(dqh + 8192, 1024, BF16)
        hb = self.V(dqh + 10240, 1024, BF16)
        hd_ = self.V(dqh + 12288, 1024, BF16)
        wdec = self.V(L.wdec, 2048, BF16)
        wstg = self.V(L.wstg, 16 * 128, BF16)
        PS3 = self.PS[3]

        if t >= 2:
            P.w("sync", "D_dec", t - 1)
            P.w("sync", "P_dec", t - 1)
        P.o("sync", lambda E: E.dma_start(out=pk, in_=src_pk[128 * ti:128 * (ti + 1), :]),
            inc="L_pk", inc_n=16)
        P.o("sync", lambda E: E.dma_start(out=scl, in_=src_sc[128 * ti:128 * (ti + 1), :]),
            inc="L_pk", inc_n=16)

        scv = scl.unsqueeze(2).broadcast_to([128, 64, 16])
        P.w("vector", "L_pk", 32 * (t + 1))
        if t >= 1:
            P.w("vector", "PE_dqT", 4 * t)
        P.o("vector", lambda E: E.tensor_single_scalar(hi_i, pk, 4, ALU.logical_shift_right))
        P.o("vector", lambda E: E.tensor_copy(hi, hi_i), inc="P_hi")
        P.o("vector", lambda E: E.tensor_copy(wf, pk))
        P.w("scalar", "P_hi")
        P.o("scalar", lambda E: E.activation(sgh, hi, AF.Sign, bias=self.cb75, scale=-1.0), inc="A_sgh")
        P.o("vector", lambda E: E.scalar_tensor_tensor(lo, hi, -16.0, wf, ALU.mult, ALU.add),
            inc="D_lo")
        P.w("scalar", "D_lo")
        P.o("scalar", lambda E: E.activation(sgl, lo, AF.Sign, bias=self.cb75, scale=-1.0), inc="A_sgl")
        P.o("scalar", lambda E: E.activation(hb, hi, AF.Identity, bias=self.cbm4),
            inc="A_nm4h")
        P.w("vector", "A_sgh")
        P.w("vector", "A_nm4h")
        P.o("vector", lambda E: E.scalar_tensor_tensor(ha, sgh, 4.0, hb, ALU.mult, ALU.add),
            inc="P_kh")
        P.w("scalar", "P_kh")
        P.o("scalar", lambda E: E.activation(hb, ha, AF.Relu, bias=self.cbm4))
        P.o("scalar", lambda E: E.activation(hd_, ha, AF.Relu, bias=self.cbm6), inc="A_r2h")
        P.w("vector", "A_r2h")
        P.o("vector", lambda E: E.tensor_add(hb, hb, ha))
        P.o("vector", lambda E: E.scalar_tensor_tensor(hb, hb, 0.5, hd_, ALU.mult, ALU.add))
        P.o("vector", lambda E: E.tensor_mul(hb, hb, sgh))
        P.o("vector", lambda E: E.tensor_tensor(
            wdec[:, 1::2].rearrange("p (a c) -> p a c", c=16),
            hb.rearrange("p (a c) -> p a c", c=16), scv, ALU.mult), inc="P_dec")
        P.o("scalar", lambda E: E.activation(la, lo, AF.Identity, bias=self.cbm4),
            inc="A_nm4l")
        P.w("vector", "A_sgl")
        P.w("vector", "A_nm4l")
        P.o("vector", lambda E: E.scalar_tensor_tensor(lb, sgl, 4.0, la, ALU.mult, ALU.add),
            inc="D_kl")
        P.w("scalar", "D_kl")
        P.o("scalar", lambda E: E.activation(la, lb, AF.Relu, bias=self.cbm4))
        P.o("scalar", lambda E: E.activation(ld_, lb, AF.Relu, bias=self.cbm6), inc="A_r2l")
        P.w("vector", "A_r2l")
        P.o("vector", lambda E: E.tensor_add(la, la, lb))
        P.o("vector", lambda E: E.scalar_tensor_tensor(la, la, 0.5, ld_, ALU.mult, ALU.add))
        P.o("vector", lambda E: E.tensor_mul(la, la, sgl))
        P.o("vector", lambda E: E.tensor_tensor(
            wdec[:, 0::2].rearrange("p (a c) -> p a c", c=16),
            la.rearrange("p (a c) -> p a c", c=16), scv, ALU.mult), inc="D_dec")

        P.w("tensor", "D_dec", t + 1)
        P.w("tensor", "P_dec", t + 1)
        if t >= 1:
            P.w("scalar", "L_stg", 16 * t)
        for r in range(4):
            P.w("tensor", "A_stg", 4 * t + r)
            for k in range(4):
                kc = 4 * r + k
                P.o("tensor", lambda E, kc=kc, k=k: E.matmul(
                    PS3[:, 512 + 128 * k:512 + 128 * (k + 1)],
                    wdec[:, 128 * kc:128 * (kc + 1)], self.ident,
                    start=True, stop=True), inc=("PE_dqT" if k == 3 else None))
            P.w("scalar", "PE_dqT")
            P.o("scalar", lambda E, r=r: E.activation(
                wstg[:, 512 * r:512 * (r + 1)], PS3[:, 512:1024], AF.Copy),
                inc="A_stg")
        P.w("sync", "A_stg", 4 * (t + 1))
        if mat in ("g", "u"):
            dst = (self.contrib_g if mat == "g" else self.contrib_u)[ti // 4]
            dview = dst.ap().rearrange("(kc p) n -> p kc n", p=128)[
                :, :, 128 * (ti % 4):128 * (ti % 4 + 1)]
        else:
            dview = self.contrib_d.ap().rearrange("(nc p) h -> p nc h", p=128)[
                :, :, 128 * ti:128 * (ti + 1)]
        sview = wstg.rearrange("p (kc n) -> p kc n", n=128)
        P.o("sync", lambda E: E.dma_start(out=dview, in_=sview), inc="L_stg", inc_n=16)

    def emit_ag(self, kind, idx):
        P = self.P
        if kind == "g":
            src, dst, wait = self.contrib_g[idx], self.gath_g[idx], self.agw[("g", idx)]
        elif kind == "u":
            src, dst, wait = self.contrib_u[idx], self.gath_u[idx], self.agw[("u", idx)]
        else:
            src = self.contrib_d.ap()[512 * idx:512 * (idx + 1), :]
            dst, wait = self.gath_d[idx], self.agw[("d", idx)]
        P.w("gpsimd", "L_stg", wait)
        sem = f"CC_{kind}{idx}"
        if kind == "d":
            P.o("gpsimd", lambda E: E.collective_compute(
                "AllGather", ALU.bypass, replica_groups=[list(range(NC))],
                ins=[src.opt()], outs=[dst.ap().opt()]), inc=sem)
        else:
            P.o("gpsimd", lambda E: E.collective_compute(
                "AllGather", ALU.bypass, replica_groups=[list(range(NC))],
                ins=[src.ap().opt()], outs=[dst.ap().opt()]), inc=sem)

    def emit_sweep_chunk(self, g):
        P, L = self.P, self.lay
        par = g % 2
        xb = self.V(L.xring + 8192 * par, 2048, F32)
        xnb = self.V(L.wq0 + 4096 * par, 2048, BF16)
        cs = self.V(L.csring + 1024 * par, 128, F32)
        sn = self.V(L.csring + 1024 * par + 512, 128, F32)
        k0 = self.V(L.k0, 128, BF16)
        vt0 = self.V(L.vt0, 128, BF16)
        u1 = self.V(L.u1, 512, F32)
        u2 = self.V(L.u2, 512, F32)
        PSp = self.PS[par]
        PS2 = self.PS[2]
        own = (g % 4 == 3)
        m = g // 4

        if g >= 2:
            P.w("sync", "A_xn", g - 1)
        P.o("sync", lambda E: E.dma_start(out=xb, in_=self.xseq[128 * g:128 * (g + 1), :]),
            inc="L_x", inc_n=16)
        if g >= 2:
            P.w("scalar", "D_kv2", g - 1)
        P.o("scalar", lambda E: E.dma_start(out=cs, in_=self.cost[:, 128 * g:128 * (g + 1)]),
            inc="L_cs", inc_n=16)
        P.o("scalar", lambda E: E.dma_start(out=sn, in_=self.sint[:, 128 * g:128 * (g + 1)]),
            inc="L_cs", inc_n=16)

        if "dump" in self.dbg and g == 0:
            dx0 = self.nc.dram_tensor("dbg_x0", [128, 2048], F32, kind="ExternalOutput").ap()
            P.o("sync", lambda E, xb=xb, dx0=dx0: E.dma_start(out=dx0, in_=xb),
                inc="L_dbg5", inc_n=16)
        wb = [("vector", P.sem("L_x"), P.sem("L_x").n),
              ("scalar", P.sem("L_x"), P.sem("L_x").n)]
        if g >= 2:
            wb.append(("vector", P.sem("PE_xnt"), 2 * (g - 1)))
            wb.append(("scalar", P.sem("PE_xnt"), 2 * (g - 1)))
        self.emit_ln(xb, xnb, par, wait_before=wb)

        if "dump" in self.dbg and g == 0:
            dxn0 = self.nc.dram_tensor("dbg_xn0", [128, 2048], BF16, kind="ExternalOutput").ap()
            dln0 = self.nc.dram_tensor("dbg_ln0", [128, 128], F32, kind="ExternalOutput").ap()
            P.w("sync", "A_xn", P.sem("A_xn").n)
            P.o("sync", lambda E, xnb=xnb, dxn0=dxn0: E.dma_start(out=dxn0, in_=xnb),
                inc="L_dbg5", inc_n=16)
            P.o("sync", lambda E, dln0=dln0: E.dma_start(
                out=dln0, in_=self.V(L.lnt, 128, F32)), inc="L_dbg5", inc_n=16)
        dst_own = self.xnt_own.rearrange("p (hc t) -> p hc t", t=1024)
        dst_sw = self.V(L.xnt_sweep + 4096 * par, 16 * 128, BF16)
        P.w("tensor", "A_xn")
        for half in range(2):
            P.w("tensor", "D_xnte", (2 * (g - 1) if g >= 2 else 0) if half == 0 else 2 * g + 1)
            for i in range(8):
                hc = 8 * half + i
                P.o("tensor", lambda E, hc=hc, i=i: E.matmul(
                    PSp[:, 128 * i:128 * (i + 1)],
                    xnb[:, 128 * hc:128 * (hc + 1)], self.ident,
                    start=True, stop=True), inc=("PE_xnt" if i == 7 else None))
            P.w("scalar", "PE_xnt")
            if own:
                dv = dst_own[:, 8 * half:8 * (half + 1), 128 * m:128 * (m + 1)]
            else:
                dv = dst_sw.rearrange("p (hc t) -> p hc t", t=128)[:, 8 * half:8 * (half + 1), :]
                if g >= 2:
                    P.w("scalar", "PE_kv1", g - 1)
            P.o("scalar", lambda E, dv=dv: E.activation(
                dv, PSp[:, :].rearrange("p (hc t) -> p hc t", t=128), AF.Copy),
                inc="D_xnte")

        xnt_src = dst_own[:, :, 128 * m:128 * (m + 1)] if own else \
            dst_sw.rearrange("p (hc t) -> p hc t", t=128)
        P.w("tensor", "D_xnte")
        P.w("tensor", "D_kv2", g)
        for hc in range(16):
            P.o("tensor", lambda E, hc=hc: E.matmul(
                PS2[:, 0:128],
                self.wkb.rearrange("p (h d) -> p h d", d=128)[:, hc, :],
                xnt_src[:, hc, :], start=(hc == 0), stop=(hc == 15)))
        for hc in range(16):
            P.o("tensor", lambda E, hc=hc: E.matmul(
                PS2[:, 128:256],
                self.wvb.rearrange("p (h d) -> p h d", d=128)[:, hc, :],
                xnt_src[:, hc, :], start=(hc == 0), stop=(hc == 15)),
                inc=("PE_kv1" if hc == 15 else None))
        P.w("vector", "PE_kv1")
        P.o("vector", lambda E: E.tensor_copy(k0, PS2[:, 0:128]))
        P.o("vector", lambda E: E.tensor_copy(vt0, PS2[:, 128:256]), inc="D_kv1")
        P.w("tensor", "D_kv1")
        P.o("tensor", lambda E: E.matmul(PS2[:, 256:384], vt0, self.ident,
                                         start=True, stop=True))
        P.o("tensor", lambda E: E.matmul(PS2[:, 384:512], self.rotT, k0,
                                         start=True, stop=True), inc="PE_kv2")
        P.w("vector", "PE_kv2")
        P.w("vector", "L_cs")
        P.o("vector", lambda E: E.tensor_copy(
            self.Vsb[:, 128 * g:128 * (g + 1)], PS2[:, 256:384]))
        P.o("vector", lambda E: E.tensor_tensor(u1[:, 0:128], k0, cs, ALU.mult))
        P.o("vector", lambda E: E.tensor_tensor(u2[:, 0:128], PS2[:, 384:512], sn, ALU.mult))
        P.o("vector", lambda E: E.tensor_add(
            self.KT[:, 128 * g:128 * (g + 1)], u1[:, 0:128], u2[:, 0:128]),
            inc="D_kv2")

    def emit_q_pre(self):
        P, L = self.P, self.lay
        P.w("tensor", "D_xnte")
        P.w("tensor", "D_kv2")
        self._q0 = self.V(L.r6 + 8192, 1024, BF16)
        self._cosq = self.V(L.r6, 1024, F32)
        self._sinq = self.V(L.r6 + 4096, 1024, F32)
        P.o("sync", lambda E: E.dma_start(out=self._cosq, in_=self.cosq), inc="L_cq", inc_n=16)
        P.o("sync", lambda E: E.dma_start(out=self._sinq, in_=self.sinq), inc="L_cq", inc_n=16)
        P.w("vector", "L_cq")
        P.w("gpsimd", "PE_xnt")

    def emit_q_group(self, dcg):
        P, L = self.P, self.lay
        PS2 = self.PS[2]
        q0, cosq, sinq = self._q0, self._cosq, self._sinq
        slab = self.V(L.wq0 + 16384 * (dcg % 2), 16 * 512, BF16)
        if dcg >= 2:
            P.w("gpsimd", "PE_qmm", 4 * (dcg - 1))
        P.o("gpsimd", lambda E, dcg=dcg, slab=slab: E.dma_start(
            out=slab.rearrange("p (h d) -> p h d", d=512),
            in_=self.wq.rearrange("(h p) d -> p h d", p=128)[:, :, 512 * dcg:512 * (dcg + 1)]),
            inc="L_wq", inc_n=16)
        P.w("tensor", "L_wq", 16 * (dcg + 1))
        for dc4 in range(4):
            dc = 4 * dcg + dc4
            PSq = self.PS[dc % 2]
            if dc >= 2:
                P.w("tensor", "D_q0", dc - 1)
            for hc in range(16):
                for th in range(2):
                    P.o("tensor", lambda E, hc=hc, th=th, slab=slab, dc4=dc4, PSq=PSq: E.matmul(
                        PSq[:, 512 * th:512 * (th + 1)],
                        slab.rearrange("p (h d) -> p h d", d=512)[:, hc, 128 * dc4:128 * (dc4 + 1)],
                        self.xnt_own.rearrange("p (hc t) -> p hc t", t=1024)[:, hc, 512 * th:512 * (th + 1)],
                        start=(hc == 0), stop=(hc == 15)),
                        inc=("PE_qmm" if hc == 15 and th == 1 else None))
            P.w("scalar", "PE_qmm")
            if dc >= 1:
                P.w("scalar", "D_qr", dc)
            P.o("scalar", lambda E, PSq=PSq: E.activation(q0, PSq[:, :], AF.Copy), inc="D_q0")
            P.w("tensor", "D_q0")
            if dc >= 1:
                P.w("tensor", "D_qr", dc)
            for th in range(2):
                P.o("tensor", lambda E, th=th: E.matmul(
                    PS2[:, 512 * th:512 * (th + 1)], self.rotT,
                    q0[:, 512 * th:512 * (th + 1)], start=True, stop=True),
                    inc=("PE_qrk" if th == 1 else None))
            P.w("vector", "PE_qrk")
            u1 = self.V(L.r6 + 10240, 512, F32)
            for th in range(2):
                P.o("vector", lambda E, th=th, dc=dc: E.tensor_tensor(
                    u1, q0[:, 512 * th:512 * (th + 1)],
                    cosq[:, 512 * th:512 * (th + 1)], ALU.mult))
                P.o("vector", lambda E, th=th, dc=dc: E.tensor_tensor(
                    self.QT[:, 1024 * dc + 512 * th:1024 * dc + 512 * (th + 1)],
                    PS2[:, 512 * th:512 * (th + 1)],
                    sinq[:, 512 * th:512 * (th + 1)], ALU.mult))
                P.o("vector", lambda E, th=th, dc=dc: E.tensor_add(
                    self.QT[:, 1024 * dc + 512 * th:1024 * dc + 512 * (th + 1)],
                    self.QT[:, 1024 * dc + 512 * th:1024 * dc + 512 * (th + 1)],
                    u1), inc=("D_qr" if th == 1 else None))

    def emit_attention(self):
        P, L = self.P, self.lay
        PS2 = self.PS[2]
        PS3 = self.PS[3]
        amask = self.amask
        P.w("sync", "PE_qmm")
        P.o("sync", lambda E: E.dma_start(out=amask, in_=self.amaskD),
            inc="L_am", inc_n=16)
        P.w("vector", "L_am")
        P.w("tensor", "D_qr")
        P.w("tensor", "D_q0")
        P.w("tensor", "A_stg")
        self.Ggl = 0
        self.Hc = 0
        self.xm_r = 0
        self.m_cnt = 0
        for pm in range(4):
            for head in range(16):
                self.emit_attn_head(pm, head)
            for qc in range(2):
                self.emit_xmid_ln2(pm, qc)

    def emit_attn_head(self, pm, head):
        P, L = self.P, self.lay
        PS2 = self.PS[2]
        hp = head % 2
        PSh = PS2 if hp == 0 else self.PS[3]
        pv = PSh[:, 0:256]
        sm = PSh[:, 512:768]
        ng = 2 * pm + 2
        Hc = self.Hc
        self.Hc += 1
        qs = 256 * pm
        last_jc = 8 * pm + 7
        P.w("tensor", "D_att", max(0, Hc - 1))
        if hp == 1 and pm >= 1:
            P.w("tensor", "D_x2e", 8 * pm)

        def emit_S(gg, Ggl, slot):
            Spar = self.PS[Ggl % 2]
            P.w("tensor", "A_exp", max(0, Ggl - 1))
            for k in range(4):
                jc = 4 * gg + k
                P.o("tensor", lambda E, jc=jc, k=k, Spar=Spar: E.matmul(
                    Spar[:, 256 * k:256 * (k + 1)],
                    self.KT[:, 128 * jc:128 * (jc + 1)],
                    self.QT[:, 1024 * head + qs:1024 * head + qs + 256],
                    start=True, stop=True), inc=("PE_S" if k == 3 else None))
            P.w("scalar", "PE_S")
            P.w("scalar", "PE_ptf", max(0, Ggl - 2))
            P.o("scalar", lambda E, slot=slot, Spar=Spar: E.activation(
                slot, Spar[:, :], AF.Exp, scale=ISQ, bias=self.cbsh), inc="A_exp")
            zone = gg >= ng - 2
            if zone:
                zi = gg - (ng - 2)
                P.w("vector", "A_exp")
                P.o("vector", lambda E, slot=slot, zi=zi: E.tensor_tensor(
                    slot, slot, self.amask[:, 1024 * zi:1024 * (zi + 1)], ALU.mult),
                    inc="D_msk")
            return zone

        def emit_PV(gg, zone, slot):
            if zone:
                P.w("tensor", "D_msk")
            else:
                P.w("tensor", "A_exp", self.exp_at[gg])
            for k in range(4):
                jc = 4 * gg + k
                P.o("tensor", lambda E, jc=jc, k=k, slot=slot: E.matmul(
                    pv, self.Vsb[:, 128 * jc:128 * (jc + 1)],
                    slot[:, 256 * k:256 * (k + 1)],
                    start=(jc == 0), stop=(jc == last_jc)))
                P.o("tensor", lambda E, jc=jc, k=k, slot=slot: E.matmul(
                    sm, self.ones, slot[:, 256 * k:256 * (k + 1)],
                    start=(jc == 0), stop=(jc == last_jc)),
                    inc=("PE_ptf" if k == 3 else None))

        self.exp_at = {}
        pend = None
        for gg in range(ng):
            Ggl = self.Ggl
            self.Ggl += 1
            slot = self.V(L.PT + 2048 * (Ggl % 3), 1024, BF16)
            zone = emit_S(gg, Ggl, slot)
            self.exp_at[gg] = self.P.sem("A_exp").n
            if pend is not None:
                emit_PV(*pend)
            pend = (gg, zone, slot)
        emit_PV(*pend)
        self._attn_tail(pm, head, pv, sm)


    def _attn_tail(self, pm, head, pv, sm):
        P, L = self.P, self.lay
        recip = self.V(L.recip, 256, F32)
        att = self.V(L.attnT + 8192 * (pm % 2), 16 * 256, BF16)
        P.w("vector", "PE_ptf")
        if "dump" in self.dbg and pm == 0 and head == 0:
            dkt = self.nc.dram_tensor("dbg_kt", [128, 4096], BF16, kind="ExternalOutput").ap()
            dqt = self.nc.dram_tensor("dbg_qt", [128, 16384], BF16, kind="ExternalOutput").ap()
            dvs = self.nc.dram_tensor("dbg_vs", [128, 4096], BF16, kind="ExternalOutput").ap()
            dam = self.nc.dram_tensor("dbg_am", [128, 2048], BF16, kind="ExternalOutput").ap()
            P.o("sync", lambda E: E.dma_start(out=dkt, in_=self.KT), inc="L_dbg3", inc_n=16)
            P.o("sync", lambda E: E.dma_start(out=dqt, in_=self.QT), inc="L_dbg3", inc_n=16)
            P.o("sync", lambda E: E.dma_start(out=dvs, in_=self.Vsb), inc="L_dbg3", inc_n=16)
            P.o("sync", lambda E: E.dma_start(out=dam, in_=self.amask), inc="L_dbg3", inc_n=16)
        if "dump" in self.dbg and pm == 1 and head == 0:
            pvd = self.V(L.r6 + 18432, 256, F32)
            smd = self.V(L.r6 + 19456, 256, F32)
            P.o("vector", lambda E: E.tensor_copy(pvd, pv))
            P.o("vector", lambda E: E.tensor_copy(smd, sm), inc="D_dbgpv")
            P.w("sync", "D_dbgpv")
            dpv = self.nc.dram_tensor("dbg_pv", [128, 512], F32, kind="ExternalOutput").ap()
            P.o("sync", lambda E, dpv=dpv: E.dma_start(out=dpv[:, 0:256], in_=pvd), inc="L_dbg2", inc_n=16)
            P.o("sync", lambda E, dpv=dpv: E.dma_start(out=dpv[:, 256:512], in_=smd), inc="L_dbg2", inc_n=16)
        P.o("vector", lambda E, sm=sm: E.reciprocal(recip, sm))
        P.o("vector", lambda E, pv=pv, head=head, att=att: E.tensor_tensor(
            att[:, 256 * head:256 * (head + 1)], pv, recip, ALU.mult), inc="D_att")

    def emit_xmid_ln2(self, pm, qc):
        P, L = self.P, self.lay
        PS3 = self.PS[3]
        m = 2 * pm + qc
        xmid = self.V(L.xmid, 2048, F32)
        att = self.V(L.attnT + 8192 * (pm % 2), 16 * 256, BF16)
        xn2b = self.V(L.r6 + 14336, 2048, BF16)
        if m >= 1:
            P.w("sync", "A_xn", P.sem("A_xn").n)
            P.w("sync", "L_xmD", 16 * m)
        g = 4 * m + 3
        P.o("sync", lambda E, g=g, xmid=xmid: E.dma_start(
            out=xmid, in_=self.xseq[128 * g:128 * (g + 1), :]), inc="L_xo", inc_n=16)
        P.w("tensor", "D_att")
        for r in range(4):
            xr = self.xm_r
            self.xm_r += 1
            P.w("tensor", "D_xm", max(0, xr))
            if r == 0 and m >= 1:
                P.w("tensor", "D_x2e", 4 * m)
            for i in range(4):
                head = 4 * r + i
                P.o("tensor", lambda E, head=head, i=i, att=att, qc=qc: E.matmul(
                    PS3[:, 128 * i:128 * (i + 1)],
                    att[:, 256 * head + 128 * qc:256 * head + 128 * (qc + 1)],
                    self.ident, start=True, stop=True),
                    inc=("PE_xm" if i == 3 else None))
            P.w("vector", "PE_xm")
            if r == 0:
                P.w("vector", "L_xo", 16 * (m + 1))
            P.o("vector", lambda E, r=r, xmid=xmid: E.scalar_tensor_tensor(
                xmid[:, 512 * r:512 * (r + 1)], PS3[:, 0:512], 1.0,
                xmid[:, 512 * r:512 * (r + 1)], ALU.mult, ALU.add), inc="D_xm")
        P.w("sync", "D_xm", P.sem("D_xm").n)
        P.o("sync", lambda E, m=m, xmid=xmid: E.dma_start(
            out=self.xmidD.ap()[128 * m:128 * (m + 1), :], in_=xmid),
            inc="L_xmD", inc_n=16)
        wb = [("vector", P.sem("D_xm"), P.sem("D_xm").n),
              ("scalar", P.sem("D_xm"), P.sem("D_xm").n)]
        if m >= 1:
            wb.append(("vector", P.sem("PE_x2t"), 4 * m))
            wb.append(("scalar", P.sem("PE_x2t"), 4 * m))
        self.emit_ln(xmid, xn2b, m, wait_before=wb, lnt_off=L.r6 + 12288 + 1024 * (m % 2))
        P.w("tensor", "A_xn")
        dst = self.xn2T.rearrange("p (hc t) -> p hc t", t=1024)
        for r in range(4):
            P.w("tensor", "D_x2e", 4 * m + r)
            for i in range(4):
                hc = 4 * r + i
                P.o("tensor", lambda E, hc=hc, i=i, xn2b=xn2b: E.matmul(
                    PS3[:, 128 * i:128 * (i + 1)],
                    xn2b[:, 128 * hc:128 * (hc + 1)], self.ident,
                    start=True, stop=True), inc=("PE_x2t" if i == 3 else None))
            P.w("vector", "PE_x2t")
            P.o("vector", lambda E, r=r, m=m, dst=dst: E.tensor_copy(
                dst[:, 4 * r:4 * (r + 1), 128 * m:128 * (m + 1)],
                PS3[:, 0:512].rearrange("p (hc t) -> p hc t", t=128)), inc="D_x2e")

    def emit_gate(self):
        P, L = self.P, self.lay
        for eng in ("sync", "vector", "scalar", "tensor"):
            for sem in ("PE_dqT", "D_dec", "P_dec", "L_stg", "L_pk"):
                P.w(eng, sem)
        for sem in ("A_exp", "D_att", "D_x2e", "A_stg", "D_qr"):
            P.w("tensor", sem)
        for sem in ("PE_ptf", "PE_xm", "D_att", "D_x2e", "D_msk", "A_xn"):
            P.w("scalar", sem)
        self.gn = 0
        srr = 0
        for sl in range(4):
            P.w("sync", f"CC_g{sl}", 1)
            P.w("sync", f"CC_u{sl}", 1)
            P.w("scalar", f"CC_g{sl}", 1)
            P.w("scalar", f"CC_u{sl}", 1)
            for rr in range(8):
                self.emit_gate_srr(sl, rr, srr)
                srr += 1

    def emit_gate_srr(self, sl, rr, srr):
        P, L = self.P, self.lay
        par = srr % 2
        wg = self.V(L.gslab if par == 0 else L.KT, 16 * 512, BF16)
        wu = self.V((L.gslab + 16384) if par == 0 else L.r6, 16 * 512, BF16)
        gelu = self.V(L.gelu, 1024, F32)
        if srr >= 2:
            P.w("scalar", "PE_gmm", 4 * (srr - 1))
        for (wsl, gath) in ((wg, self.gath_g[sl]), (wu, self.gath_u[sl])):
            P.o("scalar", lambda E, wsl=wsl, gath=gath, rr=rr: E.dma_start(
                out=wsl.rearrange("p (h n) -> p h n", n=512),
                in_=gath.ap().rearrange("(r h p) n -> r p h n", p=128, h=16)[rr]),
                inc="L_gs", inc_n=16)
        P.w("tensor", "L_gs", 32 * (srr + 1))
        for h2 in range(2):
            for nc2 in range(2):
                gn = self.gn
                self.gn += 1
                psG = self.PS[2 * (gn % 2)]
                psU = self.PS[2 * (gn % 2) + 1]
                if gn >= 2:
                    P.w("tensor", "A_gelu", gn - 1)
                    P.w("tensor", "D_int", gn - 1)
                nco = 256 * h2 + 128 * nc2
                for wsl, ps in ((wg, psG), (wu, psU)):
                    for hc in range(16):
                        for th in range(2):
                            lastg = wsl is wu and hc == 15 and th == 1
                            P.o("tensor", lambda E, wsl=wsl, ps=ps, hc=hc, th=th, nco=nco: E.matmul(
                                ps[:, 512 * th:512 * (th + 1)],
                                wsl.rearrange("p (h n) -> p h n", n=512)[:, hc, nco:nco + 128],
                                self.xn2T.rearrange("p (hc t) -> p hc t", t=1024)[:, hc, 512 * th:512 * (th + 1)],
                                start=(hc == 0), stop=(hc == 15)),
                                inc=("PE_gmm" if lastg else None))
                P.w("scalar", "PE_gmm")
                if gn >= 1:
                    P.w("scalar", "D_int", gn)
                P.o("scalar", lambda E, psG=psG, gelu=gelu: E.activation(
                    gelu, psG[:, :], AF.Gelu_apprx_tanh), inc="A_gelu")
                ib = self.V(L.inter + 2048 * (gn % 2), 1024, BF16)
                P.w("vector", "A_gelu")
                if gn >= 2:
                    P.w("vector", "L_int", 16 * (gn - 1))
                P.o("vector", lambda E, ib=ib, gelu=gelu, psU=psU: E.tensor_tensor(
                    ib, gelu, psU[:, :], ALU.mult), inc="D_int")
                nrow = 2048 * rr + 512 * sl + 128 * (2 * h2 + nc2)
                P.w("sync", "D_int")
                P.o("sync", lambda E, ib=ib, nrow=nrow: E.dma_start(
                    out=self.interT.ap()[nrow:nrow + 128, :], in_=ib),
                    inc="L_int", inc_n=16)

    def emit_down(self):
        P, L = self.P, self.lay
        for sem in ("A_gelu", "D_int"):
            P.w("tensor", sem)
        for sem in ("PE_ptf", "PE_xm", "A_xn", "D_att", "D_msk", "L_int", "PE_gmm",
                    "L_xmD", "L_xo", "L_am", "L_cq"):
            P.w("sync", sem)
        acc = [self.V((L.acc0 if t < 4 else L.acc1) + 8192 * (t % 4), 2048, F32)
               for t in range(8)]
        for t in range(8):
            P.o("sync", lambda E, t=t, a=acc[t]: E.dma_start(
                out=a, in_=self.xmidD.ap()[128 * t:128 * (t + 1), :]),
                inc="L_seed", inc_n=16)
        P.w("vector", "L_seed")
        P.w("tensor", "L_seed")
        gidx = 0
        gt = 0
        for i_d in range(4):
            P.w("sync", f"CC_d{i_d}", 1)
            P.w("scalar", f"CC_d{i_d}", 1)
            for rr in range(8):
                par = gidx % 2
                wdt = [self.V((L.dn_wdt if par == 0 else L.r6) + 4096 * k, 2048, BF16)
                       for k in range(4)]
                itt = [self.V(L.xn2T + 8192 * par + 2048 * k, 1024, BF16)
                       for k in range(4)]
                if gidx >= 2:
                    P.w("scalar", "PE_dn", 8 * (gidx - 1))
                    P.w("sync", "PE_dn", 8 * (gidx - 1))
                for k in range(4):
                    P.o("scalar", lambda E, k=k, i_d=i_d, rr=rr, w=wdt[k]: E.dma_start(
                        out=w, in_=self.gath_d[i_d].ap().rearrange(
                            "(r k p) h -> r k p h", k=4, p=128)[rr, k]),
                        inc="L_dnw", inc_n=16)
                    P.o("sync", lambda E, k=k, i_d=i_d, rr=rr, it=itt[k]: E.dma_start(
                        out=it, in_=self.interT.ap().rearrange(
                            "(r s k p) t -> r s k p t", s=4, k=4, p=128)[rr, i_d, k]),
                        inc="L_dn", inc_n=16)
                P.w("tensor", "L_dn", 64 * (gidx + 1))
                P.w("tensor", "L_dnw", 64 * (gidx + 1))
                for t in range(8):
                    pa = self.PS[2 * (t % 2)]
                    pb = self.PS[2 * (t % 2) + 1]
                    if gt >= 2:
                        P.w("tensor", "D_acc", gt - 1)
                    for k in range(4):
                        for hr in range(4):
                            ps = (pa, pa, pb, pb)[hr][:, 512 * (hr % 2):512 * (hr % 2 + 1)]
                            P.o("tensor", lambda E, k=k, hr=hr, t=t, ps=ps, w=wdt[k], it=itt[k]: E.matmul(
                                ps, it[:, 128 * t:128 * (t + 1)],
                                w[:, 512 * hr:512 * (hr + 1)],
                                start=(k == 0), stop=(k == 3)),
                                inc=("PE_dn" if k == 3 and hr == 3 else None))
                    P.w("vector", "PE_dn")
                    P.o("vector", lambda E, pa=pa, t=t, a=acc[t]: E.scalar_tensor_tensor(
                        a[:, 0:1024], pa[:, :], 1.0, a[:, 0:1024], ALU.mult, ALU.add))
                    P.o("vector", lambda E, pb=pb, t=t, a=acc[t]: E.scalar_tensor_tensor(
                        a[:, 1024:2048], pb[:, :], 1.0, a[:, 1024:2048], ALU.mult, ALU.add),
                        inc="D_acc")
                    if i_d == 3 and rr == 7:
                        P.w("sync", "D_acc")
                        P.o("sync", lambda E, t=t, a=acc[t]: E.dma_start(
                            out=self.out[128 * t:128 * (t + 1), :], in_=a),
                            inc="L_out", inc_n=16)
                    gt += 1
                gidx += 1
        P.w("sync", "L_out", 16 * 8)

    def schedule(self):
        self.deq_cnt = 0
        self.ln_cnt = 0
        self.agw = {}
        self.boot()
        sweeponly = ("sweeponly" in self.dbg) or ("swdeq" in self.dbg)
        nodeq = ("sweeponly" in self.dbg) or ("nodeq" in self.dbg)
        dorder = []
        for blk in range(4):
            dorder += [("g", 4 * blk + i) for i in range(4)]
            dorder += [("u", 4 * blk + i) for i in range(4)]
        for g in range(SEQ_CH):
            self.emit_sweep_chunk(g)
            if not nodeq:
                mat, ti = dorder[g]
                self.emit_deq_tile(mat, ti)
                if g % 4 == 3:
                    mat, ti = dorder[g]
                    sl = ti // 4
                    self.agw[(mat, sl)] = self.P.sem("L_stg").n
                    self.emit_ag(mat, sl)
        if sweeponly:
            P = self.P
            dkt = self.nc.dram_tensor("dbg_kt", [128, 4096], BF16, kind="ExternalOutput").ap()
            dvs = self.nc.dram_tensor("dbg_vs", [128, 4096], BF16, kind="ExternalOutput").ap()
            dxo = self.nc.dram_tensor("dbg_xo", [128, 16384], BF16, kind="ExternalOutput").ap()
            P.w("sync", "D_kv2")
            P.w("sync", "D_xnte")
            P.o("sync", lambda E: E.dma_start(out=dkt, in_=self.KT), inc="L_dbg4", inc_n=16)
            P.o("sync", lambda E: E.dma_start(out=dvs, in_=self.Vsb), inc="L_dbg4", inc_n=16)
            P.o("sync", lambda E: E.dma_start(out=dxo, in_=self.xnt_own), inc="L_dbg4", inc_n=16)
            P.w("sync", "L_dbg4", 48)
            self.P.emit()
            return
        self.emit_q_pre()
        for dcg in range(4):
            self.emit_q_group(dcg)
            if not nodeq:
                for i in range(4):
                    self.emit_deq_tile("d", 4 * dcg + i)
        if not nodeq:
            for i in range(4):
                self.agw[("d", i)] = self.P.sem("L_stg").n
                self.emit_ag("d", i)
        if "noattn" in self.dbg:
            P = self.P
            dkt = self.nc.dram_tensor("dbg_kt", [128, 4096], BF16, kind="ExternalOutput").ap()
            dvs = self.nc.dram_tensor("dbg_vs", [128, 4096], BF16, kind="ExternalOutput").ap()
            dqt = self.nc.dram_tensor("dbg_qt", [128, 16384], BF16, kind="ExternalOutput").ap()
            for sem in ("D_qr", "D_dec", "P_dec", "A_stg"):
                P.w("sync", sem)
            P.o("sync", lambda E: E.dma_start(out=dkt, in_=self.KT), inc="L_dbg4", inc_n=16)
            P.o("sync", lambda E: E.dma_start(out=dvs, in_=self.Vsb), inc="L_dbg4", inc_n=16)
            P.o("sync", lambda E: E.dma_start(out=dqt, in_=self.QT), inc="L_dbg4", inc_n=16)
            dar = self.nc.dram_tensor("dbg_arena", [128, self.ar.size], U8, kind="ExternalOutput").ap()
            P.o("sync", lambda E: E.dma_start(out=dar, in_=self.ar.h[0:128, :]), inc="L_dbg4", inc_n=16)
            P.w("sync", "L_dbg4", 64)
            if not nodeq:
                for i in range(4):
                    P.w("gpsimd", f"CC_d{i}", 1)
            self.P.emit()
            return
        self.emit_attention()
        self.emit_gate()
        self.emit_down()
        if "dump" in self.dbg:
            P = self.P
            dxm = self.nc.dram_tensor("dbg_xmid", [TPC, EMB], F32, kind="ExternalOutput").ap()
            dcg = self.nc.dram_tensor("dbg_cg", [EMB, 512], BF16, kind="ExternalOutput").ap()
            dgd = self.nc.dram_tensor("dbg_gd", [512, EMB], BF16, kind="ExternalOutput").ap()
            dint = self.nc.dram_tensor("dbg_int", [DFF, TPC], BF16, kind="ExternalOutput").ap()
            dx2 = self.nc.dram_tensor("dbg_xn2t", [128, 16 * TPC], BF16, kind="ExternalOutput").ap()
            P.w("sync", "L_out", 16 * 8)
            P.o("sync", lambda E: E.dma_start(out=dxm, in_=self.xmidD.ap()), inc="L_dbg", inc_n=16)
            P.o("sync", lambda E: E.dma_start(out=dcg, in_=self.contrib_g[0].ap()), inc="L_dbg", inc_n=16)
            P.o("sync", lambda E: E.dma_start(out=dgd, in_=self.gath_d[0].ap()[0:512, :]), inc="L_dbg", inc_n=16)
            P.o("sync", lambda E: E.dma_start(out=dint, in_=self.interT.ap()), inc="L_dbg", inc_n=16)
            P.o("sync", lambda E: E.dma_start(out=dx2, in_=self.xn2T), inc="L_dbg", inc_n=16)
            P.w("sync", "L_dbg", 80)
        self.P.emit()


def build_nc(dbg=()):
    nc = bass.Bass("TRN2", target_bir_lowering=False)
    b = Build(nc, dbg=dbg)
    b.schedule()
    return nc


def kernel(**inputs):
    maps = make_in_maps(inputs)
    nc = build_nc()
    res = run_bass_kernel_spmd(nc, maps, core_ids=list(range(NC)))
    return assemble_output(res.results)


# version 06 (final)
# speedup vs baseline: 1.0400x; 1.0063x over previous
import math
import numpy as np
import ml_dtypes
import concourse.bass as bass
import concourse.mybir as mybir
from concourse.bass_utils import run_bass_kernel_spmd

F32 = mybir.dt.float32
BF16 = mybir.dt.bfloat16
I32 = mybir.dt.int32
U8 = mybir.dt.uint8
AF = mybir.ActivationFunctionType
ALU = mybir.AluOpType
AX = mybir.AxisListType

B, L, EMB, HD, NH, DFF = 2, 4096, 2048, 128, 16, 16384
NC = 8
SEQ_CH = L // 128
OWN_CH = 8
TPC = OWN_CH * 128
NPC = DFF // NC
ISQ = 1.0 / math.sqrt(128.0)
CSHIFT = 10.0
EPS = 1e-5

BF = ml_dtypes.bfloat16


def _bf(x):
    return np.ascontiguousarray(x.astype(BF))



def host_constants():
    ident = np.eye(128, dtype=np.float32)
    rt = np.zeros((128, 128), dtype=np.float32)
    for i in range(64):
        rt[2 * i + 1, 2 * i] = -1.0
        rt[2 * i, 2 * i + 1] = 1.0
    ones = np.ones((128, 128), dtype=np.float32)
    tri = (np.arange(128)[None, :] >= np.arange(128)[:, None]).astype(np.float32)
    inv = 1.0 / (10000.0 ** (np.arange(0, HD, 2, dtype=np.float32) / HD))
    t = np.arange(L, dtype=np.float32)
    f = np.outer(t, inv)
    f = np.repeat(f, 2, axis=-1)
    cosT = np.cos(f).astype(np.float32).T.copy()
    sinT = np.sin(f).astype(np.float32).T.copy()
    return ident, rt, ones, tri, cosT, sinT


def chunk_perm(j):
    perm = []
    for m in range(SEQ_CH // 4):
        others = [4 * m + t for t in range(4) if t != j]
        perm.extend(others + [4 * m + j])
    return np.array(perm, dtype=np.int64)


def build_amask(j, tri):
    ones = np.ones((128, 128), dtype=np.float32)
    zeros = np.zeros((128, 128), dtype=np.float32)

    def A(s):
        if s == 3:
            return tri
        return ones if s < j else zeros

    slots = []
    for sp in range(8):
        if sp < 4:
            m = np.concatenate([A(sp), ones], axis=1)
        else:
            m = np.concatenate([zeros, A(sp - 4)], axis=1)
        slots.append(m)
    return np.concatenate(slots, axis=1)


def make_in_maps(inputs):
    x = np.asarray(inputs["L0_attn_x"])
    wq = np.asarray(inputs["L0_attn_Wq"])
    wk = np.asarray(inputs["L0_attn_Wk"])
    wv = np.asarray(inputs["L0_attn_Wv"])
    gpk = np.asarray(inputs["L0_ffn_gate_W_packed"])
    gsc = np.asarray(inputs["L0_ffn_gate_scales"])
    upk = np.asarray(inputs["L0_ffn_up_W_packed"])
    usc = np.asarray(inputs["L0_ffn_up_scales"])
    dpk = np.asarray(inputs["L0_ffn_down_W_packed"])
    dsc = np.asarray(inputs["L0_ffn_down_scales"])
    ident, rt, ones, tri, cosT, sinT = host_constants()

    maps = []
    for c in range(NC):
        s, j = c // 4, c % 4
        perm = chunk_perm(j)
        xs = x[s].reshape(SEQ_CH, 128, EMB)[perm].reshape(L, EMB)
        ct = cosT.reshape(128, SEQ_CH, 128)[:, perm].reshape(128, L)
        st = sinT.reshape(128, SEQ_CH, 128)[:, perm].reshape(128, L)
        own = [4 * m + j for m in range(OWN_CH)]
        cq = cosT.reshape(128, SEQ_CH, 128)[:, own].reshape(128, TPC)
        sq = sinT.reshape(128, SEQ_CH, 128)[:, own].reshape(128, TPC)
        m = {
            "xseq": np.ascontiguousarray(xs),
            "wq": np.ascontiguousarray(wq),
            "wk": np.ascontiguousarray(wk),
            "wv": np.ascontiguousarray(wv),
            "gpk": np.ascontiguousarray(gpk[NPC * c:NPC * (c + 1)]),
            "gsc": np.ascontiguousarray(gsc[NPC * c:NPC * (c + 1)]),
            "upk": np.ascontiguousarray(upk[NPC * c:NPC * (c + 1)]),
            "usc": np.ascontiguousarray(usc[NPC * c:NPC * (c + 1)]),
            "dpk": np.ascontiguousarray(dpk[:, 1024 * c:1024 * (c + 1)]),
            "dsc": np.ascontiguousarray(dsc[:, 64 * c:64 * (c + 1)]),
            "cost": np.ascontiguousarray(ct),
            "sint": np.ascontiguousarray(st),
            "cosq": np.ascontiguousarray(cq),
            "sinq": np.ascontiguousarray(sq),
            "ident": _bf(ident),
            "rotT": _bf(rt),
            "ones": _bf(ones),
            "amask": _bf(build_amask(j, tri)),
        }
        maps.append(m)
    return maps


def assemble_output(results):
    out = np.zeros((B, L, EMB), dtype=np.float32)
    for c in range(NC):
        s, j = c // 4, c % 4
        shard = results[c]["out"].reshape(OWN_CH, 128, EMB)
        for m in range(OWN_CH):
            g = 4 * m + j
            out[s, 128 * g:128 * (g + 1)] = shard[m]
    return out



ENGINES = ("sync", "tensor", "vector", "scalar", "gpsimd")


class Sem:
    __slots__ = ("h", "n", "virtual", "eng", "counts")

    def __init__(self, nc, name):
        self.h = nc.alloc_semaphore(name)
        self.n = 0
        self.virtual = False


class VSem:
    __slots__ = ("n", "eng", "counts", "virtual")

    def __init__(self):
        self.n = 0
        self.eng = None
        self.counts = []
        self.virtual = True


CHAINED = ("vector", "scalar")


class Prog:
    def __init__(self, nc):
        self.nc = nc
        self.st = {e: [] for e in ENGINES}
        self.sems = {}
        self.chain = {e: Sem(nc, f"CH_{e}") for e in CHAINED}

    def sem(self, name):
        if name not in self.sems:
            self.sems[name] = VSem()
        return self.sems[name]

    def _realize(self, name):
        v = self.sems.get(name)
        if v is None or v.virtual and v.eng is None and not v.counts and v.n == 0:
            pass
        return v

    def o(self, eng, fn, inc=None, inc_n=1):
        chained = eng in CHAINED and inc_n == 1
        if chained:
            ch = self.chain[eng]
            prior = ch.n
            ch.n += 1
            if inc is not None:
                vs = self.sem(inc)
                if isinstance(vs, Sem):
                    raise RuntimeError(f"sem {inc} already real, inc from {eng}")
                if vs.eng is None:
                    vs.eng = eng
                assert vs.eng == eng, (inc, vs.eng, eng)
                vs.n += 1
                vs.counts.append(ch.n)

            def wrapped(E, fn=fn, h=ch.h, prior=prior):
                if prior:
                    E.wait_ge(h, prior)
                fn(E).then_inc(h, 1)
            self.st[eng].append(wrapped)
        else:
            if inc is not None:
                vs = self.sems.get(inc)
                if vs is None or isinstance(vs, VSem):
                    if vs is not None and (vs.counts or vs.eng):
                        raise RuntimeError(f"sem {inc} already virtual")
                    self.sems[inc] = Sem(self.nc, inc)
                sem = self.sems[inc]
                prior = sem.n
                sem.n += inc_n
                if inc_n == 16 and prior:
                    self.st[eng].append(lambda E, h=sem.h, v=prior: E.wait_ge(h, v))

                def wrapped(E, fn=fn, h=sem.h, k=inc_n):
                    fn(E).then_inc(h, k)
                self.st[eng].append(wrapped)
            else:
                self.st[eng].append(fn)

    def w(self, eng, sem, val=None):
        if isinstance(sem, str):
            sem = self.sem(sem)
        v = sem.n if val is None else val
        if v <= 0:
            return
        if getattr(sem, "virtual", False):
            assert sem.eng is not None, "wait on never-incremented virtual sem"
            assert v <= len(sem.counts), (v, len(sem.counts))
            h = self.chain[sem.eng].h
            tgt = sem.counts[v - 1]
            self.st[eng].append(lambda E, h=h, t=tgt: E.wait_ge(h, t))
        else:
            self.st[eng].append(lambda E, h=sem.h, v=v: E.wait_ge(h, v))

    def emit(self):
        with self.nc.Block() as blk:
            for e in ENGINES:
                ops = self.st[e]
                if not ops:
                    continue

                def runner(E, ops=ops):
                    for f in ops:
                        f(E)
                getattr(blk, e)(runner)


ESZ = {F32: 4, BF16: 2, I32: 4, U8: 1}


class Arena:

    def __init__(self, nc, size):
        self.h = nc.alloc_sbuf_tensor("arena", [128, size], U8)
        self.size = size

    def view(self, off, n_elems, dtype, p=128):
        nb = n_elems * ESZ[dtype]
        assert off % 64 == 0 and off + nb <= self.size, (off, nb, self.size)
        return self.h[0:p, off:off + nb].bitcast(dtype)


class Layout:

    def __init__(self):
        o = 0

        def take(nb):
            nonlocal o
            off = (o + 63) & ~63
            o = off + nb
            return off

        self.cbank = take(64)
        self.ident = take(256)
        self.rotT = take(256)
        self.ones = take(256)
        self.wkb = take(4096)
        self.wvb = take(4096)
        self.PT = self.wkb
        self.KT = take(8192)
        self.Vsb = take(8192)
        self.dn_wdt = self.KT
        self.dn_int = None
        r2 = take(32768)
        self.QT = r2
        self.acc0 = r2
        s = r2
        self.xring = s; s += 16384
        self.xnt_sweep = s; s += 8192
        self.csring = s; s += 2048
        self.k0 = s; s += 512
        self.vt0 = s; s += 512
        self.u1 = s; s += 512
        self.u2 = s; s += 512
        self.lnt = s; s += 2048
        assert s - r2 <= 32768, s - r2
        r3 = take(32768)
        self.xnt_own = r3
        self.acc1 = r3
        s = r3
        self.attnT = s; s += 16384
        self.recip = s; s += 1024
        self.xmid = s; s += 8192
        self.amask = s; s += 4096
        assert s - r3 <= 32768, s - r3
        r4 = take(32768)
        self.xn2T = r4
        self.wq0 = r4
        self.wq1 = r4 + 16384
        self.accq = None
        r5 = take(45056)
        s = r5
        self.pk = s; s += 8192
        self.scl = s; s += 1024
        self.dq = s; s += 26624
        self.wdec = s; s += 4096
        self.wstg = s; s += 4096
        assert s - r5 <= 45056, s - r5
        self.gslab = r5
        self.gelu = r5 + 32768
        self.inter = r5 + 32768 + 4096
        r6 = take(24576)
        self.r6 = r6
        self.total = (o + 63) & ~63



class Build:
    def __init__(self, nc, dbg=()):
        self.nc = nc
        self.dbg = set(dbg)
        self.P = Prog(nc)
        self.lay = Layout()
        self.ar = Arena(nc, self.lay.total)
        L = self.lay

        def din(name, shape, dt):
            return nc.dram_tensor(name, shape, dt, kind="ExternalInput").ap()

        self.xseq = din("xseq", [L_ := 4096, EMB], F32)
        self.wq = din("wq", [EMB, EMB], F32)
        self.wk = din("wk", [EMB, HD], F32)
        self.wv = din("wv", [EMB, HD], F32)
        self.gpk = din("gpk", [NPC, 1024], I32)
        self.gsc = din("gsc", [NPC, 64], F32)
        self.upk = din("upk", [NPC, 1024], I32)
        self.usc = din("usc", [NPC, 64], F32)
        self.dpk = din("dpk", [EMB, 1024], I32)
        self.dsc = din("dsc", [EMB, 64], F32)
        self.cost = din("cost", [128, 4096], F32)
        self.sint = din("sint", [128, 4096], F32)
        self.cosq = din("cosq", [128, TPC], F32)
        self.sinq = din("sinq", [128, TPC], F32)
        self.identD = din("ident", [128, 128], BF16)
        self.rotD = din("rotT", [128, 128], BF16)
        self.onesD = din("ones", [128, 128], BF16)
        self.amaskD = din("amask", [128, 2048], BF16)
        self.out = nc.dram_tensor("out", [TPC, EMB], F32, kind="ExternalOutput").ap()

        self.contrib_g = [nc.dram_tensor(f"cg{s}", [EMB, 512], BF16) for s in range(4)]
        self.contrib_u = [nc.dram_tensor(f"cu{s}", [EMB, 512], BF16) for s in range(4)]
        self.contrib_d = nc.dram_tensor("cd", [NPC, EMB], BF16)
        self.gath_g = [nc.dram_tensor(f"gg{s}", [DFF, 512], BF16) for s in range(4)]
        self.gath_u = [nc.dram_tensor(f"gu{s}", [DFF, 512], BF16) for s in range(4)]
        self.gath_d = [nc.dram_tensor(f"gd{i}", [4096, EMB], BF16) for i in range(4)]
        self.interT = nc.dram_tensor("interT", [DFF, TPC], BF16)
        self.xmidD = nc.dram_tensor("xmidD", [TPC, EMB], F32)

        self.PS = [nc.alloc_psum_tensor(f"ps{i}", [128, 1024], F32) for i in range(4)]

        a = self.ar
        self.ident = a.view(L.ident, 128 * 128 // 128, BF16)
        self.rotT = a.view(L.rotT, 128, BF16)
        self.ones = a.view(L.ones, 128, BF16)
        self.wkb = a.view(L.wkb, 16 * 128, BF16)
        self.wvb = a.view(L.wvb, 16 * 128, BF16)
        self.KT = a.view(L.KT, 4096, BF16)
        self.Vsb = a.view(L.Vsb, 4096, BF16)
        self.QT = a.view(L.QT, 16 * 1024, BF16)
        self.xn2T = a.view(L.xn2T, 16 * 1024, BF16)
        self.xnt_own = a.view(L.xnt_own, 16 * 1024, BF16)
        self.amask = a.view(L.amask, 2048, BF16)
        cb = a.view(L.cbank, 16, F32)
        self.cb75 = cb[:, 0:1]
        self.cbm6 = cb[:, 1:2]
        self.cbeps = cb[:, 2:3]
        self.cbsh = cb[:, 3:4]
        self.cbm4 = cb[:, 4:5]

    def V(self, off, n, dt):
        return self.ar.view(off, n, dt)

    def boot(self):
        P, L = self.P, self.lay
        for src, off in ((self.identD, L.ident), (self.rotD, L.rotT),
                         (self.onesD, L.ones)):
            n = src.shape[1]
            P.o("sync", lambda E, s=src, o=off, n=n: E.dma_start(
                out=self.V(o, n, BF16), in_=s), inc="Lc", inc_n=16)
        for src, off in ((self.wk, L.wkb), (self.wv, L.wvb)):
            P.o("gpsimd", lambda E, s=src, o=off: E.dma_start(
                out=self.V(o, 16 * 128, BF16).rearrange("p (h d) -> p h d", d=128),
                in_=s.rearrange("(h p) d -> p h d", p=128)), inc="Lcg", inc_n=16)
        P.o("vector", lambda E: E.memset(self.cb75, 7.5))
        P.o("vector", lambda E: E.memset(self.cbm6, -6.0))
        P.o("vector", lambda E: E.memset(self.cbeps, EPS))
        P.o("vector", lambda E: E.memset(self.cbm4, -4.0))
        P.o("vector", lambda E: E.memset(self.cbsh, -CSHIFT), inc="D_cb")
        self.P.w("tensor", "Lc")
        self.P.w("tensor", "Lcg")
        self.P.w("vector", "Lc")
        self.P.w("vector", "Lcg")
        self.P.w("scalar", "Lc")
        self.P.w("scalar", "Lcg")
        self.P.w("scalar", "D_cb")
        self.P.w("gpsimd", "Lc")

    def emit_ln(self, xap, xnap, par, wait_before=None, lnt_off=None):
        P, L = self.P, self.lay
        lnt = (L.lnt + 1024 * (par & 1)) if lnt_off is None else lnt_off
        cnt = self.ln_cnt
        self.ln_cnt += 1
        s1 = self.V(lnt, 8, F32)
        sq4 = self.V(lnt + 64, 8, F32)
        mm = self.V(lnt + 128, 8, F32)
        msq = self.V(lnt + 192, 8, F32)
        var = self.V(lnt + 256, 8, F32)
        rsd = self.V(lnt + 320, 8, F32)
        nb = self.V(lnt + 384, 8, F32)
        sqs = self.V(lnt + 448, 8, F32)
        PS3 = self.PS[3]

        if wait_before:
            for eng, sem, v in wait_before:
                P.w(eng, sem, v)
        P.o("vector", lambda E: E.reduce_sum(s1[:, 0:1], xap, AX.X), inc="D_s1")
        if cnt >= 2:
            P.w("scalar", "D_var", cnt - 1)
        P.o("scalar", lambda E: E.activation(xnap, xap, AF.Square), inc="A_sqp")
        P.w("vector", "A_sqp")
        P.o("vector", lambda E: E.reduce_sum(sqs[:, 0:1], xnap, AX.X), inc="D_sqr")
        P.o("vector", lambda E: E.tensor_scalar_mul(mm[:, 0:1], s1[:, 0:1], 1.0 / EMB))
        P.o("vector", lambda E: E.tensor_mul(msq[:, 0:1], mm[:, 0:1], mm[:, 0:1]))
        P.o("vector", lambda E: E.scalar_tensor_tensor(
            var[:, 0:1], sqs[:, 0:1], 1.0 / EMB, msq[:, 0:1],
            ALU.mult, ALU.subtract), inc="D_var")
        P.w("scalar", "D_var")
        if cnt >= 2:
            P.w("scalar", "D_ln", cnt - 1)
        P.o("scalar", lambda E: E.activation(rsd[:, 0:1], var[:, 0:1], AF.Sqrt,
                                             bias=self.cbeps), inc="A_sqrt")
        P.w("vector", "A_sqrt")
        if cnt >= 2:
            P.w("vector", "A_xn", cnt - 1)
        P.o("vector", lambda E: E.reciprocal(rsd[:, 0:1], rsd[:, 0:1]))
        P.o("vector", lambda E: E.scalar_tensor_tensor(
            nb[:, 0:1], mm[:, 0:1], -1.0, rsd[:, 0:1], ALU.mult, ALU.mult),
            inc="D_ln")
        P.w("scalar", "D_ln")
        P.o("scalar", lambda E: E.activation(
            xnap, xap, AF.Identity, bias=nb[:, 0:1], scale=rsd[:, 0:1]),
            inc="A_xn")
        if "dump" in self.dbg and cnt == 32:
            snap = self.V(L.r6 + 20480, 128, F32)
            P.o("vector", lambda E, snap=snap: E.tensor_copy(
                snap[:, 0:16], self.V(lnt, 16, F32)[:, 0:16]))
            P.o("vector", lambda E, snap=snap: E.tensor_copy(
                snap[:, 16:32], self.V(lnt + 448, 16, F32)[:, 0:16]))
            P.o("vector", lambda E, snap=snap: E.tensor_copy(
                snap[:, 32:48], self.V(lnt + 256, 16, F32)[:, 0:16]))
            P.o("vector", lambda E, snap=snap: E.tensor_copy(
                snap[:, 48:64], self.V(lnt + 320, 16, F32)[:, 0:16]), inc="D_snap")
            dsn = self.nc.dram_tensor("dbg_ln2m0", [128, 128], F32, kind="ExternalOutput").ap()
            P.w("sync", "D_snap")
            P.o("sync", lambda E, dsn=dsn, snap=snap: E.dma_start(out=dsn, in_=snap),
                inc="L_dbg6", inc_n=16)

    def emit_deq_tile(self, mat, ti):
        P, L = self.P, self.lay
        t = self.deq_cnt
        self.deq_cnt += 1
        par = t % 2
        pk = self.V(L.pk + 4096 * par, 1024, I32)
        scl = self.V(L.scl + 512 * par, 64, F32)
        src_pk = {"g": self.gpk, "u": self.upk, "d": self.dpk}[mat]
        src_sc = {"g": self.gsc, "u": self.usc, "d": self.dsc}[mat]
        dqo = L.dq
        wf = self.V(dqo, 1024, BF16)
        lo = self.V(dqo + 2048, 1024, BF16)
        sgl = self.V(dqo + 4096, 1024, BF16)
        la = self.V(dqo + 6144, 1024, BF16)
        lb = self.V(dqo + 8192, 1024, BF16)
        ld_ = self.V(dqo + 10240, 1024, BF16)
        dqh = L.dq + 12288
        hi_i = self.V(dqh, 1024, I32)
        hi = self.V(dqh + 4096, 1024, BF16)
        sgh = self.V(dqh + 6144, 1024, BF16)
        ha = self.V(dqh + 8192, 1024, BF16)
        hb = self.V(dqh + 10240, 1024, BF16)
        hd_ = self.V(dqh + 12288, 1024, BF16)
        wdec = self.V(L.wdec, 2048, BF16)
        wstg = self.V(L.wstg, 16 * 128, BF16)
        PS3 = self.PS[3]

        if t >= 2:
            P.w("sync", "D_dec", t - 1)
            P.w("sync", "P_dec", t - 1)
        P.o("sync", lambda E: E.dma_start(out=pk, in_=src_pk[128 * ti:128 * (ti + 1), :]),
            inc="L_pk", inc_n=16)
        P.o("sync", lambda E: E.dma_start(out=scl, in_=src_sc[128 * ti:128 * (ti + 1), :]),
            inc="L_pk", inc_n=16)

        scv = scl.unsqueeze(2).broadcast_to([128, 64, 16])
        P.w("vector", "L_pk", 32 * (t + 1))
        if t >= 1:
            P.w("vector", "PE_dqT", 4 * t)
        P.o("vector", lambda E: E.tensor_single_scalar(hi_i, pk, 4, ALU.logical_shift_right))
        P.o("vector", lambda E: E.tensor_copy(hi, hi_i), inc="P_hi")
        P.o("vector", lambda E: E.tensor_copy(wf, pk))
        P.w("scalar", "P_hi")
        P.o("scalar", lambda E: E.activation(sgh, hi, AF.Sign, bias=self.cb75, scale=-1.0), inc="A_sgh")
        P.o("vector", lambda E: E.scalar_tensor_tensor(lo, hi, -16.0, wf, ALU.mult, ALU.add),
            inc="D_lo")
        P.w("scalar", "D_lo")
        P.o("scalar", lambda E: E.activation(sgl, lo, AF.Sign, bias=self.cb75, scale=-1.0), inc="A_sgl")
        P.o("scalar", lambda E: E.activation(hb, hi, AF.Identity, bias=self.cbm4),
            inc="A_nm4h")
        P.w("vector", "A_sgh")
        P.w("vector", "A_nm4h")
        P.o("vector", lambda E: E.scalar_tensor_tensor(ha, sgh, 4.0, hb, ALU.mult, ALU.add),
            inc="P_kh")
        P.w("scalar", "P_kh")
        P.o("scalar", lambda E: E.activation(hb, ha, AF.Relu, bias=self.cbm4))
        P.o("scalar", lambda E: E.activation(hd_, ha, AF.Relu, bias=self.cbm6), inc="A_r2h")
        P.w("vector", "A_r2h")
        P.o("vector", lambda E: E.tensor_add(hb, hb, ha))
        P.o("vector", lambda E: E.scalar_tensor_tensor(hb, hb, 0.5, hd_, ALU.mult, ALU.add))
        P.o("vector", lambda E: E.tensor_mul(hb, hb, sgh))
        P.o("vector", lambda E: E.tensor_tensor(
            wdec[:, 1::2].rearrange("p (a c) -> p a c", c=16),
            hb.rearrange("p (a c) -> p a c", c=16), scv, ALU.mult), inc="P_dec")
        P.o("scalar", lambda E: E.activation(la, lo, AF.Identity, bias=self.cbm4),
            inc="A_nm4l")
        P.w("vector", "A_sgl")
        P.w("vector", "A_nm4l")
        P.o("vector", lambda E: E.scalar_tensor_tensor(lb, sgl, 4.0, la, ALU.mult, ALU.add),
            inc="D_kl")
        P.w("scalar", "D_kl")
        P.o("scalar", lambda E: E.activation(la, lb, AF.Relu, bias=self.cbm4))
        P.o("scalar", lambda E: E.activation(ld_, lb, AF.Relu, bias=self.cbm6), inc="A_r2l")
        P.w("vector", "A_r2l")
        P.o("vector", lambda E: E.tensor_add(la, la, lb))
        P.o("vector", lambda E: E.scalar_tensor_tensor(la, la, 0.5, ld_, ALU.mult, ALU.add))
        P.o("vector", lambda E: E.tensor_mul(la, la, sgl))
        P.o("vector", lambda E: E.tensor_tensor(
            wdec[:, 0::2].rearrange("p (a c) -> p a c", c=16),
            la.rearrange("p (a c) -> p a c", c=16), scv, ALU.mult), inc="D_dec")

        P.w("tensor", "D_dec", t + 1)
        P.w("tensor", "P_dec", t + 1)
        if t >= 1:
            P.w("scalar", "L_stg", 16 * t)
        for r in range(4):
            P.w("tensor", "A_stg", 4 * t + r)
            for k in range(4):
                kc = 4 * r + k
                P.o("tensor", lambda E, kc=kc, k=k: E.matmul(
                    PS3[:, 512 + 128 * k:512 + 128 * (k + 1)],
                    wdec[:, 128 * kc:128 * (kc + 1)], self.ident,
                    start=True, stop=True), inc=("PE_dqT" if k == 3 else None))
            P.w("scalar", "PE_dqT")
            P.o("scalar", lambda E, r=r: E.activation(
                wstg[:, 512 * r:512 * (r + 1)], PS3[:, 512:1024], AF.Copy),
                inc="A_stg")
        P.w("sync", "A_stg", 4 * (t + 1))
        if mat in ("g", "u"):
            dst = (self.contrib_g if mat == "g" else self.contrib_u)[ti // 4]
            dview = dst.ap().rearrange("(kc p) n -> p kc n", p=128)[
                :, :, 128 * (ti % 4):128 * (ti % 4 + 1)]
        else:
            dview = self.contrib_d.ap().rearrange("(nc p) h -> p nc h", p=128)[
                :, :, 128 * ti:128 * (ti + 1)]
        sview = wstg.rearrange("p (kc n) -> p kc n", n=128)
        P.o("sync", lambda E: E.dma_start(out=dview, in_=sview), inc="L_stg", inc_n=16)

    def emit_ag(self, kind, idx):
        P = self.P
        if kind == "g":
            src, dst, wait = self.contrib_g[idx], self.gath_g[idx], self.agw[("g", idx)]
        elif kind == "u":
            src, dst, wait = self.contrib_u[idx], self.gath_u[idx], self.agw[("u", idx)]
        else:
            src = self.contrib_d.ap()[512 * idx:512 * (idx + 1), :]
            dst, wait = self.gath_d[idx], self.agw[("d", idx)]
        P.w("gpsimd", "L_stg", wait)
        sem = f"CC_{kind}{idx}"
        if kind == "d":
            P.o("gpsimd", lambda E: E.collective_compute(
                "AllGather", ALU.bypass, replica_groups=[list(range(NC))],
                ins=[src.opt()], outs=[dst.ap().opt()]), inc=sem)
        else:
            P.o("gpsimd", lambda E: E.collective_compute(
                "AllGather", ALU.bypass, replica_groups=[list(range(NC))],
                ins=[src.ap().opt()], outs=[dst.ap().opt()]), inc=sem)

    def emit_sweep_chunk(self, g):
        P, L = self.P, self.lay
        par = g % 2
        xb = self.V(L.xring + 8192 * par, 2048, F32)
        xnb = self.V(L.wq0 + 4096 * par, 2048, BF16)
        cs = self.V(L.csring + 1024 * par, 128, F32)
        sn = self.V(L.csring + 1024 * par + 512, 128, F32)
        k0 = self.V(L.k0, 128, BF16)
        vt0 = self.V(L.vt0, 128, BF16)
        u1 = self.V(L.u1, 128, F32)
        u2 = self.V(L.u2, 128, F32)
        PSp = self.PS[par]
        PS2 = self.PS[2]
        own = (g % 4 == 3)
        m = g // 4

        if g >= 2:
            P.w("sync", "A_xn", g - 1)
        P.o("sync", lambda E: E.dma_start(out=xb, in_=self.xseq[128 * g:128 * (g + 1), :]),
            inc="L_x", inc_n=16)
        if g >= 2:
            P.w("scalar", "D_kv2", g - 1)
        P.o("scalar", lambda E: E.dma_start(out=cs, in_=self.cost[:, 128 * g:128 * (g + 1)]),
            inc="L_cs", inc_n=16)
        P.o("scalar", lambda E: E.dma_start(out=sn, in_=self.sint[:, 128 * g:128 * (g + 1)]),
            inc="L_cs", inc_n=16)

        if "dump" in self.dbg and g == 0:
            dx0 = self.nc.dram_tensor("dbg_x0", [128, 2048], F32, kind="ExternalOutput").ap()
            P.o("sync", lambda E, xb=xb, dx0=dx0: E.dma_start(out=dx0, in_=xb),
                inc="L_dbg5", inc_n=16)
        wb = [("vector", P.sem("L_x"), P.sem("L_x").n),
              ("scalar", P.sem("L_x"), P.sem("L_x").n)]
        if g >= 2:
            wb.append(("vector", P.sem("PE_xnt"), 2 * (g - 1)))
            wb.append(("scalar", P.sem("PE_xnt"), 2 * (g - 1)))
        self.emit_ln(xb, xnb, par, wait_before=wb)

        if "dump" in self.dbg and g == 0:
            dxn0 = self.nc.dram_tensor("dbg_xn0", [128, 2048], BF16, kind="ExternalOutput").ap()
            dln0 = self.nc.dram_tensor("dbg_ln0", [128, 128], F32, kind="ExternalOutput").ap()
            P.w("sync", "A_xn", P.sem("A_xn").n)
            P.o("sync", lambda E, xnb=xnb, dxn0=dxn0: E.dma_start(out=dxn0, in_=xnb),
                inc="L_dbg5", inc_n=16)
            P.o("sync", lambda E, dln0=dln0: E.dma_start(
                out=dln0, in_=self.V(L.lnt, 128, F32)), inc="L_dbg5", inc_n=16)
        dst_own = self.xnt_own.rearrange("p (hc t) -> p hc t", t=1024)
        dst_sw = self.V(L.xnt_sweep + 4096 * par, 16 * 128, BF16)
        P.w("tensor", "A_xn")
        for half in range(2):
            P.w("tensor", "D_xnte", (2 * (g - 1) if g >= 2 else 0) if half == 0 else 2 * g + 1)
            for i in range(8):
                hc = 8 * half + i
                P.o("tensor", lambda E, hc=hc, i=i: E.matmul(
                    PSp[:, 128 * i:128 * (i + 1)],
                    xnb[:, 128 * hc:128 * (hc + 1)], self.ident,
                    start=True, stop=True), inc=("PE_xnt" if i == 7 else None))
            P.w("scalar", "PE_xnt")
            if own:
                dv = dst_own[:, 8 * half:8 * (half + 1), 128 * m:128 * (m + 1)]
            else:
                dv = dst_sw.rearrange("p (hc t) -> p hc t", t=128)[:, 8 * half:8 * (half + 1), :]
                if g >= 2:
                    P.w("scalar", "PE_kv1", g - 1)
            P.o("scalar", lambda E, dv=dv: E.activation(
                dv, PSp[:, :].rearrange("p (hc t) -> p hc t", t=128), AF.Copy),
                inc="D_xnte")

        xnt_src = dst_own[:, :, 128 * m:128 * (m + 1)] if own else \
            dst_sw.rearrange("p (hc t) -> p hc t", t=128)
        kvb = PS2[:, 512 * par:512 * par + 512]
        k0 = self.V(L.k0 + 256 * par, 128, BF16)
        vt0 = self.V(L.vt0 + 256 * par, 128, BF16)
        P.w("tensor", "D_xnte")
        P.w("tensor", "D_kv2", max(0, g - 1))
        for hc in range(16):
            P.o("tensor", lambda E, hc=hc, kvb=kvb: E.matmul(
                kvb[:, 0:128],
                self.wkb.rearrange("p (h d) -> p h d", d=128)[:, hc, :],
                xnt_src[:, hc, :], start=(hc == 0), stop=(hc == 15)))
        for hc in range(16):
            P.o("tensor", lambda E, hc=hc, kvb=kvb: E.matmul(
                kvb[:, 128:256],
                self.wvb.rearrange("p (h d) -> p h d", d=128)[:, hc, :],
                xnt_src[:, hc, :], start=(hc == 0), stop=(hc == 15)),
                inc=("PE_kv1" if hc == 15 else None))
        P.w("vector", "PE_kv1")
        P.o("vector", lambda E, kvb=kvb, k0=k0: E.tensor_copy(k0, kvb[:, 0:128]))
        P.o("vector", lambda E, kvb=kvb, vt0=vt0: E.tensor_copy(vt0, kvb[:, 128:256]), inc="D_kv1")
        P.w("tensor", "D_kv1")
        P.o("tensor", lambda E, kvb=kvb, vt0=vt0: E.matmul(kvb[:, 256:384], vt0, self.ident,
                                         start=True, stop=True))
        P.o("tensor", lambda E, kvb=kvb, k0=k0: E.matmul(kvb[:, 384:512], self.rotT, k0,
                                         start=True, stop=True), inc="PE_kv2")
        P.w("vector", "PE_kv2")
        P.w("vector", "L_cs")
        P.o("vector", lambda E, kvb=kvb: E.tensor_copy(
            self.Vsb[:, 128 * g:128 * (g + 1)], kvb[:, 256:384]))
        P.o("vector", lambda E, k0=k0: E.tensor_tensor(u1, k0, cs, ALU.mult))
        P.o("vector", lambda E, kvb=kvb: E.tensor_tensor(u2, kvb[:, 384:512], sn, ALU.mult))
        P.o("vector", lambda E: E.tensor_add(
            self.KT[:, 128 * g:128 * (g + 1)], u1, u2),
            inc="D_kv2")

    def emit_q_pre(self):
        P, L = self.P, self.lay
        P.w("tensor", "D_xnte")
        P.w("tensor", "D_kv2")
        self._q0 = self.V(L.r6 + 8192, 1024, BF16)
        self._cosq = self.V(L.r6, 1024, F32)
        self._sinq = self.V(L.r6 + 4096, 1024, F32)
        P.o("sync", lambda E: E.dma_start(out=self._cosq, in_=self.cosq), inc="L_cq", inc_n=16)
        P.o("sync", lambda E: E.dma_start(out=self._sinq, in_=self.sinq), inc="L_cq", inc_n=16)
        P.w("vector", "L_cq")
        P.w("gpsimd", "PE_xnt")

    def emit_q_group(self, dcg):
        P, L = self.P, self.lay
        PS2 = self.PS[2]
        q0, cosq, sinq = self._q0, self._cosq, self._sinq
        slab = self.V(L.wq0 + 16384 * (dcg % 2), 16 * 512, BF16)
        if dcg >= 2:
            P.w("gpsimd", "PE_qmm", 4 * (dcg - 1))
        P.o("gpsimd", lambda E, dcg=dcg, slab=slab: E.dma_start(
            out=slab.rearrange("p (h d) -> p h d", d=512),
            in_=self.wq.rearrange("(h p) d -> p h d", p=128)[:, :, 512 * dcg:512 * (dcg + 1)]),
            inc="L_wq", inc_n=16)
        P.w("tensor", "L_wq", 16 * (dcg + 1))
        for dc4 in range(4):
            dc = 4 * dcg + dc4
            PSq = self.PS[dc % 2]
            if dc >= 2:
                P.w("tensor", "D_q0", dc - 1)
            for hc in range(16):
                for th in range(2):
                    P.o("tensor", lambda E, hc=hc, th=th, slab=slab, dc4=dc4, PSq=PSq: E.matmul(
                        PSq[:, 512 * th:512 * (th + 1)],
                        slab.rearrange("p (h d) -> p h d", d=512)[:, hc, 128 * dc4:128 * (dc4 + 1)],
                        self.xnt_own.rearrange("p (hc t) -> p hc t", t=1024)[:, hc, 512 * th:512 * (th + 1)],
                        start=(hc == 0), stop=(hc == 15)),
                        inc=("PE_qmm" if hc == 15 and th == 1 else None))
            P.w("scalar", "PE_qmm")
            if dc >= 1:
                P.w("scalar", "D_qr", dc)
            P.o("scalar", lambda E, PSq=PSq: E.activation(q0, PSq[:, :], AF.Copy), inc="D_q0")
            P.w("tensor", "D_q0")
            if dc >= 1:
                P.w("tensor", "D_qr", dc)
            for th in range(2):
                P.o("tensor", lambda E, th=th: E.matmul(
                    PS2[:, 512 * th:512 * (th + 1)], self.rotT,
                    q0[:, 512 * th:512 * (th + 1)], start=True, stop=True),
                    inc=("PE_qrk" if th == 1 else None))
            P.w("vector", "PE_qrk")
            u1 = self.V(L.r6 + 10240, 512, F32)
            for th in range(2):
                P.o("vector", lambda E, th=th, dc=dc: E.tensor_tensor(
                    u1, q0[:, 512 * th:512 * (th + 1)],
                    cosq[:, 512 * th:512 * (th + 1)], ALU.mult))
                P.o("vector", lambda E, th=th, dc=dc: E.tensor_tensor(
                    self.QT[:, 1024 * dc + 512 * th:1024 * dc + 512 * (th + 1)],
                    PS2[:, 512 * th:512 * (th + 1)],
                    sinq[:, 512 * th:512 * (th + 1)], ALU.mult))
                P.o("vector", lambda E, th=th, dc=dc: E.tensor_add(
                    self.QT[:, 1024 * dc + 512 * th:1024 * dc + 512 * (th + 1)],
                    self.QT[:, 1024 * dc + 512 * th:1024 * dc + 512 * (th + 1)],
                    u1), inc=("D_qr" if th == 1 else None))

    def emit_attention(self):
        P, L = self.P, self.lay
        PS2 = self.PS[2]
        PS3 = self.PS[3]
        amask = self.amask
        P.w("sync", "PE_qmm")
        P.o("sync", lambda E: E.dma_start(out=amask, in_=self.amaskD),
            inc="L_am", inc_n=16)
        P.w("vector", "L_am")
        P.w("tensor", "D_qr")
        P.w("tensor", "D_q0")
        P.w("tensor", "A_stg")
        self.Ggl = 0
        self.Hc = 0
        self.xm_r = 0
        self.m_cnt = 0
        for pm in range(4):
            for head in range(16):
                self.emit_attn_head(pm, head)
            for qc in range(2):
                self.emit_xmid_ln2(pm, qc)

    def emit_attn_head(self, pm, head):
        P, L = self.P, self.lay
        PS2 = self.PS[2]
        hp = head % 2
        PSh = PS2 if hp == 0 else self.PS[3]
        pv = PSh[:, 0:256]
        sm = PSh[:, 512:768]
        ng = 2 * pm + 2
        Hc = self.Hc
        self.Hc += 1
        qs = 256 * pm
        last_jc = 8 * pm + 7
        P.w("tensor", "D_att", max(0, Hc - 1))
        if hp == 1 and pm >= 1:
            P.w("tensor", "D_x2e", 8 * pm)

        def emit_S(gg, Ggl, slot):
            Spar = self.PS[Ggl % 2]
            P.w("tensor", "A_exp", max(0, Ggl - 1))
            for k in range(4):
                jc = 4 * gg + k
                P.o("tensor", lambda E, jc=jc, k=k, Spar=Spar: E.matmul(
                    Spar[:, 256 * k:256 * (k + 1)],
                    self.KT[:, 128 * jc:128 * (jc + 1)],
                    self.QT[:, 1024 * head + qs:1024 * head + qs + 256],
                    start=True, stop=True), inc=("PE_S" if k == 3 else None))
            P.w("scalar", "PE_S")
            P.w("scalar", "PE_ptf", max(0, Ggl - 2))
            P.o("scalar", lambda E, slot=slot, Spar=Spar: E.activation(
                slot, Spar[:, :], AF.Exp, scale=ISQ, bias=self.cbsh), inc="A_exp")
            zone = gg >= ng - 2
            if zone:
                zi = gg - (ng - 2)
                P.w("vector", "A_exp")
                P.o("vector", lambda E, slot=slot, zi=zi: E.tensor_tensor(
                    slot, slot, self.amask[:, 1024 * zi:1024 * (zi + 1)], ALU.mult),
                    inc="D_msk")
            return zone

        def emit_PV(gg, zone, slot):
            if zone:
                P.w("tensor", "D_msk")
            else:
                P.w("tensor", "A_exp", self.exp_at[gg])
            for k in range(4):
                jc = 4 * gg + k
                P.o("tensor", lambda E, jc=jc, k=k, slot=slot: E.matmul(
                    pv, self.Vsb[:, 128 * jc:128 * (jc + 1)],
                    slot[:, 256 * k:256 * (k + 1)],
                    start=(jc == 0), stop=(jc == last_jc)))
                P.o("tensor", lambda E, jc=jc, k=k, slot=slot: E.matmul(
                    sm, self.ones, slot[:, 256 * k:256 * (k + 1)],
                    start=(jc == 0), stop=(jc == last_jc)),
                    inc=("PE_ptf" if k == 3 else None))

        self.exp_at = {}
        pend = None
        for gg in range(ng):
            Ggl = self.Ggl
            self.Ggl += 1
            slot = self.V(L.PT + 2048 * (Ggl % 3), 1024, BF16)
            zone = emit_S(gg, Ggl, slot)
            self.exp_at[gg] = self.P.sem("A_exp").n
            if pend is not None:
                emit_PV(*pend)
            pend = (gg, zone, slot)
        emit_PV(*pend)
        self._attn_tail(pm, head, pv, sm)


    def _attn_tail(self, pm, head, pv, sm):
        P, L = self.P, self.lay
        recip = self.V(L.recip, 256, F32)
        att = self.V(L.attnT + 8192 * (pm % 2), 16 * 256, BF16)
        P.w("vector", "PE_ptf")
        if "dump" in self.dbg and pm == 0 and head == 0:
            dkt = self.nc.dram_tensor("dbg_kt", [128, 4096], BF16, kind="ExternalOutput").ap()
            dqt = self.nc.dram_tensor("dbg_qt", [128, 16384], BF16, kind="ExternalOutput").ap()
            dvs = self.nc.dram_tensor("dbg_vs", [128, 4096], BF16, kind="ExternalOutput").ap()
            dam = self.nc.dram_tensor("dbg_am", [128, 2048], BF16, kind="ExternalOutput").ap()
            P.o("sync", lambda E: E.dma_start(out=dkt, in_=self.KT), inc="L_dbg3", inc_n=16)
            P.o("sync", lambda E: E.dma_start(out=dqt, in_=self.QT), inc="L_dbg3", inc_n=16)
            P.o("sync", lambda E: E.dma_start(out=dvs, in_=self.Vsb), inc="L_dbg3", inc_n=16)
            P.o("sync", lambda E: E.dma_start(out=dam, in_=self.amask), inc="L_dbg3", inc_n=16)
        if "dump" in self.dbg and pm == 1 and head == 0:
            pvd = self.V(L.r6 + 18432, 256, F32)
            smd = self.V(L.r6 + 19456, 256, F32)
            P.o("vector", lambda E: E.tensor_copy(pvd, pv))
            P.o("vector", lambda E: E.tensor_copy(smd, sm), inc="D_dbgpv")
            P.w("sync", "D_dbgpv")
            dpv = self.nc.dram_tensor("dbg_pv", [128, 512], F32, kind="ExternalOutput").ap()
            P.o("sync", lambda E, dpv=dpv: E.dma_start(out=dpv[:, 0:256], in_=pvd), inc="L_dbg2", inc_n=16)
            P.o("sync", lambda E, dpv=dpv: E.dma_start(out=dpv[:, 256:512], in_=smd), inc="L_dbg2", inc_n=16)
        P.o("vector", lambda E, sm=sm: E.reciprocal(recip, sm))
        P.o("vector", lambda E, pv=pv, head=head, att=att: E.tensor_tensor(
            att[:, 256 * head:256 * (head + 1)], pv, recip, ALU.mult), inc="D_att")

    def emit_xmid_ln2(self, pm, qc):
        P, L = self.P, self.lay
        PS3 = self.PS[3]
        m = 2 * pm + qc
        xmid = self.V(L.xmid, 2048, F32)
        att = self.V(L.attnT + 8192 * (pm % 2), 16 * 256, BF16)
        xn2b = self.V(L.r6 + 14336, 2048, BF16)
        if m >= 1:
            P.w("sync", "A_xn", P.sem("A_xn").n)
            P.w("sync", "L_xmD", 16 * m)
        g = 4 * m + 3
        P.o("sync", lambda E, g=g, xmid=xmid: E.dma_start(
            out=xmid, in_=self.xseq[128 * g:128 * (g + 1), :]), inc="L_xo", inc_n=16)
        P.w("tensor", "D_att")
        for r in range(4):
            xr = self.xm_r
            self.xm_r += 1
            P.w("tensor", "D_xm", max(0, xr))
            if r == 0 and m >= 1:
                P.w("tensor", "D_x2e", 4 * m)
            for i in range(4):
                head = 4 * r + i
                P.o("tensor", lambda E, head=head, i=i, att=att, qc=qc: E.matmul(
                    PS3[:, 128 * i:128 * (i + 1)],
                    att[:, 256 * head + 128 * qc:256 * head + 128 * (qc + 1)],
                    self.ident, start=True, stop=True),
                    inc=("PE_xm" if i == 3 else None))
            P.w("vector", "PE_xm")
            if r == 0:
                P.w("vector", "L_xo", 16 * (m + 1))
            P.o("vector", lambda E, r=r, xmid=xmid: E.scalar_tensor_tensor(
                xmid[:, 512 * r:512 * (r + 1)], PS3[:, 0:512], 1.0,
                xmid[:, 512 * r:512 * (r + 1)], ALU.mult, ALU.add), inc="D_xm")
        P.w("sync", "D_xm", P.sem("D_xm").n)
        P.o("sync", lambda E, m=m, xmid=xmid: E.dma_start(
            out=self.xmidD.ap()[128 * m:128 * (m + 1), :], in_=xmid),
            inc="L_xmD", inc_n=16)
        wb = [("vector", P.sem("D_xm"), P.sem("D_xm").n),
              ("scalar", P.sem("D_xm"), P.sem("D_xm").n)]
        if m >= 1:
            wb.append(("vector", P.sem("PE_x2t"), 4 * m))
            wb.append(("scalar", P.sem("PE_x2t"), 4 * m))
        self.emit_ln(xmid, xn2b, m, wait_before=wb, lnt_off=L.r6 + 12288 + 1024 * (m % 2))
        P.w("tensor", "A_xn")
        dst = self.xn2T.rearrange("p (hc t) -> p hc t", t=1024)
        for r in range(4):
            P.w("tensor", "D_x2e", 4 * m + r)
            for i in range(4):
                hc = 4 * r + i
                P.o("tensor", lambda E, hc=hc, i=i, xn2b=xn2b: E.matmul(
                    PS3[:, 128 * i:128 * (i + 1)],
                    xn2b[:, 128 * hc:128 * (hc + 1)], self.ident,
                    start=True, stop=True), inc=("PE_x2t" if i == 3 else None))
            P.w("vector", "PE_x2t")
            P.o("vector", lambda E, r=r, m=m, dst=dst: E.tensor_copy(
                dst[:, 4 * r:4 * (r + 1), 128 * m:128 * (m + 1)],
                PS3[:, 0:512].rearrange("p (hc t) -> p hc t", t=128)), inc="D_x2e")

    def emit_gate(self):
        P, L = self.P, self.lay
        for eng in ("sync", "vector", "scalar", "tensor"):
            for sem in ("PE_dqT", "D_dec", "P_dec", "L_stg", "L_pk"):
                P.w(eng, sem)
        for sem in ("A_exp", "D_att", "D_x2e", "A_stg", "D_qr"):
            P.w("tensor", sem)
        for sem in ("PE_ptf", "PE_xm", "D_att", "D_x2e", "D_msk", "A_xn"):
            P.w("scalar", sem)
        self.gn = 0
        srr = 0
        for sl in range(4):
            P.w("sync", f"CC_g{sl}", 1)
            P.w("sync", f"CC_u{sl}", 1)
            P.w("scalar", f"CC_g{sl}", 1)
            P.w("scalar", f"CC_u{sl}", 1)
            for rr in range(8):
                self.emit_gate_srr(sl, rr, srr)
                srr += 1

    def emit_gate_srr(self, sl, rr, srr):
        P, L = self.P, self.lay
        par = srr % 2
        wg = self.V(L.gslab if par == 0 else L.KT, 16 * 512, BF16)
        wu = self.V((L.gslab + 16384) if par == 0 else L.r6, 16 * 512, BF16)
        gelu = self.V(L.gelu, 1024, F32)
        if srr >= 2:
            P.w("scalar", "PE_gmm", 4 * (srr - 1))
        for (wsl, gath) in ((wg, self.gath_g[sl]), (wu, self.gath_u[sl])):
            P.o("scalar", lambda E, wsl=wsl, gath=gath, rr=rr: E.dma_start(
                out=wsl.rearrange("p (h n) -> p h n", n=512),
                in_=gath.ap().rearrange("(r h p) n -> r p h n", p=128, h=16)[rr]),
                inc="L_gs", inc_n=16)
        P.w("tensor", "L_gs", 32 * (srr + 1))
        for h2 in range(2):
            for nc2 in range(2):
                gn = self.gn
                self.gn += 1
                psG = self.PS[2 * (gn % 2)]
                psU = self.PS[2 * (gn % 2) + 1]
                if gn >= 2:
                    P.w("tensor", "A_gelu", gn - 1)
                    P.w("tensor", "D_int", gn - 1)
                nco = 256 * h2 + 128 * nc2
                for wsl, ps in ((wg, psG), (wu, psU)):
                    for hc in range(16):
                        for th in range(2):
                            lastg = wsl is wu and hc == 15 and th == 1
                            P.o("tensor", lambda E, wsl=wsl, ps=ps, hc=hc, th=th, nco=nco: E.matmul(
                                ps[:, 512 * th:512 * (th + 1)],
                                wsl.rearrange("p (h n) -> p h n", n=512)[:, hc, nco:nco + 128],
                                self.xn2T.rearrange("p (hc t) -> p hc t", t=1024)[:, hc, 512 * th:512 * (th + 1)],
                                start=(hc == 0), stop=(hc == 15)),
                                inc=("PE_gmm" if lastg else None))
                P.w("scalar", "PE_gmm")
                if gn >= 1:
                    P.w("scalar", "D_int", gn)
                P.o("scalar", lambda E, psG=psG, gelu=gelu: E.activation(
                    gelu, psG[:, :], AF.Gelu_apprx_tanh), inc="A_gelu")
                ib = self.V(L.inter + 2048 * (gn % 2), 1024, BF16)
                P.w("vector", "A_gelu")
                if gn >= 2:
                    P.w("vector", "L_int", 16 * (gn - 1))
                P.o("vector", lambda E, ib=ib, gelu=gelu, psU=psU: E.tensor_tensor(
                    ib, gelu, psU[:, :], ALU.mult), inc="D_int")
                nrow = 2048 * rr + 512 * sl + 128 * (2 * h2 + nc2)
                P.w("sync", "D_int")
                P.o("sync", lambda E, ib=ib, nrow=nrow: E.dma_start(
                    out=self.interT.ap()[nrow:nrow + 128, :], in_=ib),
                    inc="L_int", inc_n=16)

    def emit_down(self):
        P, L = self.P, self.lay
        for sem in ("A_gelu", "D_int"):
            P.w("tensor", sem)
        for sem in ("PE_ptf", "PE_xm", "A_xn", "D_att", "D_msk", "L_int", "PE_gmm",
                    "L_xmD", "L_xo", "L_am", "L_cq"):
            P.w("sync", sem)
        acc = [self.V((L.acc0 if t < 4 else L.acc1) + 8192 * (t % 4), 2048, F32)
               for t in range(8)]
        for t in range(8):
            P.o("sync", lambda E, t=t, a=acc[t]: E.dma_start(
                out=a, in_=self.xmidD.ap()[128 * t:128 * (t + 1), :]),
                inc="L_seed", inc_n=16)
        P.w("vector", "L_seed")
        P.w("tensor", "L_seed")
        gidx = 0
        gt = 0
        for i_d in range(4):
            P.w("sync", f"CC_d{i_d}", 1)
            P.w("scalar", f"CC_d{i_d}", 1)
            for rr in range(8):
                par = gidx % 2
                wdt = [self.V((L.dn_wdt if par == 0 else L.r6) + 4096 * k, 2048, BF16)
                       for k in range(4)]
                itt = [self.V(L.xn2T + 8192 * par + 2048 * k, 1024, BF16)
                       for k in range(4)]
                if gidx >= 2:
                    P.w("scalar", "PE_dn", 8 * (gidx - 1))
                    P.w("sync", "PE_dn", 8 * (gidx - 1))
                for k in range(4):
                    P.o("scalar", lambda E, k=k, i_d=i_d, rr=rr, w=wdt[k]: E.dma_start(
                        out=w, in_=self.gath_d[i_d].ap().rearrange(
                            "(r k p) h -> r k p h", k=4, p=128)[rr, k]),
                        inc="L_dnw", inc_n=16)
                    P.o("sync", lambda E, k=k, i_d=i_d, rr=rr, it=itt[k]: E.dma_start(
                        out=it, in_=self.interT.ap().rearrange(
                            "(r s k p) t -> r s k p t", s=4, k=4, p=128)[rr, i_d, k]),
                        inc="L_dn", inc_n=16)
                P.w("tensor", "L_dn", 64 * (gidx + 1))
                P.w("tensor", "L_dnw", 64 * (gidx + 1))
                for t in range(8):
                    pa = self.PS[2 * (t % 2)]
                    pb = self.PS[2 * (t % 2) + 1]
                    if gt >= 2:
                        P.w("tensor", "D_acc", gt - 1)
                    for k in range(4):
                        for hr in range(4):
                            ps = (pa, pa, pb, pb)[hr][:, 512 * (hr % 2):512 * (hr % 2 + 1)]
                            P.o("tensor", lambda E, k=k, hr=hr, t=t, ps=ps, w=wdt[k], it=itt[k]: E.matmul(
                                ps, it[:, 128 * t:128 * (t + 1)],
                                w[:, 512 * hr:512 * (hr + 1)],
                                start=(k == 0), stop=(k == 3)),
                                inc=("PE_dn" if k == 3 and hr == 3 else None))
                    P.w("vector", "PE_dn")
                    P.o("vector", lambda E, pa=pa, t=t, a=acc[t]: E.scalar_tensor_tensor(
                        a[:, 0:1024], pa[:, :], 1.0, a[:, 0:1024], ALU.mult, ALU.add))
                    P.o("vector", lambda E, pb=pb, t=t, a=acc[t]: E.scalar_tensor_tensor(
                        a[:, 1024:2048], pb[:, :], 1.0, a[:, 1024:2048], ALU.mult, ALU.add),
                        inc="D_acc")
                    if i_d == 3 and rr == 7:
                        P.w("sync", "D_acc")
                        P.o("sync", lambda E, t=t, a=acc[t]: E.dma_start(
                            out=self.out[128 * t:128 * (t + 1), :], in_=a),
                            inc="L_out", inc_n=16)
                    gt += 1
                gidx += 1
        P.w("sync", "L_out", 16 * 8)

    def schedule(self):
        self.deq_cnt = 0
        self.ln_cnt = 0
        self.agw = {}
        self.boot()
        sweeponly = ("sweeponly" in self.dbg) or ("swdeq" in self.dbg)
        nodeq = ("sweeponly" in self.dbg) or ("nodeq" in self.dbg)
        dorder = []
        for blk in range(4):
            dorder += [("g", 4 * blk + i) for i in range(4)]
            dorder += [("u", 4 * blk + i) for i in range(4)]
        for g in range(SEQ_CH):
            self.emit_sweep_chunk(g)
            if not nodeq:
                mat, ti = dorder[g]
                self.emit_deq_tile(mat, ti)
                if g % 4 == 3:
                    mat, ti = dorder[g]
                    sl = ti // 4
                    self.agw[(mat, sl)] = self.P.sem("L_stg").n
                    self.emit_ag(mat, sl)
        if sweeponly:
            P = self.P
            dkt = self.nc.dram_tensor("dbg_kt", [128, 4096], BF16, kind="ExternalOutput").ap()
            dvs = self.nc.dram_tensor("dbg_vs", [128, 4096], BF16, kind="ExternalOutput").ap()
            dxo = self.nc.dram_tensor("dbg_xo", [128, 16384], BF16, kind="ExternalOutput").ap()
            P.w("sync", "D_kv2")
            P.w("sync", "D_xnte")
            P.o("sync", lambda E: E.dma_start(out=dkt, in_=self.KT), inc="L_dbg4", inc_n=16)
            P.o("sync", lambda E: E.dma_start(out=dvs, in_=self.Vsb), inc="L_dbg4", inc_n=16)
            P.o("sync", lambda E: E.dma_start(out=dxo, in_=self.xnt_own), inc="L_dbg4", inc_n=16)
            P.w("sync", "L_dbg4", 48)
            self.P.emit()
            return
        self.emit_q_pre()
        for dcg in range(4):
            self.emit_q_group(dcg)
            if not nodeq:
                for i in range(4):
                    self.emit_deq_tile("d", 4 * dcg + i)
        if not nodeq:
            for i in range(4):
                self.agw[("d", i)] = self.P.sem("L_stg").n
                self.emit_ag("d", i)
        if "noattn" in self.dbg:
            P = self.P
            dkt = self.nc.dram_tensor("dbg_kt", [128, 4096], BF16, kind="ExternalOutput").ap()
            dvs = self.nc.dram_tensor("dbg_vs", [128, 4096], BF16, kind="ExternalOutput").ap()
            dqt = self.nc.dram_tensor("dbg_qt", [128, 16384], BF16, kind="ExternalOutput").ap()
            for sem in ("D_qr", "D_dec", "P_dec", "A_stg"):
                P.w("sync", sem)
            P.o("sync", lambda E: E.dma_start(out=dkt, in_=self.KT), inc="L_dbg4", inc_n=16)
            P.o("sync", lambda E: E.dma_start(out=dvs, in_=self.Vsb), inc="L_dbg4", inc_n=16)
            P.o("sync", lambda E: E.dma_start(out=dqt, in_=self.QT), inc="L_dbg4", inc_n=16)
            dar = self.nc.dram_tensor("dbg_arena", [128, self.ar.size], U8, kind="ExternalOutput").ap()
            P.o("sync", lambda E: E.dma_start(out=dar, in_=self.ar.h[0:128, :]), inc="L_dbg4", inc_n=16)
            P.w("sync", "L_dbg4", 64)
            if not nodeq:
                for i in range(4):
                    P.w("gpsimd", f"CC_d{i}", 1)
            self.P.emit()
            return
        self.emit_attention()
        self.emit_gate()
        self.emit_down()
        if "dump" in self.dbg:
            P = self.P
            dxm = self.nc.dram_tensor("dbg_xmid", [TPC, EMB], F32, kind="ExternalOutput").ap()
            dcg = self.nc.dram_tensor("dbg_cg", [EMB, 512], BF16, kind="ExternalOutput").ap()
            dgd = self.nc.dram_tensor("dbg_gd", [512, EMB], BF16, kind="ExternalOutput").ap()
            dint = self.nc.dram_tensor("dbg_int", [DFF, TPC], BF16, kind="ExternalOutput").ap()
            dx2 = self.nc.dram_tensor("dbg_xn2t", [128, 16 * TPC], BF16, kind="ExternalOutput").ap()
            P.w("sync", "L_out", 16 * 8)
            P.o("sync", lambda E: E.dma_start(out=dxm, in_=self.xmidD.ap()), inc="L_dbg", inc_n=16)
            P.o("sync", lambda E: E.dma_start(out=dcg, in_=self.contrib_g[0].ap()), inc="L_dbg", inc_n=16)
            P.o("sync", lambda E: E.dma_start(out=dgd, in_=self.gath_d[0].ap()[0:512, :]), inc="L_dbg", inc_n=16)
            P.o("sync", lambda E: E.dma_start(out=dint, in_=self.interT.ap()), inc="L_dbg", inc_n=16)
            P.o("sync", lambda E: E.dma_start(out=dx2, in_=self.xn2T), inc="L_dbg", inc_n=16)
            P.w("sync", "L_dbg", 80)
        self.P.emit()


def build_nc(dbg=()):
    nc = bass.Bass("TRN2", target_bir_lowering=False)
    b = Build(nc, dbg=dbg)
    b.schedule()
    return nc


def kernel(**inputs):
    maps = make_in_maps(inputs)
    nc = build_nc()
    res = run_bass_kernel_spmd(nc, maps, core_ids=list(range(NC)))
    return assemble_output(res.results)


# version 07
# speedup vs baseline: 1.0858x; 1.0441x over previous
import math
import numpy as np
import ml_dtypes
import concourse.bass as bass
import concourse.mybir as mybir
from concourse.bass_utils import run_bass_kernel_spmd

F32 = mybir.dt.float32
BF16 = mybir.dt.bfloat16
I32 = mybir.dt.int32
U8 = mybir.dt.uint8
AF = mybir.ActivationFunctionType
ALU = mybir.AluOpType
AX = mybir.AxisListType

B, L, EMB, HD, NH, DFF = 2, 4096, 2048, 128, 16, 16384
NC = 8
SEQ_CH = L // 128
OWN_CH = 8
TPC = OWN_CH * 128
NPC = DFF // NC
ISQ = 1.0 / math.sqrt(128.0)
CSHIFT = 10.0
EPS = 1e-5

BF = ml_dtypes.bfloat16


def _bf(x):
    return np.ascontiguousarray(x.astype(BF))



def host_constants():
    ident = np.eye(128, dtype=np.float32)
    rt = np.zeros((128, 128), dtype=np.float32)
    for i in range(64):
        rt[2 * i + 1, 2 * i] = -1.0
        rt[2 * i, 2 * i + 1] = 1.0
    ones = np.ones((128, 128), dtype=np.float32)
    tri = (np.arange(128)[None, :] >= np.arange(128)[:, None]).astype(np.float32)
    inv = 1.0 / (10000.0 ** (np.arange(0, HD, 2, dtype=np.float32) / HD))
    t = np.arange(L, dtype=np.float32)
    f = np.outer(t, inv)
    f = np.repeat(f, 2, axis=-1)
    cosT = np.cos(f).astype(np.float32).T.copy()
    sinT = np.sin(f).astype(np.float32).T.copy()
    return ident, rt, ones, tri, cosT, sinT


def chunk_perm(j):
    perm = []
    for m in range(SEQ_CH // 4):
        others = [4 * m + t for t in range(4) if t != j]
        perm.extend(others + [4 * m + j])
    return np.array(perm, dtype=np.int64)


def build_amask(j, tri):
    ones = np.ones((128, 128), dtype=np.float32)
    zeros = np.zeros((128, 128), dtype=np.float32)

    def A(s):
        if s == 3:
            return tri
        return ones if s < j else zeros

    slots = []
    for sp in range(8):
        if sp < 4:
            m = np.concatenate([A(sp), ones], axis=1)
        else:
            m = np.concatenate([zeros, A(sp - 4)], axis=1)
        slots.append(m)
    return np.concatenate(slots, axis=1)


def make_in_maps(inputs):
    x = np.asarray(inputs["L0_attn_x"])
    wq = np.asarray(inputs["L0_attn_Wq"])
    wk = np.asarray(inputs["L0_attn_Wk"])
    wv = np.asarray(inputs["L0_attn_Wv"])
    gpk = np.asarray(inputs["L0_ffn_gate_W_packed"])
    gsc = np.asarray(inputs["L0_ffn_gate_scales"])
    upk = np.asarray(inputs["L0_ffn_up_W_packed"])
    usc = np.asarray(inputs["L0_ffn_up_scales"])
    dpk = np.asarray(inputs["L0_ffn_down_W_packed"])
    dsc = np.asarray(inputs["L0_ffn_down_scales"])
    ident, rt, ones, tri, cosT, sinT = host_constants()

    maps = []
    for c in range(NC):
        s, j = c // 4, c % 4
        perm = chunk_perm(j)
        xs = x[s].reshape(SEQ_CH, 128, EMB)[perm].reshape(L, EMB)
        ct = cosT.reshape(128, SEQ_CH, 128)[:, perm].reshape(128, L)
        st = sinT.reshape(128, SEQ_CH, 128)[:, perm].reshape(128, L)
        own = [4 * m + j for m in range(OWN_CH)]
        cq = cosT.reshape(128, SEQ_CH, 128)[:, own].reshape(128, TPC)
        sq = sinT.reshape(128, SEQ_CH, 128)[:, own].reshape(128, TPC)
        m = {
            "xseq": np.ascontiguousarray(xs),
            "wq": np.ascontiguousarray(wq),
            "wk": np.ascontiguousarray(wk),
            "wv": np.ascontiguousarray(wv),
            "gpk": np.ascontiguousarray(gpk[NPC * c:NPC * (c + 1)]),
            "gsc": np.ascontiguousarray(gsc[NPC * c:NPC * (c + 1)]),
            "upk": np.ascontiguousarray(upk[NPC * c:NPC * (c + 1)]),
            "usc": np.ascontiguousarray(usc[NPC * c:NPC * (c + 1)]),
            "dpk": np.ascontiguousarray(dpk[:, 1024 * c:1024 * (c + 1)]),
            "dsc": np.ascontiguousarray(dsc[:, 64 * c:64 * (c + 1)]),
            "cost": np.ascontiguousarray(ct),
            "sint": np.ascontiguousarray(st),
            "cosq": np.ascontiguousarray(cq),
            "sinq": np.ascontiguousarray(sq),
            "ident": _bf(ident),
            "rotT": _bf(rt),
            "ones": _bf(ones),
            "amask": _bf(build_amask(j, tri)),
        }
        maps.append(m)
    return maps


def assemble_output(results):
    out = np.zeros((B, L, EMB), dtype=np.float32)
    for c in range(NC):
        s, j = c // 4, c % 4
        shard = results[c]["out"].reshape(OWN_CH, 128, EMB)
        for m in range(OWN_CH):
            g = 4 * m + j
            out[s, 128 * g:128 * (g + 1)] = shard[m]
    return out



ENGINES = ("sync", "tensor", "vector", "scalar", "gpsimd")


class Sem:
    __slots__ = ("h", "n", "virtual", "eng", "counts")

    def __init__(self, nc, name):
        self.h = nc.alloc_semaphore(name)
        self.n = 0
        self.virtual = False


class VSem:
    __slots__ = ("n", "eng", "counts", "virtual")

    def __init__(self):
        self.n = 0
        self.eng = None
        self.counts = []
        self.virtual = True


CHAINED = ("vector", "scalar")


class Prog:
    def __init__(self, nc):
        self.nc = nc
        self.st = {e: [] for e in ENGINES}
        self.sems = {}
        self.chain = {e: Sem(nc, f"CH_{e}") for e in CHAINED}

    def sem(self, name):
        if name not in self.sems:
            self.sems[name] = VSem()
        return self.sems[name]

    def _realize(self, name):
        v = self.sems.get(name)
        if v is None or v.virtual and v.eng is None and not v.counts and v.n == 0:
            pass
        return v

    def o(self, eng, fn, inc=None, inc_n=1):
        chained = eng in CHAINED and inc_n == 1
        if chained:
            ch = self.chain[eng]
            prior = ch.n
            ch.n += 1
            if inc is not None:
                vs = self.sem(inc)
                if isinstance(vs, Sem):
                    raise RuntimeError(f"sem {inc} already real, inc from {eng}")
                if vs.eng is None:
                    vs.eng = eng
                assert vs.eng == eng, (inc, vs.eng, eng)
                vs.n += 1
                vs.counts.append(ch.n)

            def wrapped(E, fn=fn, h=ch.h, prior=prior):
                if prior:
                    E.wait_ge(h, prior)
                fn(E).then_inc(h, 1)
            self.st[eng].append(wrapped)
        else:
            if inc is not None:
                vs = self.sems.get(inc)
                if vs is None or isinstance(vs, VSem):
                    if vs is not None and (vs.counts or vs.eng):
                        raise RuntimeError(f"sem {inc} already virtual")
                    self.sems[inc] = Sem(self.nc, inc)
                sem = self.sems[inc]
                prior = sem.n
                sem.n += inc_n
                if inc_n == 16 and prior:
                    self.st[eng].append(lambda E, h=sem.h, v=prior: E.wait_ge(h, v))

                def wrapped(E, fn=fn, h=sem.h, k=inc_n):
                    fn(E).then_inc(h, k)
                self.st[eng].append(wrapped)
            else:
                self.st[eng].append(fn)

    def w(self, eng, sem, val=None):
        if isinstance(sem, str):
            sem = self.sem(sem)
        v = sem.n if val is None else val
        if v <= 0:
            return
        if getattr(sem, "virtual", False):
            assert sem.eng is not None, "wait on never-incremented virtual sem"
            assert v <= len(sem.counts), (v, len(sem.counts))
            h = self.chain[sem.eng].h
            tgt = sem.counts[v - 1]
            self.st[eng].append(lambda E, h=h, t=tgt: E.wait_ge(h, t))
        else:
            self.st[eng].append(lambda E, h=sem.h, v=v: E.wait_ge(h, v))

    def emit(self):
        with self.nc.Block() as blk:
            for e in ENGINES:
                ops = self.st[e]
                if not ops:
                    continue

                def runner(E, ops=ops):
                    for f in ops:
                        f(E)
                getattr(blk, e)(runner)


ESZ = {F32: 4, BF16: 2, I32: 4, U8: 1}


class Arena:

    def __init__(self, nc, size):
        self.h = nc.alloc_sbuf_tensor("arena", [128, size], U8)
        self.size = size

    def view(self, off, n_elems, dtype, p=128):
        nb = n_elems * ESZ[dtype]
        assert off % 64 == 0 and off + nb <= self.size, (off, nb, self.size)
        return self.h[0:p, off:off + nb].bitcast(dtype)


class Layout:

    def __init__(self):
        o = 0

        def take(nb):
            nonlocal o
            off = (o + 63) & ~63
            o = off + nb
            return off

        self.cbank = take(64)
        self.ident = take(256)
        self.rotT = take(256)
        self.ones = take(256)
        self.wkb = take(4096)
        self.wvb = take(4096)
        self.PT = self.wkb
        self.KT = take(8192)
        self.Vsb = take(8192)
        self.dn_wdt = self.KT
        self.dn_int = None
        r2 = take(32768)
        self.QT = r2
        self.acc0 = r2
        s = r2
        self.xring = s; s += 16384
        self.xnt_sweep = s; s += 8192
        self.csring = s; s += 2048
        self.k0 = s; s += 512
        self.vt0 = s; s += 512
        self.u1 = s; s += 512
        self.u2 = s; s += 512
        self.lnt = s; s += 2048
        assert s - r2 <= 32768, s - r2
        r3 = take(32768)
        self.xnt_own = r3
        self.acc1 = r3
        s = r3
        self.attnT = s; s += 16384
        self.recip = s; s += 1024
        self.xmid = s; s += 8192
        self.amask = s; s += 4096
        assert s - r3 <= 32768, s - r3
        r4 = take(32768)
        self.xn2T = r4
        self.wq0 = r4
        self.wq1 = r4 + 16384
        self.accq = None
        r5 = take(45056)
        s = r5
        self.pk = s; s += 8192
        self.scl = s; s += 1024
        self.dq = s; s += 26624
        self.wdec = s; s += 4096
        self.wstg = s; s += 4096
        assert s - r5 <= 45056, s - r5
        self.gslab = r5
        self.gelu = r5 + 32768
        self.inter = r5 + 32768 + 4096
        r6 = take(24576)
        self.r6 = r6
        self.total = (o + 63) & ~63



class Build:
    def __init__(self, nc, dbg=()):
        self.nc = nc
        self.dbg = set(dbg)
        self.P = Prog(nc)
        self.lay = Layout()
        self.ar = Arena(nc, self.lay.total)
        L = self.lay

        def din(name, shape, dt):
            return nc.dram_tensor(name, shape, dt, kind="ExternalInput").ap()

        self.xseq = din("xseq", [L_ := 4096, EMB], F32)
        self.wq = din("wq", [EMB, EMB], F32)
        self.wk = din("wk", [EMB, HD], F32)
        self.wv = din("wv", [EMB, HD], F32)
        self.gpk = din("gpk", [NPC, 1024], I32)
        self.gsc = din("gsc", [NPC, 64], F32)
        self.upk = din("upk", [NPC, 1024], I32)
        self.usc = din("usc", [NPC, 64], F32)
        self.dpk = din("dpk", [EMB, 1024], I32)
        self.dsc = din("dsc", [EMB, 64], F32)
        self.cost = din("cost", [128, 4096], F32)
        self.sint = din("sint", [128, 4096], F32)
        self.cosq = din("cosq", [128, TPC], F32)
        self.sinq = din("sinq", [128, TPC], F32)
        self.identD = din("ident", [128, 128], BF16)
        self.rotD = din("rotT", [128, 128], BF16)
        self.onesD = din("ones", [128, 128], BF16)
        self.amaskD = din("amask", [128, 2048], BF16)
        self.out = nc.dram_tensor("out", [TPC, EMB], F32, kind="ExternalOutput").ap()

        self.contrib_g = [nc.dram_tensor(f"cg{s}", [EMB, 512], BF16) for s in range(4)]
        self.contrib_u = [nc.dram_tensor(f"cu{s}", [EMB, 512], BF16) for s in range(4)]
        self.contrib_d = nc.dram_tensor("cd", [NPC, EMB], BF16)
        self.gath_g = [nc.dram_tensor(f"gg{s}", [DFF, 512], BF16, addr_space="Shared") for s in range(4)]
        self.gath_u = [nc.dram_tensor(f"gu{s}", [DFF, 512], BF16, addr_space="Shared") for s in range(4)]
        self.gath_d = [nc.dram_tensor(f"gd{i}", [4096, EMB], BF16, addr_space="Shared") for i in range(4)]
        self.interT = nc.dram_tensor("interT", [DFF, TPC], BF16)
        self.xmidD = nc.dram_tensor("xmidD", [TPC, EMB], F32)

        self.PS = [nc.alloc_psum_tensor(f"ps{i}", [128, 1024], F32) for i in range(4)]

        a = self.ar
        self.ident = a.view(L.ident, 128 * 128 // 128, BF16)
        self.rotT = a.view(L.rotT, 128, BF16)
        self.ones = a.view(L.ones, 128, BF16)
        self.wkb = a.view(L.wkb, 16 * 128, BF16)
        self.wvb = a.view(L.wvb, 16 * 128, BF16)
        self.KT = a.view(L.KT, 4096, BF16)
        self.Vsb = a.view(L.Vsb, 4096, BF16)
        self.QT = a.view(L.QT, 16 * 1024, BF16)
        self.xn2T = a.view(L.xn2T, 16 * 1024, BF16)
        self.xnt_own = a.view(L.xnt_own, 16 * 1024, BF16)
        self.amask = a.view(L.amask, 2048, BF16)
        cb = a.view(L.cbank, 16, F32)
        self.cb75 = cb[:, 0:1]
        self.cbm6 = cb[:, 1:2]
        self.cbeps = cb[:, 2:3]
        self.cbsh = cb[:, 3:4]
        self.cbm4 = cb[:, 4:5]

    def V(self, off, n, dt):
        return self.ar.view(off, n, dt)

    def boot(self):
        P, L = self.P, self.lay
        for src, off in ((self.identD, L.ident), (self.rotD, L.rotT),
                         (self.onesD, L.ones)):
            n = src.shape[1]
            P.o("sync", lambda E, s=src, o=off, n=n: E.dma_start(
                out=self.V(o, n, BF16), in_=s), inc="Lc", inc_n=16)
        for src, off in ((self.wk, L.wkb), (self.wv, L.wvb)):
            P.o("gpsimd", lambda E, s=src, o=off: E.dma_start(
                out=self.V(o, 16 * 128, BF16).rearrange("p (h d) -> p h d", d=128),
                in_=s.rearrange("(h p) d -> p h d", p=128)), inc="Lcg", inc_n=16)
        P.o("vector", lambda E: E.memset(self.cb75, 7.5))
        P.o("vector", lambda E: E.memset(self.cbm6, -6.0))
        P.o("vector", lambda E: E.memset(self.cbeps, EPS))
        P.o("vector", lambda E: E.memset(self.cbm4, -4.0))
        P.o("vector", lambda E: E.memset(self.cbsh, -CSHIFT), inc="D_cb")
        self.P.w("tensor", "Lc")
        self.P.w("tensor", "Lcg")
        self.P.w("vector", "Lc")
        self.P.w("vector", "Lcg")
        self.P.w("scalar", "Lc")
        self.P.w("scalar", "Lcg")
        self.P.w("scalar", "D_cb")
        self.P.w("gpsimd", "Lc")

    def emit_ln(self, xap, xnap, par, wait_before=None, lnt_off=None):
        P, L = self.P, self.lay
        lnt = (L.lnt + 1024 * (par & 1)) if lnt_off is None else lnt_off
        cnt = self.ln_cnt
        self.ln_cnt += 1
        s1 = self.V(lnt, 8, F32)
        sq4 = self.V(lnt + 64, 8, F32)
        mm = self.V(lnt + 128, 8, F32)
        msq = self.V(lnt + 192, 8, F32)
        var = self.V(lnt + 256, 8, F32)
        rsd = self.V(lnt + 320, 8, F32)
        nb = self.V(lnt + 384, 8, F32)
        sqs = self.V(lnt + 448, 8, F32)
        PS3 = self.PS[3]

        if wait_before:
            for eng, sem, v in wait_before:
                P.w(eng, sem, v)
        P.o("vector", lambda E: E.reduce_sum(s1[:, 0:1], xap, AX.X), inc="D_s1")
        if cnt >= 2:
            P.w("scalar", "D_var", cnt - 1)
        P.o("scalar", lambda E: E.activation(xnap, xap, AF.Square), inc="A_sqp")
        P.w("vector", "A_sqp")
        P.o("vector", lambda E: E.reduce_sum(sqs[:, 0:1], xnap, AX.X), inc="D_sqr")
        P.o("vector", lambda E: E.tensor_scalar_mul(mm[:, 0:1], s1[:, 0:1], 1.0 / EMB))
        P.o("vector", lambda E: E.tensor_mul(msq[:, 0:1], mm[:, 0:1], mm[:, 0:1]))
        P.o("vector", lambda E: E.scalar_tensor_tensor(
            var[:, 0:1], sqs[:, 0:1], 1.0 / EMB, msq[:, 0:1],
            ALU.mult, ALU.subtract), inc="D_var")
        P.w("scalar", "D_var")
        if cnt >= 2:
            P.w("scalar", "D_ln", cnt - 1)
        P.o("scalar", lambda E: E.activation(rsd[:, 0:1], var[:, 0:1], AF.Sqrt,
                                             bias=self.cbeps), inc="A_sqrt")
        P.w("vector", "A_sqrt")
        if cnt >= 2:
            P.w("vector", "A_xn", cnt - 1)
        P.o("vector", lambda E: E.reciprocal(rsd[:, 0:1], rsd[:, 0:1]))
        P.o("vector", lambda E: E.scalar_tensor_tensor(
            nb[:, 0:1], mm[:, 0:1], -1.0, rsd[:, 0:1], ALU.mult, ALU.mult),
            inc="D_ln")
        P.w("scalar", "D_ln")
        P.o("scalar", lambda E: E.activation(
            xnap, xap, AF.Identity, bias=nb[:, 0:1], scale=rsd[:, 0:1]),
            inc="A_xn")
        if "dump" in self.dbg and cnt == 32:
            snap = self.V(L.r6 + 20480, 128, F32)
            P.o("vector", lambda E, snap=snap: E.tensor_copy(
                snap[:, 0:16], self.V(lnt, 16, F32)[:, 0:16]))
            P.o("vector", lambda E, snap=snap: E.tensor_copy(
                snap[:, 16:32], self.V(lnt + 448, 16, F32)[:, 0:16]))
            P.o("vector", lambda E, snap=snap: E.tensor_copy(
                snap[:, 32:48], self.V(lnt + 256, 16, F32)[:, 0:16]))
            P.o("vector", lambda E, snap=snap: E.tensor_copy(
                snap[:, 48:64], self.V(lnt + 320, 16, F32)[:, 0:16]), inc="D_snap")
            dsn = self.nc.dram_tensor("dbg_ln2m0", [128, 128], F32, kind="ExternalOutput").ap()
            P.w("sync", "D_snap")
            P.o("sync", lambda E, dsn=dsn, snap=snap: E.dma_start(out=dsn, in_=snap),
                inc="L_dbg6", inc_n=16)

    def emit_deq_tile(self, mat, ti):
        P, L = self.P, self.lay
        t = self.deq_cnt
        self.deq_cnt += 1
        par = t % 2
        pk = self.V(L.pk + 4096 * par, 1024, I32)
        scl = self.V(L.scl + 512 * par, 64, F32)
        src_pk = {"g": self.gpk, "u": self.upk, "d": self.dpk}[mat]
        src_sc = {"g": self.gsc, "u": self.usc, "d": self.dsc}[mat]
        dqo = L.dq
        wf = self.V(dqo, 1024, BF16)
        lo = self.V(dqo + 2048, 1024, BF16)
        sgl = self.V(dqo + 4096, 1024, BF16)
        la = self.V(dqo + 6144, 1024, BF16)
        lb = self.V(dqo + 8192, 1024, BF16)
        ld_ = self.V(dqo + 10240, 1024, BF16)
        dqh = L.dq + 12288
        hi_i = self.V(dqh, 1024, I32)
        hi = self.V(dqh + 4096, 1024, BF16)
        sgh = self.V(dqh + 6144, 1024, BF16)
        ha = self.V(dqh + 8192, 1024, BF16)
        hb = self.V(dqh + 10240, 1024, BF16)
        hd_ = self.V(dqh + 12288, 1024, BF16)
        wdec = self.V(L.wdec, 2048, BF16)
        wstg = self.V(L.wstg, 16 * 128, BF16)
        PS3 = self.PS[3]

        if t >= 2:
            P.w("sync", "D_dec", t - 1)
            P.w("sync", "P_dec", t - 1)
        P.o("sync", lambda E: E.dma_start(out=pk, in_=src_pk[128 * ti:128 * (ti + 1), :]),
            inc="L_pk", inc_n=16)
        P.o("sync", lambda E: E.dma_start(out=scl, in_=src_sc[128 * ti:128 * (ti + 1), :]),
            inc="L_pk", inc_n=16)

        scv = scl.unsqueeze(2).broadcast_to([128, 64, 16])
        P.w("vector", "L_pk", 32 * (t + 1))
        if t >= 1:
            P.w("vector", "PE_dqT", 4 * t)
        P.o("vector", lambda E: E.tensor_single_scalar(hi_i, pk, 4, ALU.logical_shift_right))
        P.o("vector", lambda E: E.tensor_copy(hi, hi_i), inc="P_hi")
        P.o("vector", lambda E: E.tensor_copy(wf, pk))
        P.w("scalar", "P_hi")
        P.o("scalar", lambda E: E.activation(sgh, hi, AF.Sign, bias=self.cb75, scale=-1.0), inc="A_sgh")
        P.o("vector", lambda E: E.scalar_tensor_tensor(lo, hi, -16.0, wf, ALU.mult, ALU.add),
            inc="D_lo")
        P.w("scalar", "D_lo")
        P.o("scalar", lambda E: E.activation(sgl, lo, AF.Sign, bias=self.cb75, scale=-1.0), inc="A_sgl")
        P.o("scalar", lambda E: E.activation(hb, hi, AF.Identity, bias=self.cbm4),
            inc="A_nm4h")
        P.w("vector", "A_sgh")
        P.w("vector", "A_nm4h")
        P.o("vector", lambda E: E.scalar_tensor_tensor(ha, sgh, 4.0, hb, ALU.mult, ALU.add),
            inc="P_kh")
        P.w("scalar", "P_kh")
        P.o("scalar", lambda E: E.activation(hb, ha, AF.Relu, bias=self.cbm4))
        P.o("scalar", lambda E: E.activation(hd_, ha, AF.Relu, bias=self.cbm6), inc="A_r2h")
        P.w("vector", "A_r2h")
        P.o("vector", lambda E: E.tensor_add(hb, hb, ha))
        P.o("vector", lambda E: E.scalar_tensor_tensor(hb, hb, 0.5, hd_, ALU.mult, ALU.add))
        P.o("vector", lambda E: E.tensor_mul(hb, hb, sgh))
        P.o("vector", lambda E: E.tensor_tensor(
            wdec[:, 1::2].rearrange("p (a c) -> p a c", c=16),
            hb.rearrange("p (a c) -> p a c", c=16), scv, ALU.mult), inc="P_dec")
        P.o("scalar", lambda E: E.activation(la, lo, AF.Identity, bias=self.cbm4),
            inc="A_nm4l")
        P.w("vector", "A_sgl")
        P.w("vector", "A_nm4l")
        P.o("vector", lambda E: E.scalar_tensor_tensor(lb, sgl, 4.0, la, ALU.mult, ALU.add),
            inc="D_kl")
        P.w("scalar", "D_kl")
        P.o("scalar", lambda E: E.activation(la, lb, AF.Relu, bias=self.cbm4))
        P.o("scalar", lambda E: E.activation(ld_, lb, AF.Relu, bias=self.cbm6), inc="A_r2l")
        P.w("vector", "A_r2l")
        P.o("vector", lambda E: E.tensor_add(la, la, lb))
        P.o("vector", lambda E: E.scalar_tensor_tensor(la, la, 0.5, ld_, ALU.mult, ALU.add))
        P.o("vector", lambda E: E.tensor_mul(la, la, sgl))
        P.o("vector", lambda E: E.tensor_tensor(
            wdec[:, 0::2].rearrange("p (a c) -> p a c", c=16),
            la.rearrange("p (a c) -> p a c", c=16), scv, ALU.mult), inc="D_dec")

        P.w("tensor", "D_dec", t + 1)
        P.w("tensor", "P_dec", t + 1)
        if t >= 1:
            P.w("scalar", "L_stg", 16 * t)
        for r in range(4):
            P.w("tensor", "A_stg", 4 * t + r)
            for k in range(4):
                kc = 4 * r + k
                P.o("tensor", lambda E, kc=kc, k=k: E.matmul(
                    PS3[:, 512 + 128 * k:512 + 128 * (k + 1)],
                    wdec[:, 128 * kc:128 * (kc + 1)], self.ident,
                    start=True, stop=True), inc=("PE_dqT" if k == 3 else None))
            P.w("scalar", "PE_dqT")
            P.o("scalar", lambda E, r=r: E.activation(
                wstg[:, 512 * r:512 * (r + 1)], PS3[:, 512:1024], AF.Copy),
                inc="A_stg")
        P.w("sync", "A_stg", 4 * (t + 1))
        if mat in ("g", "u"):
            dst = (self.contrib_g if mat == "g" else self.contrib_u)[ti // 4]
            dview = dst.ap().rearrange("(kc p) n -> p kc n", p=128)[
                :, :, 128 * (ti % 4):128 * (ti % 4 + 1)]
        else:
            dview = self.contrib_d.ap().rearrange("(nc p) h -> p nc h", p=128)[
                :, :, 128 * ti:128 * (ti + 1)]
        sview = wstg.rearrange("p (kc n) -> p kc n", n=128)
        P.o("sync", lambda E: E.dma_start(out=dview, in_=sview), inc="L_stg", inc_n=16)

    def emit_ag(self, kind, idx):
        P = self.P
        if kind == "g":
            src, dst, wait = self.contrib_g[idx], self.gath_g[idx], self.agw[("g", idx)]
        elif kind == "u":
            src, dst, wait = self.contrib_u[idx], self.gath_u[idx], self.agw[("u", idx)]
        else:
            src = self.contrib_d.ap()[512 * idx:512 * (idx + 1), :]
            dst, wait = self.gath_d[idx], self.agw[("d", idx)]
        P.w("gpsimd", "L_stg", wait)
        sem = f"CC_{kind}{idx}"
        if kind == "d":
            P.o("gpsimd", lambda E: E.collective_compute(
                "AllGather", ALU.bypass, replica_groups=[list(range(NC))],
                ins=[src.opt()], outs=[dst.ap().opt()]), inc=sem)
        else:
            P.o("gpsimd", lambda E: E.collective_compute(
                "AllGather", ALU.bypass, replica_groups=[list(range(NC))],
                ins=[src.ap().opt()], outs=[dst.ap().opt()]), inc=sem)

    def emit_sweep_chunk(self, g):
        P, L = self.P, self.lay
        par = g % 2
        xb = self.V(L.xring + 8192 * par, 2048, F32)
        xnb = self.V(L.wq0 + 4096 * par, 2048, BF16)
        cs = self.V(L.csring + 1024 * par, 128, F32)
        sn = self.V(L.csring + 1024 * par + 512, 128, F32)
        k0 = self.V(L.k0, 128, BF16)
        vt0 = self.V(L.vt0, 128, BF16)
        u1 = self.V(L.u1, 128, F32)
        u2 = self.V(L.u2, 128, F32)
        PSp = self.PS[par]
        PS2 = self.PS[2]
        own = (g % 4 == 3)
        m = g // 4

        if g >= 2:
            P.w("sync", "A_xn", g - 1)
        P.o("sync", lambda E: E.dma_start(out=xb, in_=self.xseq[128 * g:128 * (g + 1), :]),
            inc="L_x", inc_n=16)
        if g >= 2:
            P.w("scalar", "D_kv2", g - 1)
        P.o("scalar", lambda E: E.dma_start(out=cs, in_=self.cost[:, 128 * g:128 * (g + 1)]),
            inc="L_cs", inc_n=16)
        P.o("scalar", lambda E: E.dma_start(out=sn, in_=self.sint[:, 128 * g:128 * (g + 1)]),
            inc="L_cs", inc_n=16)

        if "dump" in self.dbg and g == 0:
            dx0 = self.nc.dram_tensor("dbg_x0", [128, 2048], F32, kind="ExternalOutput").ap()
            P.o("sync", lambda E, xb=xb, dx0=dx0: E.dma_start(out=dx0, in_=xb),
                inc="L_dbg5", inc_n=16)
        wb = [("vector", P.sem("L_x"), P.sem("L_x").n),
              ("scalar", P.sem("L_x"), P.sem("L_x").n)]
        if g >= 2:
            wb.append(("vector", P.sem("PE_xnt"), 2 * (g - 1)))
            wb.append(("scalar", P.sem("PE_xnt"), 2 * (g - 1)))
        self.emit_ln(xb, xnb, par, wait_before=wb)

        if "dump" in self.dbg and g == 0:
            dxn0 = self.nc.dram_tensor("dbg_xn0", [128, 2048], BF16, kind="ExternalOutput").ap()
            dln0 = self.nc.dram_tensor("dbg_ln0", [128, 128], F32, kind="ExternalOutput").ap()
            P.w("sync", "A_xn", P.sem("A_xn").n)
            P.o("sync", lambda E, xnb=xnb, dxn0=dxn0: E.dma_start(out=dxn0, in_=xnb),
                inc="L_dbg5", inc_n=16)
            P.o("sync", lambda E, dln0=dln0: E.dma_start(
                out=dln0, in_=self.V(L.lnt, 128, F32)), inc="L_dbg5", inc_n=16)
        dst_own = self.xnt_own.rearrange("p (hc t) -> p hc t", t=1024)
        dst_sw = self.V(L.xnt_sweep + 4096 * par, 16 * 128, BF16)
        P.w("tensor", "A_xn")
        for half in range(2):
            P.w("tensor", "D_xnte", (2 * (g - 1) if g >= 2 else 0) if half == 0 else 2 * g + 1)
            for i in range(8):
                hc = 8 * half + i
                P.o("tensor", lambda E, hc=hc, i=i: E.matmul(
                    PSp[:, 128 * i:128 * (i + 1)],
                    xnb[:, 128 * hc:128 * (hc + 1)], self.ident,
                    start=True, stop=True), inc=("PE_xnt" if i == 7 else None))
            P.w("scalar", "PE_xnt")
            if own:
                dv = dst_own[:, 8 * half:8 * (half + 1), 128 * m:128 * (m + 1)]
            else:
                dv = dst_sw.rearrange("p (hc t) -> p hc t", t=128)[:, 8 * half:8 * (half + 1), :]
                if g >= 2:
                    P.w("scalar", "PE_kv1", g - 1)
            P.o("scalar", lambda E, dv=dv: E.activation(
                dv, PSp[:, :].rearrange("p (hc t) -> p hc t", t=128), AF.Copy),
                inc="D_xnte")

        xnt_src = dst_own[:, :, 128 * m:128 * (m + 1)] if own else \
            dst_sw.rearrange("p (hc t) -> p hc t", t=128)
        kvb = PS2[:, 512 * par:512 * par + 512]
        k0 = self.V(L.k0 + 256 * par, 128, BF16)
        vt0 = self.V(L.vt0 + 256 * par, 128, BF16)
        P.w("tensor", "D_xnte")
        P.w("tensor", "D_kv2", max(0, g - 1))
        for hc in range(16):
            P.o("tensor", lambda E, hc=hc, kvb=kvb: E.matmul(
                kvb[:, 0:128],
                self.wkb.rearrange("p (h d) -> p h d", d=128)[:, hc, :],
                xnt_src[:, hc, :], start=(hc == 0), stop=(hc == 15)))
        for hc in range(16):
            P.o("tensor", lambda E, hc=hc, kvb=kvb: E.matmul(
                kvb[:, 128:256],
                self.wvb.rearrange("p (h d) -> p h d", d=128)[:, hc, :],
                xnt_src[:, hc, :], start=(hc == 0), stop=(hc == 15)),
                inc=("PE_kv1" if hc == 15 else None))
        P.w("vector", "PE_kv1")
        P.o("vector", lambda E, kvb=kvb, k0=k0: E.tensor_copy(k0, kvb[:, 0:128]))
        P.o("vector", lambda E, kvb=kvb, vt0=vt0: E.tensor_copy(vt0, kvb[:, 128:256]), inc="D_kv1")
        P.w("tensor", "D_kv1")
        P.o("tensor", lambda E, kvb=kvb, vt0=vt0: E.matmul(kvb[:, 256:384], vt0, self.ident,
                                         start=True, stop=True))
        P.o("tensor", lambda E, kvb=kvb, k0=k0: E.matmul(kvb[:, 384:512], self.rotT, k0,
                                         start=True, stop=True), inc="PE_kv2")
        P.w("vector", "PE_kv2")
        P.w("vector", "L_cs")
        P.o("vector", lambda E, kvb=kvb: E.tensor_copy(
            self.Vsb[:, 128 * g:128 * (g + 1)], kvb[:, 256:384]))
        P.o("vector", lambda E, k0=k0: E.tensor_tensor(u1, k0, cs, ALU.mult))
        P.o("vector", lambda E, kvb=kvb: E.tensor_tensor(u2, kvb[:, 384:512], sn, ALU.mult))
        P.o("vector", lambda E: E.tensor_add(
            self.KT[:, 128 * g:128 * (g + 1)], u1, u2),
            inc="D_kv2")

    def emit_q_pre(self):
        P, L = self.P, self.lay
        P.w("tensor", "D_xnte")
        P.w("tensor", "D_kv2")
        self._q0 = self.V(L.r6 + 8192, 1024, BF16)
        self._cosq = self.V(L.r6, 1024, F32)
        self._sinq = self.V(L.r6 + 4096, 1024, F32)
        P.o("sync", lambda E: E.dma_start(out=self._cosq, in_=self.cosq), inc="L_cq", inc_n=16)
        P.o("sync", lambda E: E.dma_start(out=self._sinq, in_=self.sinq), inc="L_cq", inc_n=16)
        P.w("vector", "L_cq")
        P.w("gpsimd", "PE_xnt")

    def emit_q_group(self, dcg):
        P, L = self.P, self.lay
        PS2 = self.PS[2]
        q0, cosq, sinq = self._q0, self._cosq, self._sinq
        slab = self.V(L.wq0 + 16384 * (dcg % 2), 16 * 512, BF16)
        if dcg >= 2:
            P.w("gpsimd", "PE_qmm", 4 * (dcg - 1))
        P.o("gpsimd", lambda E, dcg=dcg, slab=slab: E.dma_start(
            out=slab.rearrange("p (h d) -> p h d", d=512),
            in_=self.wq.rearrange("(h p) d -> p h d", p=128)[:, :, 512 * dcg:512 * (dcg + 1)]),
            inc="L_wq", inc_n=16)
        P.w("tensor", "L_wq", 16 * (dcg + 1))
        for dc4 in range(4):
            dc = 4 * dcg + dc4
            PSq = self.PS[dc % 2]
            if dc >= 2:
                P.w("tensor", "D_q0", dc - 1)
            for hc in range(16):
                for th in range(2):
                    P.o("tensor", lambda E, hc=hc, th=th, slab=slab, dc4=dc4, PSq=PSq: E.matmul(
                        PSq[:, 512 * th:512 * (th + 1)],
                        slab.rearrange("p (h d) -> p h d", d=512)[:, hc, 128 * dc4:128 * (dc4 + 1)],
                        self.xnt_own.rearrange("p (hc t) -> p hc t", t=1024)[:, hc, 512 * th:512 * (th + 1)],
                        start=(hc == 0), stop=(hc == 15)),
                        inc=("PE_qmm" if hc == 15 and th == 1 else None))
            P.w("scalar", "PE_qmm")
            if dc >= 1:
                P.w("scalar", "D_qr", dc)
            P.o("scalar", lambda E, PSq=PSq: E.activation(q0, PSq[:, :], AF.Copy), inc="D_q0")
            P.w("tensor", "D_q0")
            if dc >= 1:
                P.w("tensor", "D_qr", dc)
            for th in range(2):
                P.o("tensor", lambda E, th=th: E.matmul(
                    PS2[:, 512 * th:512 * (th + 1)], self.rotT,
                    q0[:, 512 * th:512 * (th + 1)], start=True, stop=True),
                    inc=("PE_qrk" if th == 1 else None))
            P.w("vector", "PE_qrk")
            u1 = self.V(L.r6 + 10240, 512, F32)
            for th in range(2):
                P.o("vector", lambda E, th=th, dc=dc: E.tensor_tensor(
                    u1, q0[:, 512 * th:512 * (th + 1)],
                    cosq[:, 512 * th:512 * (th + 1)], ALU.mult))
                P.o("vector", lambda E, th=th, dc=dc: E.tensor_tensor(
                    self.QT[:, 1024 * dc + 512 * th:1024 * dc + 512 * (th + 1)],
                    PS2[:, 512 * th:512 * (th + 1)],
                    sinq[:, 512 * th:512 * (th + 1)], ALU.mult))
                P.o("vector", lambda E, th=th, dc=dc: E.tensor_add(
                    self.QT[:, 1024 * dc + 512 * th:1024 * dc + 512 * (th + 1)],
                    self.QT[:, 1024 * dc + 512 * th:1024 * dc + 512 * (th + 1)],
                    u1), inc=("D_qr" if th == 1 else None))

    def emit_attention(self):
        P, L = self.P, self.lay
        PS2 = self.PS[2]
        PS3 = self.PS[3]
        amask = self.amask
        P.w("sync", "PE_qmm")
        P.o("sync", lambda E: E.dma_start(out=amask, in_=self.amaskD),
            inc="L_am", inc_n=16)
        P.w("vector", "L_am")
        P.w("tensor", "D_qr")
        P.w("tensor", "D_q0")
        P.w("tensor", "A_stg")
        self.Ggl = 0
        self.Hc = 0
        self.xm_r = 0
        self.m_cnt = 0
        for pm in range(4):
            for head in range(16):
                self.emit_attn_head(pm, head)
            for qc in range(2):
                self.emit_xmid_ln2(pm, qc)

    def emit_attn_head(self, pm, head):
        P, L = self.P, self.lay
        PS2 = self.PS[2]
        hp = head % 2
        PSh = PS2 if hp == 0 else self.PS[3]
        pv = PSh[:, 0:256]
        sm = PSh[:, 512:768]
        ng = 2 * pm + 2
        Hc = self.Hc
        self.Hc += 1
        qs = 256 * pm
        last_jc = 8 * pm + 7
        P.w("tensor", "D_att", max(0, Hc - 1))
        if hp == 1 and pm >= 1:
            P.w("tensor", "D_x2e", 8 * pm)

        def emit_S(gg, Ggl, slot):
            Spar = self.PS[Ggl % 2]
            P.w("tensor", "A_exp", max(0, Ggl - 1))
            for k in range(4):
                jc = 4 * gg + k
                P.o("tensor", lambda E, jc=jc, k=k, Spar=Spar: E.matmul(
                    Spar[:, 256 * k:256 * (k + 1)],
                    self.KT[:, 128 * jc:128 * (jc + 1)],
                    self.QT[:, 1024 * head + qs:1024 * head + qs + 256],
                    start=True, stop=True), inc=("PE_S" if k == 3 else None))
            P.w("scalar", "PE_S")
            P.w("scalar", "PE_ptf", max(0, Ggl - 2))
            P.o("scalar", lambda E, slot=slot, Spar=Spar: E.activation(
                slot, Spar[:, :], AF.Exp, scale=ISQ, bias=self.cbsh), inc="A_exp")
            zone = gg >= ng - 2
            if zone:
                zi = gg - (ng - 2)
                P.w("vector", "A_exp")
                P.o("vector", lambda E, slot=slot, zi=zi: E.tensor_tensor(
                    slot, slot, self.amask[:, 1024 * zi:1024 * (zi + 1)], ALU.mult),
                    inc="D_msk")
            return zone

        def emit_PV(gg, zone, slot):
            if zone:
                P.w("tensor", "D_msk")
            else:
                P.w("tensor", "A_exp", self.exp_at[gg])
            for k in range(4):
                jc = 4 * gg + k
                P.o("tensor", lambda E, jc=jc, k=k, slot=slot: E.matmul(
                    pv, self.Vsb[:, 128 * jc:128 * (jc + 1)],
                    slot[:, 256 * k:256 * (k + 1)],
                    start=(jc == 0), stop=(jc == last_jc)))
                P.o("tensor", lambda E, jc=jc, k=k, slot=slot: E.matmul(
                    sm, self.ones, slot[:, 256 * k:256 * (k + 1)],
                    start=(jc == 0), stop=(jc == last_jc)),
                    inc=("PE_ptf" if k == 3 else None))

        self.exp_at = {}
        pend = None
        for gg in range(ng):
            Ggl = self.Ggl
            self.Ggl += 1
            slot = self.V(L.PT + 2048 * (Ggl % 3), 1024, BF16)
            zone = emit_S(gg, Ggl, slot)
            self.exp_at[gg] = self.P.sem("A_exp").n
            if pend is not None:
                emit_PV(*pend)
            pend = (gg, zone, slot)
        emit_PV(*pend)
        self._attn_tail(pm, head, pv, sm)


    def _attn_tail(self, pm, head, pv, sm):
        P, L = self.P, self.lay
        recip = self.V(L.recip, 256, F32)
        att = self.V(L.attnT + 8192 * (pm % 2), 16 * 256, BF16)
        P.w("vector", "PE_ptf")
        if "dump" in self.dbg and pm == 0 and head == 0:
            dkt = self.nc.dram_tensor("dbg_kt", [128, 4096], BF16, kind="ExternalOutput").ap()
            dqt = self.nc.dram_tensor("dbg_qt", [128, 16384], BF16, kind="ExternalOutput").ap()
            dvs = self.nc.dram_tensor("dbg_vs", [128, 4096], BF16, kind="ExternalOutput").ap()
            dam = self.nc.dram_tensor("dbg_am", [128, 2048], BF16, kind="ExternalOutput").ap()
            P.o("sync", lambda E: E.dma_start(out=dkt, in_=self.KT), inc="L_dbg3", inc_n=16)
            P.o("sync", lambda E: E.dma_start(out=dqt, in_=self.QT), inc="L_dbg3", inc_n=16)
            P.o("sync", lambda E: E.dma_start(out=dvs, in_=self.Vsb), inc="L_dbg3", inc_n=16)
            P.o("sync", lambda E: E.dma_start(out=dam, in_=self.amask), inc="L_dbg3", inc_n=16)
        if "dump" in self.dbg and pm == 1 and head == 0:
            pvd = self.V(L.r6 + 18432, 256, F32)
            smd = self.V(L.r6 + 19456, 256, F32)
            P.o("vector", lambda E: E.tensor_copy(pvd, pv))
            P.o("vector", lambda E: E.tensor_copy(smd, sm), inc="D_dbgpv")
            P.w("sync", "D_dbgpv")
            dpv = self.nc.dram_tensor("dbg_pv", [128, 512], F32, kind="ExternalOutput").ap()
            P.o("sync", lambda E, dpv=dpv: E.dma_start(out=dpv[:, 0:256], in_=pvd), inc="L_dbg2", inc_n=16)
            P.o("sync", lambda E, dpv=dpv: E.dma_start(out=dpv[:, 256:512], in_=smd), inc="L_dbg2", inc_n=16)
        P.o("vector", lambda E, sm=sm: E.reciprocal(recip, sm))
        P.o("vector", lambda E, pv=pv, head=head, att=att: E.tensor_tensor(
            att[:, 256 * head:256 * (head + 1)], pv, recip, ALU.mult), inc="D_att")

    def emit_xmid_ln2(self, pm, qc):
        P, L = self.P, self.lay
        PS3 = self.PS[3]
        m = 2 * pm + qc
        xmid = self.V(L.xmid, 2048, F32)
        att = self.V(L.attnT + 8192 * (pm % 2), 16 * 256, BF16)
        xn2b = self.V(L.r6 + 14336, 2048, BF16)
        if m >= 1:
            P.w("sync", "A_xn", P.sem("A_xn").n)
            P.w("sync", "L_xmD", 16 * m)
        g = 4 * m + 3
        P.o("sync", lambda E, g=g, xmid=xmid: E.dma_start(
            out=xmid, in_=self.xseq[128 * g:128 * (g + 1), :]), inc="L_xo", inc_n=16)
        P.w("tensor", "D_att")
        for r in range(4):
            xr = self.xm_r
            self.xm_r += 1
            P.w("tensor", "D_xm", max(0, xr))
            if r == 0 and m >= 1:
                P.w("tensor", "D_x2e", 4 * m)
            for i in range(4):
                head = 4 * r + i
                P.o("tensor", lambda E, head=head, i=i, att=att, qc=qc: E.matmul(
                    PS3[:, 128 * i:128 * (i + 1)],
                    att[:, 256 * head + 128 * qc:256 * head + 128 * (qc + 1)],
                    self.ident, start=True, stop=True),
                    inc=("PE_xm" if i == 3 else None))
            P.w("vector", "PE_xm")
            if r == 0:
                P.w("vector", "L_xo", 16 * (m + 1))
            P.o("vector", lambda E, r=r, xmid=xmid: E.scalar_tensor_tensor(
                xmid[:, 512 * r:512 * (r + 1)], PS3[:, 0:512], 1.0,
                xmid[:, 512 * r:512 * (r + 1)], ALU.mult, ALU.add), inc="D_xm")
        P.w("sync", "D_xm", P.sem("D_xm").n)
        P.o("sync", lambda E, m=m, xmid=xmid: E.dma_start(
            out=self.xmidD.ap()[128 * m:128 * (m + 1), :], in_=xmid),
            inc="L_xmD", inc_n=16)
        wb = [("vector", P.sem("D_xm"), P.sem("D_xm").n),
              ("scalar", P.sem("D_xm"), P.sem("D_xm").n)]
        if m >= 1:
            wb.append(("vector", P.sem("PE_x2t"), 4 * m))
            wb.append(("scalar", P.sem("PE_x2t"), 4 * m))
        self.emit_ln(xmid, xn2b, m, wait_before=wb, lnt_off=L.r6 + 12288 + 1024 * (m % 2))
        P.w("tensor", "A_xn")
        dst = self.xn2T.rearrange("p (hc t) -> p hc t", t=1024)
        for r in range(4):
            P.w("tensor", "D_x2e", 4 * m + r)
            for i in range(4):
                hc = 4 * r + i
                P.o("tensor", lambda E, hc=hc, i=i, xn2b=xn2b: E.matmul(
                    PS3[:, 128 * i:128 * (i + 1)],
                    xn2b[:, 128 * hc:128 * (hc + 1)], self.ident,
                    start=True, stop=True), inc=("PE_x2t" if i == 3 else None))
            P.w("vector", "PE_x2t")
            P.o("vector", lambda E, r=r, m=m, dst=dst: E.tensor_copy(
                dst[:, 4 * r:4 * (r + 1), 128 * m:128 * (m + 1)],
                PS3[:, 0:512].rearrange("p (hc t) -> p hc t", t=128)), inc="D_x2e")

    def emit_gate(self):
        P, L = self.P, self.lay
        for eng in ("sync", "vector", "scalar", "tensor"):
            for sem in ("PE_dqT", "D_dec", "P_dec", "L_stg", "L_pk"):
                P.w(eng, sem)
        for sem in ("A_exp", "D_att", "D_x2e", "A_stg", "D_qr"):
            P.w("tensor", sem)
        for sem in ("PE_ptf", "PE_xm", "D_att", "D_x2e", "D_msk", "A_xn"):
            P.w("scalar", sem)
        self.gn = 0
        srr = 0
        for sl in range(4):
            P.w("sync", f"CC_g{sl}", 1)
            P.w("sync", f"CC_u{sl}", 1)
            P.w("scalar", f"CC_g{sl}", 1)
            P.w("scalar", f"CC_u{sl}", 1)
            for rr in range(8):
                self.emit_gate_srr(sl, rr, srr)
                srr += 1

    def emit_gate_srr(self, sl, rr, srr):
        P, L = self.P, self.lay
        par = srr % 2
        wg = self.V(L.gslab if par == 0 else L.KT, 16 * 512, BF16)
        wu = self.V((L.gslab + 16384) if par == 0 else L.r6, 16 * 512, BF16)
        gelu = self.V(L.gelu, 1024, F32)
        if srr >= 2:
            P.w("scalar", "PE_gmm", 4 * (srr - 1))
        for (wsl, gath) in ((wg, self.gath_g[sl]), (wu, self.gath_u[sl])):
            P.o("scalar", lambda E, wsl=wsl, gath=gath, rr=rr: E.dma_start(
                out=wsl.rearrange("p (h n) -> p h n", n=512),
                in_=gath.ap().rearrange("(r h p) n -> r p h n", p=128, h=16)[rr]),
                inc="L_gs", inc_n=16)
        P.w("tensor", "L_gs", 32 * (srr + 1))
        for h2 in range(2):
            for nc2 in range(2):
                gn = self.gn
                self.gn += 1
                psG = self.PS[2 * (gn % 2)]
                psU = self.PS[2 * (gn % 2) + 1]
                if gn >= 2:
                    P.w("tensor", "A_gelu", gn - 1)
                    P.w("tensor", "D_int", gn - 1)
                nco = 256 * h2 + 128 * nc2
                for wsl, ps in ((wg, psG), (wu, psU)):
                    for hc in range(16):
                        for th in range(2):
                            lastg = wsl is wu and hc == 15 and th == 1
                            P.o("tensor", lambda E, wsl=wsl, ps=ps, hc=hc, th=th, nco=nco: E.matmul(
                                ps[:, 512 * th:512 * (th + 1)],
                                wsl.rearrange("p (h n) -> p h n", n=512)[:, hc, nco:nco + 128],
                                self.xn2T.rearrange("p (hc t) -> p hc t", t=1024)[:, hc, 512 * th:512 * (th + 1)],
                                start=(hc == 0), stop=(hc == 15)),
                                inc=("PE_gmm" if lastg else None))
                P.w("scalar", "PE_gmm")
                if gn >= 1:
                    P.w("scalar", "D_int", gn)
                P.o("scalar", lambda E, psG=psG, gelu=gelu: E.activation(
                    gelu, psG[:, :], AF.Gelu_apprx_tanh), inc="A_gelu")
                ib = self.V(L.inter + 2048 * (gn % 2), 1024, BF16)
                P.w("vector", "A_gelu")
                if gn >= 2:
                    P.w("vector", "L_int", 16 * (gn - 1))
                P.o("vector", lambda E, ib=ib, gelu=gelu, psU=psU: E.tensor_tensor(
                    ib, gelu, psU[:, :], ALU.mult), inc="D_int")
                nrow = 2048 * rr + 512 * sl + 128 * (2 * h2 + nc2)
                P.w("sync", "D_int")
                P.o("sync", lambda E, ib=ib, nrow=nrow: E.dma_start(
                    out=self.interT.ap()[nrow:nrow + 128, :], in_=ib),
                    inc="L_int", inc_n=16)

    def emit_down(self):
        P, L = self.P, self.lay
        for sem in ("A_gelu", "D_int"):
            P.w("tensor", sem)
        for sem in ("PE_ptf", "PE_xm", "A_xn", "D_att", "D_msk", "L_int", "PE_gmm",
                    "L_xmD", "L_xo", "L_am", "L_cq"):
            P.w("sync", sem)
        acc = [self.V((L.acc0 if t < 4 else L.acc1) + 8192 * (t % 4), 2048, F32)
               for t in range(8)]
        for t in range(8):
            P.o("sync", lambda E, t=t, a=acc[t]: E.dma_start(
                out=a, in_=self.xmidD.ap()[128 * t:128 * (t + 1), :]),
                inc="L_seed", inc_n=16)
        P.w("vector", "L_seed")
        P.w("tensor", "L_seed")
        gidx = 0
        gt = 0
        for i_d in range(4):
            P.w("sync", f"CC_d{i_d}", 1)
            P.w("scalar", f"CC_d{i_d}", 1)
            for rr in range(8):
                par = gidx % 2
                wdt = [self.V((L.dn_wdt if par == 0 else L.r6) + 4096 * k, 2048, BF16)
                       for k in range(4)]
                itt = [self.V(L.xn2T + 8192 * par + 2048 * k, 1024, BF16)
                       for k in range(4)]
                if gidx >= 2:
                    P.w("scalar", "PE_dn", 8 * (gidx - 1))
                    P.w("sync", "PE_dn", 8 * (gidx - 1))
                for k in range(4):
                    P.o("scalar", lambda E, k=k, i_d=i_d, rr=rr, w=wdt[k]: E.dma_start(
                        out=w, in_=self.gath_d[i_d].ap().rearrange(
                            "(r k p) h -> r k p h", k=4, p=128)[rr, k]),
                        inc="L_dnw", inc_n=16)
                    P.o("sync", lambda E, k=k, i_d=i_d, rr=rr, it=itt[k]: E.dma_start(
                        out=it, in_=self.interT.ap().rearrange(
                            "(r s k p) t -> r s k p t", s=4, k=4, p=128)[rr, i_d, k]),
                        inc="L_dn", inc_n=16)
                P.w("tensor", "L_dn", 64 * (gidx + 1))
                P.w("tensor", "L_dnw", 64 * (gidx + 1))
                for t in range(8):
                    pa = self.PS[2 * (t % 2)]
                    pb = self.PS[2 * (t % 2) + 1]
                    if gt >= 2:
                        P.w("tensor", "D_acc", gt - 1)
                    for k in range(4):
                        for hr in range(4):
                            ps = (pa, pa, pb, pb)[hr][:, 512 * (hr % 2):512 * (hr % 2 + 1)]
                            P.o("tensor", lambda E, k=k, hr=hr, t=t, ps=ps, w=wdt[k], it=itt[k]: E.matmul(
                                ps, it[:, 128 * t:128 * (t + 1)],
                                w[:, 512 * hr:512 * (hr + 1)],
                                start=(k == 0), stop=(k == 3)),
                                inc=("PE_dn" if k == 3 and hr == 3 else None))
                    P.w("vector", "PE_dn")
                    P.o("vector", lambda E, pa=pa, t=t, a=acc[t]: E.scalar_tensor_tensor(
                        a[:, 0:1024], pa[:, :], 1.0, a[:, 0:1024], ALU.mult, ALU.add))
                    P.o("vector", lambda E, pb=pb, t=t, a=acc[t]: E.scalar_tensor_tensor(
                        a[:, 1024:2048], pb[:, :], 1.0, a[:, 1024:2048], ALU.mult, ALU.add),
                        inc="D_acc")
                    if i_d == 3 and rr == 7:
                        P.w("sync", "D_acc")
                        P.o("sync", lambda E, t=t, a=acc[t]: E.dma_start(
                            out=self.out[128 * t:128 * (t + 1), :], in_=a),
                            inc="L_out", inc_n=16)
                    gt += 1
                gidx += 1
        P.w("sync", "L_out", 16 * 8)

    def schedule(self):
        self.deq_cnt = 0
        self.ln_cnt = 0
        self.agw = {}
        self.boot()
        sweeponly = ("sweeponly" in self.dbg) or ("swdeq" in self.dbg)
        nodeq = ("sweeponly" in self.dbg) or ("nodeq" in self.dbg)
        dorder = []
        for blk in range(4):
            dorder += [("g", 4 * blk + i) for i in range(4)]
            dorder += [("u", 4 * blk + i) for i in range(4)]
        for g in range(SEQ_CH):
            self.emit_sweep_chunk(g)
            if not nodeq:
                mat, ti = dorder[g]
                self.emit_deq_tile(mat, ti)
                if g % 4 == 3:
                    mat, ti = dorder[g]
                    sl = ti // 4
                    self.agw[(mat, sl)] = self.P.sem("L_stg").n
                    self.emit_ag(mat, sl)
        if sweeponly:
            P = self.P
            dkt = self.nc.dram_tensor("dbg_kt", [128, 4096], BF16, kind="ExternalOutput").ap()
            dvs = self.nc.dram_tensor("dbg_vs", [128, 4096], BF16, kind="ExternalOutput").ap()
            dxo = self.nc.dram_tensor("dbg_xo", [128, 16384], BF16, kind="ExternalOutput").ap()
            P.w("sync", "D_kv2")
            P.w("sync", "D_xnte")
            P.o("sync", lambda E: E.dma_start(out=dkt, in_=self.KT), inc="L_dbg4", inc_n=16)
            P.o("sync", lambda E: E.dma_start(out=dvs, in_=self.Vsb), inc="L_dbg4", inc_n=16)
            P.o("sync", lambda E: E.dma_start(out=dxo, in_=self.xnt_own), inc="L_dbg4", inc_n=16)
            P.w("sync", "L_dbg4", 48)
            self.P.emit()
            return
        self.emit_q_pre()
        for dcg in range(4):
            self.emit_q_group(dcg)
            if not nodeq:
                for i in range(4):
                    self.emit_deq_tile("d", 4 * dcg + i)
        if not nodeq:
            for i in range(4):
                self.agw[("d", i)] = self.P.sem("L_stg").n
                self.emit_ag("d", i)
        if "noattn" in self.dbg:
            P = self.P
            dkt = self.nc.dram_tensor("dbg_kt", [128, 4096], BF16, kind="ExternalOutput").ap()
            dvs = self.nc.dram_tensor("dbg_vs", [128, 4096], BF16, kind="ExternalOutput").ap()
            dqt = self.nc.dram_tensor("dbg_qt", [128, 16384], BF16, kind="ExternalOutput").ap()
            for sem in ("D_qr", "D_dec", "P_dec", "A_stg"):
                P.w("sync", sem)
            P.o("sync", lambda E: E.dma_start(out=dkt, in_=self.KT), inc="L_dbg4", inc_n=16)
            P.o("sync", lambda E: E.dma_start(out=dvs, in_=self.Vsb), inc="L_dbg4", inc_n=16)
            P.o("sync", lambda E: E.dma_start(out=dqt, in_=self.QT), inc="L_dbg4", inc_n=16)
            dar = self.nc.dram_tensor("dbg_arena", [128, self.ar.size], U8, kind="ExternalOutput").ap()
            P.o("sync", lambda E: E.dma_start(out=dar, in_=self.ar.h[0:128, :]), inc="L_dbg4", inc_n=16)
            P.w("sync", "L_dbg4", 64)
            if not nodeq:
                for i in range(4):
                    P.w("gpsimd", f"CC_d{i}", 1)
            self.P.emit()
            return
        self.emit_attention()
        self.emit_gate()
        self.emit_down()
        if "dump" in self.dbg:
            P = self.P
            dxm = self.nc.dram_tensor("dbg_xmid", [TPC, EMB], F32, kind="ExternalOutput").ap()
            dcg = self.nc.dram_tensor("dbg_cg", [EMB, 512], BF16, kind="ExternalOutput").ap()
            dgd = self.nc.dram_tensor("dbg_gd", [512, EMB], BF16, kind="ExternalOutput").ap()
            dint = self.nc.dram_tensor("dbg_int", [DFF, TPC], BF16, kind="ExternalOutput").ap()
            dx2 = self.nc.dram_tensor("dbg_xn2t", [128, 16 * TPC], BF16, kind="ExternalOutput").ap()
            P.w("sync", "L_out", 16 * 8)
            P.o("sync", lambda E: E.dma_start(out=dxm, in_=self.xmidD.ap()), inc="L_dbg", inc_n=16)
            P.o("sync", lambda E: E.dma_start(out=dcg, in_=self.contrib_g[0].ap()), inc="L_dbg", inc_n=16)
            P.o("sync", lambda E: E.dma_start(out=dgd, in_=self.gath_d[0].ap()[0:512, :]), inc="L_dbg", inc_n=16)
            P.o("sync", lambda E: E.dma_start(out=dint, in_=self.interT.ap()), inc="L_dbg", inc_n=16)
            P.o("sync", lambda E: E.dma_start(out=dx2, in_=self.xn2T), inc="L_dbg", inc_n=16)
            P.w("sync", "L_dbg", 80)
        self.P.emit()


def build_nc(dbg=()):
    nc = bass.Bass("TRN2", target_bir_lowering=False)
    b = Build(nc, dbg=dbg)
    b.schedule()
    return nc


def kernel(**inputs):
    maps = make_in_maps(inputs)
    nc = build_nc()
    res = run_bass_kernel_spmd(nc, maps, core_ids=list(range(NC)))
    return assemble_output(res.results)
